# version 14
# baseline (speedup 1.0000x reference)
import math
import numpy as np
import ml_dtypes
from contextlib import ExitStack

import concourse.bass as bass
import concourse.mybir as mybir
from concourse.bass_utils import run_bass_kernel_spmd

F32 = mybir.dt.float32
BF16 = mybir.dt.bfloat16
AF = mybir.ActivationFunctionType
ALU = mybir.AluOpType
AX = mybir.AxisListType
BF = ml_dtypes.bfloat16

D = 2048
HT = 16
TOK = 1024
NTT = 8
KV = 1664
NKT = 13
NH = 16
NG = 4
DH = 128
FF = 7168
NFC = 14
SUBS = 4
ER = 256
SLOTS = 6
EPS = 1e-5
SCALE = 1.0 / math.sqrt(DH)
MASKNEG = -1.0e9

_CACHE = {}


class _Sem:
    __slots__ = ("h", "n")

    def __init__(self, ctx, nc, name):
        self.h = ctx.enter_context(nc.semaphore(name))
        self.n = 0


def _w(eng, sem, v):
    if v > 0:
        eng.wait_ge(sem.h, v)


def _inc(ins, sem, d=1):
    ins.then_inc(sem.h, d)
    sem.n += d
    return sem.n


def build_program(upto=99):
    nc = bass.Bass("TRN2", target_bir_lowering=False)

    xkv = nc.dram_tensor("xkv", [KV, D], F32, kind="ExternalInput")
    maskd = nc.dram_tensor("maskb", [128, NTT * SLOTS * 128], BF16, kind="ExternalInput")
    cqtd = nc.dram_tensor("cqt", [128, NTT * 64], F32, kind="ExternalInput")
    sqtd = nc.dram_tensor("sqt", [128, NTT * 64], F32, kind="ExternalInput")
    cktd = nc.dram_tensor("ckt", [128, NKT * 64], F32, kind="ExternalInput")
    sktd = nc.dram_tensor("skt", [128, NKT * 64], F32, kind="ExternalInput")
    wqd = nc.dram_tensor("wq", [D, D], F32, kind="ExternalInput")
    wkd = nc.dram_tensor("wk", [D, 512], F32, kind="ExternalInput")
    wvd = nc.dram_tensor("wv", [D, 512], F32, kind="ExternalInput")
    wod = nc.dram_tensor("wo", [D, D], F32, kind="ExternalInput")
    bqbd = nc.dram_tensor("bqb", [128, D], BF16, kind="ExternalInput")
    bkbd = nc.dram_tensor("bkb", [128, 512], BF16, kind="ExternalInput")
    bvbd = nc.dram_tensor("bvb", [128, 512], BF16, kind="ExternalInput")
    botd = nc.dram_tensor("bot", [128, 16], F32, kind="ExternalInput")
    wgtd = nc.dram_tensor("wgt", [128, 128], F32, kind="ExternalInput")
    a1td = nc.dram_tensor("a1t", [D, ER], F32, kind="ExternalInput")
    a3td = nc.dram_tensor("a3t", [D, ER], F32, kind="ExternalInput")
    w1td = nc.dram_tensor("w1t", [D, FF], F32, kind="ExternalInput")
    w3td = nc.dram_tensor("w3t", [D, FF], F32, kind="ExternalInput")
    w2td = nc.dram_tensor("w2t", [FF, D], F32, kind="ExternalInput")
    a2td = nc.dram_tensor("a2t", [FF, ER], F32, kind="ExternalInput")
    b1rd = nc.dram_tensor("b1r", [ER, FF], F32, kind="ExternalInput")
    b3rd = nc.dram_tensor("b3r", [ER, FF], F32, kind="ExternalInput")
    b2rd = nc.dram_tensor("b2r", [ER, D], F32, kind="ExternalInput")
    e8d = nc.dram_tensor("e8", [8, ER], BF16, kind="ExternalInput")
    identbd = nc.dram_tensor("identb", [128, 128], BF16, kind="ExternalInput")
    identfd = nc.dram_tensor("identf", [128, 128], F32, kind="ExternalInput")
    ones1d = nc.dram_tensor("ones1", [1, 128], BF16, kind="ExternalInput")
    outd = nc.dram_tensor("out", [TOK, D], F32, kind="ExternalOutput")

    ctx = ExitStack()
    with ctx:
        d_const = _Sem(ctx, nc, "d_const")
        d_x = [_Sem(ctx, nc, f"d_x{i}") for i in range(2)]
        d_wr = [_Sem(ctx, nc, f"d_wr{i}") for i in range(3)]
        d_w1 = [_Sem(ctx, nc, f"d_w1{i}") for i in range(2)]
        d_w3 = [_Sem(ctx, nc, f"d_w3{i}") for i in range(2)]
        d_w2 = [_Sem(ctx, nc, f"d_w2{i}") for i in range(3)]
        d_be = [_Sem(ctx, nc, f"d_be{i}") for i in range(2)]
        d_sm = _Sem(ctx, nc, "d_sm")
        d_b2 = _Sem(ctx, nc, "d_b2")
        d_out = _Sem(ctx, nc, "d_out")
        d_out2 = _Sem(ctx, nc, "d_out2")
        pe_a = _Sem(ctx, nc, "pe_a")
        pe_b = _Sem(ctx, nc, "pe_b")
        pe_c = _Sem(ctx, nc, "pe_c")
        ve_a = _Sem(ctx, nc, "ve_a")
        ve_b = _Sem(ctx, nc, "ve_b")
        ve_c = _Sem(ctx, nc, "ve_c")
        ac_a = _Sem(ctx, nc, "ac_a")
        ac_b = _Sem(ctx, nc, "ac_b")
        ac_c = _Sem(ctx, nc, "ac_c")
        po_a = _Sem(ctx, nc, "po_a")
        ve_s = _Sem(ctx, nc, "ve_s")
        po_s = _Sem(ctx, nc, "po_s")
        SEMS = [d_const, d_sm, d_b2, d_out, d_out2, pe_a, pe_b, pe_c, ve_a, ve_b,
                ve_c, ac_a, ac_b, ac_c, po_a, ve_s, po_s] + d_x + d_wr + d_w1 + d_w3 + d_w2 + d_be

        xnTo = ctx.enter_context(nc.sbuf_tensor("xnTo", [128, HT * TOK], BF16))
        big0 = ctx.enter_context(nc.sbuf_tensor("big0", [128, HT * TOK], BF16))
        sumT = ctx.enter_context(nc.sbuf_tensor("sumT", [128, HT * TOK], BF16))
        identb = ctx.enter_context(nc.sbuf_tensor("identb_s", [128, 128], BF16))
        identf = ctx.enter_context(nc.sbuf_tensor("identf_s", [128, 128], F32))
        ones1 = ctx.enter_context(nc.sbuf_tensor("ones1_s", [1, 128], BF16))
        e8s = ctx.enter_context(nc.sbuf_tensor("e8_s", [8, ER], BF16))
        bots = ctx.enter_context(nc.sbuf_tensor("bot_s", [128, 16], F32))

        def emit_final():
            fin_ctx = ExitStack()
            xr = [fin_ctx.enter_context(nc.sbuf_tensor(f"xr{i}", [128, D], F32)) for i in range(2)]
            osb = [fin_ctx.enter_context(nc.sbuf_tensor(f"osb{i}", [128, D], F32)) for i in range(2)]
            pfin = [fin_ctx.enter_context(nc.psum_tensor(f"pfin{i}", [128, D], BF16))
                    for i in range(2)]
            fp, fv = pe_a.n, ve_a.n
            d_st = [d_out, d_out2]
            xr_wait = []
            st_wait = []
            fin_gate = [(sm, sm.n) for sm in SEMS]
            with nc.Block("final") as blk:
                @blk.sync
                def _(s):
                    for sm, n in fin_gate:
                        _w(s, sm, n)
                    for tt in range(NTT + 1):
                        if tt < NTT:
                            sl = tt % 2
                            _w(s, ve_a, fv + (tt - 2) + 1)
                            s.dma_start(xr[sl][:, :],
                                        xkv.ap()[512 + tt * 128:512 + (tt + 1) * 128, :])\
                                .then_inc(d_x[sl].h, 16)
                            d_x[sl].n += 16
                            xr_wait.append(d_x[sl].n)
                        if tt >= 1:
                            osl = (tt - 1) % 2
                            _w(s, ve_a, fv + tt)
                            s.dma_start(outd.ap()[(tt - 1) * 128:tt * 128, :],
                                        osb[osl][:, :]).then_inc(d_st[osl].h, 16)
                            d_st[osl].n += 16
                            st_wait.append(d_st[osl].n)
                    s.wait_ge(d_out.h, d_out.n)
                    s.wait_ge(d_out2.h, d_out2.n)

                @blk.tensor
                def _(te):
                    for sm, n in fin_gate:
                        if sm not in (d_out,):
                            _w(te, sm, n)
                    for tt in range(NTT):
                        _w(te, ve_a, fv + (tt - 2) + 1)
                        for hh in range(HT):
                            ins = te.transpose(pfin[tt % 2][:, hh * 128:(hh + 1) * 128],
                                               sumT[:, hh * TOK + tt * 128:hh * TOK + (tt + 1) * 128],
                                               identb[:, :])
                        _inc(ins, pe_a)

                @blk.vector
                def _(v):
                    for tt in range(NTT):
                        sl = tt % 2
                        _w(v, pe_a, fp + tt + 1)
                        _w(v, d_x[sl], xr_wait[tt])
                        if tt >= 2:
                            _w(v, d_st[sl], st_wait[tt - 2])
                        _inc(v.tensor_add(osb[sl][:, :], pfin[sl][:, :], xr[sl][:, :]), ve_a)

            fin_ctx.close()


        def xnTo_t(h):
            return xnTo[:, h * TOK:(h + 1) * TOK]

        def xnTo_s(h, tc):
            return xnTo[:, h * TOK + tc * 512:h * TOK + (tc + 1) * 512]

        def x2v(k):
            return big0[:, k * SUBS * TOK:(k + 1) * SUBS * TOK]

        w2slot_ap = [big0[:, 8192:12288], big0[:, 12288:16384], None]

        attn_ctx = ExitStack()
        KTs = attn_ctx.enter_context(nc.sbuf_tensor("KTs", [128, NG * KV], BF16))
        Vs = attn_ctx.enter_context(nc.sbuf_tensor("Vs", [128, NKT * 512], BF16))

        lnqkv_ctx = ExitStack()
        xnTh = lnqkv_ctx.enter_context(nc.sbuf_tensor("xnTh", [128, HT * 640], BF16))

        def xnTh_t(h):
            return xnTh[:, h * 640:(h + 1) * 640]

        def xn_kv(h, kt):
            if kt < 4:
                return xnTh_t(h)[:, kt * 128:(kt + 1) * 128]
            if kt < 12:
                return xnTo_t(h)[:, (kt - 4) * 128:(kt - 3) * 128]
            return xnTh_t(h)[:, 512:640]

        with nc.Block("consts") as blk:
            @blk.sync
            def _(s):
                for dst, src in ((identb, identbd), (identf, identfd), (ones1, ones1d),
                                 (e8s, e8d), (bots, botd)):
                    s.dma_start(dst[:, :], src.ap()).then_inc(d_const.h, 16)
                    d_const.n += 16
        CONST_N = d_const.n

        ln_ctx = ExitStack()
        xin = [ln_ctx.enter_context(nc.sbuf_tensor(f"xin{i}", [128, D], F32)) for i in range(2)]
        sqs = ln_ctx.enter_context(nc.sbuf_tensor("sqs", [128, D], F32))
        sms = ln_ctx.enter_context(nc.sbuf_tensor("sms", [128, D], F32))
        xnt = [ln_ctx.enter_context(nc.sbuf_tensor(f"xnt{i}", [128, D], BF16)) for i in range(2)]
        lst = [ln_ctx.enter_context(nc.sbuf_tensor(f"lst{i}", [128, 8], F32)) for i in range(2)]
        pt_ln = [ln_ctx.enter_context(nc.psum_tensor(f"pt_ln{i}", [128, 2048], BF16))
                 for i in range(2)]

        lnv, lna, lnp, lne = ve_a.n, ac_a.n, pe_a.n, ve_b.n

        def _ln_evac(v, t):
            _w(v, pe_a, lnp + t + 1)
            if t < 4:
                dst = xnTh[:, :].rearrange("p (h c) -> p h c", c=640)[:, :, t * 128:(t + 1) * 128]
            elif t < 12:
                dst = xnTo[:, :].rearrange("p (h c) -> p h c", c=TOK)[:, :, (t - 4) * 128:(t - 3) * 128]
            else:
                dst = xnTh[:, :].rearrange("p (h c) -> p h c", c=640)[:, :, 512:640]
            src = pt_ln[t % 2][:, :].rearrange("p (h c) -> p h c", c=128)
            _inc(v.tensor_copy(dst, src), ve_b)

        with nc.Block("ln") as blk:
            @blk.sync
            def _(s):
                for t in range(NKT):
                    sl = t % 2
                    _w(s, ac_a, lna + (t - 2) * 2 + 2)
                    s.dma_start(xin[sl][:, :], xkv.ap()[t * 128:(t + 1) * 128, :])\
                        .then_inc(d_x[sl].h, 16)
                    d_x[sl].n += 16

            @blk.vector
            def _(v):
                for t in range(NKT):
                    sl = t % 2
                    st = lst[sl]
                    _w(v, d_x[sl], 16 * (t // 2 + 1))
                    _inc(v.scalar_tensor_tensor(sqs[:, :], xin[sl][:, :], 1.0, xin[sl][:, :],
                                                ALU.mult, ALU.mult, accum_out=st[:, 1:2]), ve_s)
                    _inc(v.tensor_scalar(sms[:, :], xin[sl][:, :], 1.0, 0.0, ALU.mult,
                                         ALU.add, accum_out=st[:, 0:1]), ve_s)
                    _w(v, ve_s, ve_s.n)
                    v.tensor_scalar(st[:, 2:3], st[:, 0:1], 1.0 / D, None, ALU.mult)
                    _w(v, ve_s, _inc(v.tensor_scalar(st[:, 3:4], st[:, 1:2], 1.0 / D,
                                                     EPS, ALU.mult, ALU.add), ve_s))
                    _w(v, ve_s, _inc(v.scalar_tensor_tensor(st[:, 4:5], st[:, 2:3], 1.0,
                                                            st[:, 2:3], ALU.mult,
                                                            ALU.mult), ve_s))
                    _inc(v.scalar_tensor_tensor(st[:, 5:6], st[:, 4:5], -1.0, st[:, 3:4],
                                                ALU.mult, ALU.add), ve_a)
                    _w(v, ac_a, lna + t * 2 + 1)
                    _w(v, ve_s, _inc(v.reciprocal(st[:, 6:7], st[:, 6:7]), ve_s))
                    _inc(v.scalar_tensor_tensor(st[:, 7:8], st[:, 2:3], -1.0, st[:, 6:7],
                                                ALU.mult, ALU.mult), ve_a)
                    if t >= 1:
                        _ln_evac(v, t - 1)
                _ln_evac(v, NKT - 1)

            @blk.scalar
            def _(sc):
                for t in range(NKT):
                    sl = t % 2
                    st = lst[sl]
                    _w(sc, ve_a, lnv + t * 2 + 1)
                    _inc(sc.activation(st[:, 6:7], st[:, 5:6], AF.Sqrt, bias=0.0, scale=1.0),
                         ac_a)
                    _w(sc, ve_a, lnv + t * 2 + 2)
                    _inc(sc.activation(xnt[sl][:, :], xin[sl][:, :], AF.Identity,
                                       bias=st[:, 7:8], scale=st[:, 6:7]), ac_a)

            @blk.tensor
            def _(te):
                for t in range(NKT):
                    sl = t % 2
                    if t == 0:
                        _w(te, d_const, CONST_N)
                    _w(te, ac_a, lna + t * 2 + 2)
                    _w(te, ve_b, lne + (t - 2) + 1)
                    for hh in range(HT):
                        ins = te.transpose(pt_ln[sl][:, hh * 128:(hh + 1) * 128],
                                           xnt[sl][:, hh * 128:(hh + 1) * 128], identb[:, :])
                    _inc(ins, pe_a)

        LN_END = [(pe_a, pe_a.n), (ve_a, ve_a.n), (ve_b, ve_b.n), (ac_a, ac_a.n)]
        ln_ctx.close()
        if upto <= 1:
            qkv_ctx = None
            lnqkv_ctx.close(); attn_ctx.close()
            emit_final()
            nc._dbg = dict(out=outd, xnTo=xnTo, big0=big0, sumT=sumT, xkv=xkv)
            return nc

        qkv_ctx = ExitStack()
        wr = [qkv_ctx.enter_context(nc.sbuf_tensor(f"wr{i}", [128, HT * 256], BF16))
              for i in range(3)]
        cqs = qkv_ctx.enter_context(nc.sbuf_tensor("cqs", [128, NTT * 64], F32))
        sqss = qkv_ctx.enter_context(nc.sbuf_tensor("sqss", [128, NTT * 64], F32))
        cks = qkv_ctx.enter_context(nc.sbuf_tensor("cks", [128, NKT * 64], F32))
        sks = qkv_ctx.enter_context(nc.sbuf_tensor("sks", [128, NKT * 64], F32))
        bqbs = qkv_ctx.enter_context(nc.sbuf_tensor("bqbs", [128, D], BF16))
        bkbs = qkv_ctx.enter_context(nc.sbuf_tensor("bkbs", [128, 512], BF16))
        bvbs = qkv_ctx.enter_context(nc.sbuf_tensor("bvbs", [128, 512], BF16))
        qscr = qkv_ctx.enter_context(nc.sbuf_tensor("qscr", [128, 256], F32))
        rop = qkv_ctx.enter_context(nc.sbuf_tensor("rop", [128, 512], F32))
        qrot = [qkv_ctx.enter_context(nc.sbuf_tensor(f"qrot{i}", [128, 256], BF16))
                for i in range(2)]
        pq = [qkv_ctx.enter_context(nc.psum_tensor(f"pq{i}", [128, 256], F32))
              for i in range(4)]
        ptq = [qkv_ctx.enter_context(nc.psum_tensor(f"ptq{i}", [128, 256], BF16))
               for i in range(2)]

        qkv_items = []
        for c in range(10):
            if c < 8:
                for tt in range(NTT):
                    qkv_items.append(("q", c, tt))
            else:
                for kt in range(NKT):
                    qkv_items.append(("k", c - 8, kt))
        for c in range(2):
            for kt in range(NKT):
                qkv_items.append(("v", c, kt))
        NQKV = len(qkv_items)
        thru = []
        acc = 0
        for c in range(12):
            acc += NTT if c < 8 else NKT
            thru.append(acc)

        qp, qv, qv2, qv3, qp2 = pe_a.n, ve_a.n, ve_b.n, ve_c.n, pe_b.n
        qd = d_const.n
        with nc.Block("qkv") as blk:
            wr_wait = []
            rope_ves = {}

            @blk.gpsimd
            def _(g):
                for c in range(12):
                    sl = c % 3
                    if c == 0:
                        for sm, n in LN_END:
                            _w(g, sm, n)
                    if c >= 3:
                        _w(g, pe_a, qp + thru[c - 3])
                    if c < 8:
                        src = wqd.ap().rearrange("(h p) q -> p h q", p=128)[:, :, c * 256:(c + 1) * 256]
                    elif c < 10:
                        src = wkd.ap().rearrange("(h p) q -> p h q", p=128)[:, :, (c - 8) * 256:(c - 7) * 256]
                    else:
                        src = wvd.ap().rearrange("(h p) q -> p h q", p=128)[:, :, (c - 10) * 256:(c - 9) * 256]
                    g.dma_start(wr[sl][:, :].rearrange("p (h q) -> p h q", q=256), src)\
                        .then_inc(d_wr[sl].h, 16)
                    d_wr[sl].n += 16
                    wr_wait.append(d_wr[sl].n)

            @blk.sync
            def _(s):
                for sm, n in LN_END:
                    _w(s, sm, n)
                for dst, src in ((cqs, cqtd), (sqss, sqtd), (cks, cktd), (sks, sktd),
                                 (bqbs, bqbd), (bkbs, bkbd), (bvbs, bvbd)):
                    s.dma_start(dst[:, :], src.ap()).then_inc(d_const.h, 16)
                    d_const.n += 16

            def _qkv_tr(te, j):
                if qkv_items[j][0] == "v":
                    return
                _w(te, ve_c, qv3 + j + 1)
                _w(te, ve_b, qv2 + (j - 2) + 1)
                for u in range(2):
                    ins = te.transpose(ptq[j % 2][:, u * 128:(u + 1) * 128],
                                       qrot[j % 2][:, u * 128:(u + 1) * 128],
                                       identb[:, :])
                _inc(ins, pe_b)

            @blk.tensor
            def _(te):
                for i in range(NQKV + 2):
                    if i < NQKV:
                        kind, c, sub = qkv_items[i]
                        cg = {"q": c, "k": 8 + c, "v": 10 + c}[kind]
                        sl = cg % 3
                        if i == 0:
                            _w(te, ve_b, lne + NKT)
                        if sub == 0:
                            _w(te, d_wr[sl], wr_wait[cg])
                        _w(te, ve_a, qv + (i - 4) + 1)
                        ps = pq[i % 4]
                        for hh in range(HT):
                            lhsT = (xnTo_t(hh)[:, sub * 128:(sub + 1) * 128] if kind == "q"
                                    else xn_kv(hh, sub))
                            ins = te.matmul(ps[:, :], lhsT, wr[sl][:, hh * 256:(hh + 1) * 256],
                                            start=(hh == 0), stop=(hh == HT - 1))
                        _inc(ins, pe_a)
                    if i >= 2:
                        _qkv_tr(te, i - 2)

            @blk.vector
            def _(v):
                for i, (kind, c, sub) in enumerate(qkv_items):
                    if i == 0:
                        _w(v, d_const, qd + 7 * 16)
                    ps = pq[i % 4]
                    _w(v, pe_a, qp + i + 1)
                    if kind == "v":
                        _inc(v.tensor_tensor(
                            Vs[:, sub * 512 + c * 256:sub * 512 + (c + 1) * 256],
                            ps[:, :], bvbs[:, c * 256:(c + 1) * 256], ALU.add), ve_a)
                        continue
                    bias = (bqbs if kind == "q" else bkbs)[:, c * 256:(c + 1) * 256]
                    if i >= 1:
                        _w(v, ve_s, rope_ves[i - 1])
                        _w(v, ve_c, qv3 + i)
                    _inc(v.tensor_tensor(qscr[:, :], ps[:, :], bias, ALU.add), ve_a)
                    _w(v, ve_a, qv + i + 1)
                    if kind == "q":
                        ctab = cqs[:, sub * 64:(sub + 1) * 64]
                        stab = sqss[:, sub * 64:(sub + 1) * 64]
                    else:
                        ctab = cks[:, sub * 64:(sub + 1) * 64]
                        stab = sks[:, sub * 64:(sub + 1) * 64]
                    _w(v, pe_b, qp2 + (i - 2) + 1)
                    qr = qrot[i % 2]
                    for hd in range(2):
                        o = hd * 128
                        r = hd * 256
                        q1 = qscr[:, o:o + 64]
                        q2 = qscr[:, o + 64:o + 128]
                        v.tensor_tensor(rop[:, r:r + 64], q1, ctab, ALU.mult)
                        v.tensor_tensor(rop[:, r + 64:r + 128], q2, stab, ALU.mult)
                        v.tensor_tensor(rop[:, r + 128:r + 192], q1, stab, ALU.mult)
                        ins = v.tensor_tensor(rop[:, r + 192:r + 256], q2, ctab, ALU.mult)
                    rope_ves[i] = _inc(ins, ve_s)
                    _w(v, ve_s, rope_ves[i])
                    for hd in range(2):
                        o = hd * 128
                        r = hd * 256
                        v.tensor_sub(qr[:, o:o + 64], rop[:, r:r + 64], rop[:, r + 64:r + 128])
                        ins = v.tensor_add(qr[:, o + 64:o + 128], rop[:, r + 128:r + 192],
                                           rop[:, r + 192:r + 256])
                    _inc(ins, ve_c)
                    _w(v, pe_b, qp2 + i + 1)
                    src3 = ptq[i % 2][:, :].rearrange("p (u c2) -> p u c2", c2=128)
                    if kind == "q":
                        dst3 = big0[:, :].rearrange("p (hh t) -> p hh t", t=TOK)[
                            :, 2 * c:2 * c + 2, sub * 128:(sub + 1) * 128]
                    else:
                        dst3 = KTs[:, :].rearrange("p (gg t) -> p gg t", t=KV)[
                            :, 2 * c:2 * c + 2, sub * 128:(sub + 1) * 128]
                    _inc(v.tensor_copy(dst3, src3), ve_b)

        qkv_ctx.close()
        lnqkv_ctx.close()
        if upto <= 2:
            attn_ctx.close()
            emit_final()
            nc._dbg = dict(out=outd, xnTo=xnTo, big0=big0, sumT=sumT, xkv=xkv)
            return nc

        sc_ctx = ExitStack()
        attnT = sc_ctx.enter_context(nc.sbuf_tensor("attnT", [128, HT * TOK], BF16))
        sm_ctx = ExitStack()
        masks = sm_ctx.enter_context(nc.sbuf_tensor("masks", [128, NTT * SLOTS * 128], BF16))
        ssb = [sm_ctx.enter_context(nc.sbuf_tensor(f"ssb{i}", [128, SLOTS * 128], F32))
               for i in range(2)]
        pnb = [sm_ctx.enter_context(nc.sbuf_tensor(f"pnb{i}", [128, SLOTS * 128], BF16))
               for i in range(4)]
        pnT = [sm_ctx.enter_context(nc.sbuf_tensor(f"pnT{i}", [128, SLOTS * 128], BF16))
               for i in range(4)]
        nmz = sm_ctx.enter_context(nc.sbuf_tensor("nmz", [128, 12], F32))
        psc = [sm_ctx.enter_context(nc.psum_tensor(f"psc{i}", [128, SLOTS * 128], F32))
               for i in range(2)]
        ppn = [sm_ctx.enter_context(nc.psum_tensor(f"ppn{i}", [128, SLOTS * 128], BF16))
               for i in range(2)]
        pwv = [sm_ctx.enter_context(nc.psum_tensor(f"pwv{i}", [128, 512], F32))
               for i in range(2)]

        sp1, sp2, sp3 = pe_a.n, pe_b.n, pe_c.n
        sv1, sv2, sv3 = ve_a.n, ve_b.n, ve_c.n
        sa1, sa2, sa3 = ac_a.n, ac_b.n, ac_c.n
        sxd = d_x[0].n

        def slot_kt(qb, j):
            return qb + j if j < 5 else 12

        with nc.Block("scores") as blk:
            @blk.sync
            def _(s):
                _w(s, pe_a, sp1)
                _w(s, pe_b, sp2)
                _w(s, ve_b, sv2)
                s.dma_start(masks[:, :], maskd.ap()).then_inc(d_x[0].h, 16)
                d_x[0].n += 16

            def _sc_mm(te, i):
                hh, qb = divmod(i, NTT)
                g = hh // 4
                _w(te, ve_a, sv1 + (i - 2) + 1)
                qtile = big0[:, hh * TOK + qb * 128:hh * TOK + (qb + 1) * 128]
                base = g * KV
                te.matmul(psc[i % 2][:, 0:512], qtile,
                          KTs[:, base + qb * 128:base + qb * 128 + 512],
                          start=True, stop=True)
                te.matmul(psc[i % 2][:, 512:640], qtile,
                          KTs[:, base + (qb + 4) * 128:base + (qb + 5) * 128],
                          start=True, stop=True)
                ins = te.matmul(psc[i % 2][:, 640:768], qtile,
                                KTs[:, base + 12 * 128:base + 13 * 128],
                                start=True, stop=True)
                _inc(ins, pe_a)

            def _sc_tr(te, j):
                _w(te, ac_b, sa2 + j + 1)
                _w(te, ve_c, sv3 + (j - 2) + 1)
                for u in range(SLOTS):
                    ins = te.transpose(ppn[j % 2][:, u * 128:(u + 1) * 128],
                                       pnb[j % 4][:, u * 128:(u + 1) * 128],
                                       identb[:, :])
                _inc(ins, pe_b)

            def _sc_wv(te, w):
                hh, qb = divmod(w, NTT)
                g = hh // 4
                q4 = w // 4
                _w(te, ve_c, sv3 + w + 1)
                if w % 4 == 0:
                    _w(te, ac_c, sa3 + (q4 - 2) + 1)
                for j in range(SLOTS):
                    kt = slot_kt(qb, j)
                    ins = te.matmul(
                        pwv[q4 % 2][:, (w % 4) * 128:(w % 4 + 1) * 128],
                        Vs[:, kt * 512 + g * 128:kt * 512 + (g + 1) * 128],
                        pnT[w % 4][:, j * 128:(j + 1) * 128],
                        start=(j == 0), stop=(j == SLOTS - 1))
                _inc(ins, pe_c)

            @blk.tensor
            def _(te):
                _w(te, ve_a, sv1)
                _w(te, ve_b, sv2)
                for i in range(NH * NTT + 6):
                    if i < NH * NTT:
                        _sc_mm(te, i)
                    if i >= 4 and i - 4 < NH * NTT:
                        _sc_tr(te, i - 4)
                    if i >= 6:
                        _sc_wv(te, i - 6)

            @blk.vector
            def _(v):
                def _norm(j):
                    jl = j % 2
                    _w(v, ac_b, sa2 + (j - 2) + 1)
                    _inc(v.reciprocal(nmz[:, 4 + jl:5 + jl],
                                      nmz[:, 2 + jl:3 + jl]), ve_b)

                for i in range(NH * NTT):
                    hh, qb = divmod(i, NTT)
                    sl = i % 2
                    if i == 0:
                        _w(v, d_x[0], sxd + 16)
                    _w(v, pe_a, sp1 + i + 1)
                    _w(v, ac_a, sa1 + (i - 2) + 1)
                    _w(v, ve_s, _inc(v.tensor_tensor(ssb[sl][:, :], psc[sl][:, :],
                                                     masks[:, qb * 768:(qb + 1) * 768],
                                                     ALU.add), ve_s))
                    _inc(v.reduce_max(nmz[:, sl:sl + 1], ssb[sl][:, :], axis=AX.X,
                                      negate=True), ve_a)
                    if i >= 2:
                        _norm(i - 2)
                _w(v, ac_a, sa1 + NH * NTT)
                _norm(NH * NTT - 2)
                _norm(NH * NTT - 1)

            @blk.gpsimd
            def _(g):
                for j in range(NH * NTT):
                    jl = j % 2
                    _w(g, ac_a, sa1 + j + 1)
                    _w(g, ve_b, sv2 + j + 1)
                    _w(g, pe_b, sp2 + (j - 4) + 1)
                    _inc(g.tensor_scalar(pnb[j % 4][:, :], pnb[j % 4][:, :],
                                         nmz[:, 4 + jl:5 + jl], None, ALU.mult), ac_b)

            @blk.scalar
            def _(sc):
                def _evac(g):
                    il = 4 * g + 3
                    ghh, gqb = divmod(il, NTT)
                    _w(sc, pe_c, sp3 + il + 1)
                    _inc(sc.activation(
                        attnT[:, ghh * TOK + (gqb - 3) * 128:ghh * TOK + (gqb + 1) * 128],
                        pwv[g % 2][:, :], AF.Identity, bias=0.0, scale=1.0), ac_c)

                for i in range(NH * NTT):
                    hh, qb = divmod(i, NTT)
                    sl = i % 2
                    _w(sc, ve_a, sv1 + i + 1)
                    _w(sc, ve_b, sv2 + (i - 2) + 1)
                    _w(sc, pe_b, sp2 + (i - 4) + 1)
                    _inc(sc.activation(pnb[i % 4][:, :], ssb[sl][:, :], AF.Exp,
                                       bias=nmz[:, sl:sl + 1], scale=1.0,
                                       accum_out=nmz[:, 2 + sl:3 + sl]), ac_a)
                    if i >= 2:
                        j = i - 2
                        _w(sc, pe_b, sp2 + j + 1)
                        _w(sc, pe_c, sp3 + (j - 4) + 1)
                        _inc(sc.activation(pnT[j % 4][:, :], ppn[j % 2][:, :],
                                           AF.Identity, bias=0.0, scale=1.0), ve_c)
                    if i % 4 == 3 and i >= 7:
                        _evac(i // 4 - 1)
                for j in (NH * NTT - 2, NH * NTT - 1):
                    _w(sc, pe_b, sp2 + j + 1)
                    _w(sc, pe_c, sp3 + (j - 4) + 1)
                    _inc(sc.activation(pnT[j % 4][:, :], ppn[j % 2][:, :],
                                       AF.Identity, bias=0.0, scale=1.0), ve_c)
                _evac(NH * NTT // 4 - 1)

        sm_ctx.close()
        if upto <= 3:
            sc_ctx.close(); attn_ctx.close()
            emit_final()
            nc._dbg = dict(out=outd, xnTo=xnTo, big0=big0, sumT=sumT, xkv=xkv, attnT=attnT)
            return nc

        wo_ctx = ExitStack()
        wow = [wo_ctx.enter_context(nc.sbuf_tensor(f"wow{i}", [128, HT * 256], BF16))
               for i in range(3)]
        pwo = [wo_ctx.enter_context(nc.psum_tensor(f"pwo{i}", [128, 512], F32))
               for i in range(4)]
        op, oa = pe_a.n, ac_a.n
        WO_GATE = [(pe_b, pe_b.n), (pe_c, pe_c.n), (ve_a, ve_a.n),
                   (ve_c, ve_c.n), (ac_b, ac_b.n), (ac_c, ac_c.n)]
        with nc.Block("wo") as blk:
            wo_wait = []

            @blk.gpsimd
            def _(g):
                for c in range(8):
                    sl = c % 3
                    if c == 0:
                        for sm, n in WO_GATE:
                            _w(g, sm, n)
                    if c >= 3:
                        _w(g, pe_a, op + (c - 3 + 1) * 4)
                    g.dma_start(
                        wow[sl][:, :].rearrange("p (h q) -> p h q", q=256),
                        wod.ap().rearrange("(h p) q -> p h q", p=128)[:, :, c * 256:(c + 1) * 256]
                    ).then_inc(d_wr[sl].h, 16)
                    d_wr[sl].n += 16
                    wo_wait.append(d_wr[sl].n)

            @blk.tensor
            def _(te):
                for m in range(32):
                    co, tc = divmod(m, 2)
                    c = co // 2
                    sl = c % 3
                    if m == 0:
                        for sm, n in WO_GATE:
                            _w(te, sm, n)
                    if m % 4 == 0:
                        _w(te, d_wr[sl], wo_wait[c])
                    _w(te, ac_a, oa + (m - 4) + 1)
                    for ci in range(HT):
                        ins = te.matmul(
                            pwo[m % 4][:, :],
                            wow[sl][:, ci * 256 + (co % 2) * 128:ci * 256 + (co % 2) * 128 + 128],
                            attnT[:, ci * TOK + tc * 512:ci * TOK + (tc + 1) * 512],
                            start=(ci == 0), stop=(ci == HT - 1))
                    _inc(ins, pe_a)

            @blk.scalar
            def _(sc):
                for m in range(32):
                    co, tc = divmod(m, 2)
                    _w(sc, pe_a, op + m + 1)
                    _inc(sc.activation(sumT[:, co * TOK + tc * 512:co * TOK + (tc + 1) * 512],
                                       pwo[m % 4][:, :], AF.Identity,
                                       bias=bots[:, co:co + 1], scale=1.0), ac_a)

        wo_ctx.close()
        sc_ctx.close()
        attn_ctx.close()
        if upto <= 4:
            emit_final()
            nc._dbg = dict(out=outd, xnTo=xnTo, big0=big0, sumT=sumT, xkv=xkv)
            return nc

        ffn_ctx = ExitStack()
        m1b = [ffn_ctx.enter_context(nc.sbuf_tensor(f"m1b{i}", [128, 2 * TOK], BF16))
               for i in range(2)]
        m3b = [ffn_ctx.enter_context(nc.sbuf_tensor(f"m3b{i}", [128, 2 * TOK], BF16))
               for i in range(2)]
        mex = [ffn_ctx.enter_context(nc.sbuf_tensor(f"mex{i}", [128, 2 * TOK], BF16))
               for i in range(2)]
        rwb = [ffn_ctx.enter_context(nc.sbuf_tensor(f"rwb{i}", [128, TOK], F32))
               for i in range(2)]
        a2acc = [ffn_ctx.enter_context(nc.sbuf_tensor(f"a2acc{i}", [128, 2 * TOK], BF16))
                 for i in range(2)]
        b2rs = ffn_ctx.enter_context(nc.sbuf_tensor("b2rs", [128, 2 * D], BF16))

        rt_ctx = ExitStack()
        wgts = rt_ctx.enter_context(nc.sbuf_tensor("wgts", [128, 128], BF16))
        a1ts = rt_ctx.enter_context(nc.sbuf_tensor("a1ts", [128, HT * ER], BF16))
        a3ts = rt_ctx.enter_context(nc.sbuf_tensor("a3ts", [128, HT * ER], BF16))
        lg = rt_ctx.enter_context(nc.sbuf_tensor("lg", [8, TOK], F32))
        lgt = rt_ctx.enter_context(nc.sbuf_tensor("lgt", [128, 64], F32))
        oh = [rt_ctx.enter_context(nc.sbuf_tensor(f"oh{i}", [128, 64], F32)) for i in range(2)]
        lg2 = rt_ctx.enter_context(nc.sbuf_tensor("lg2", [128, 8], F32))
        mcol = rt_ctx.enter_context(nc.sbuf_tensor("mcol", [128, 4], F32))
        dall = rt_ctx.enter_context(nc.sbuf_tensor("dall", [128, 8], F32))
        eall = rt_ctx.enter_context(nc.sbuf_tensor("eall", [128, 8], F32))
        rwa = [rt_ctx.enter_context(nc.sbuf_tensor(f"rwa{i}", [128, 8], F32)) for i in range(2)]
        ohT = [rt_ctx.enter_context(nc.sbuf_tensor(f"ohT{i}", [8, TOK], BF16)) for i in range(2)]
        rwT = [rt_ctx.enter_context(nc.sbuf_tensor(f"rwT{i}", [1, TOK], BF16)) for i in range(2)]
        pbig = [rt_ctx.enter_context(nc.psum_tensor(f"pbig{i}", [128, 512], F32))
                for i in range(4)]
        plog = rt_ctx.enter_context(nc.psum_tensor("plog", [8, TOK], F32))
        poh = rt_ctx.enter_context(nc.psum_tensor("poh", [8, TOK], F32))

        rp, rv, ra = pe_a.n, ve_a.n, ac_a.n
        rsm = d_sm.n
        with nc.Block("routing") as blk:
            @blk.gpsimd
            def _(g):
                _w(g, pe_a, rp)
                g.dma_start(wgts[:, :], wgtd.ap()).then_inc(d_sm.h, 16); d_sm.n += 16
                g.dma_start(a1ts[:, :].rearrange("p (h e) -> p h e", e=ER),
                            a1td.ap().rearrange("(h p) e -> p h e", p=128))\
                    .then_inc(d_sm.h, 16); d_sm.n += 16
                g.dma_start(a3ts[:, :].rearrange("p (h e) -> p h e", e=ER),
                            a3td.ap().rearrange("(h p) e -> p h e", p=128))\
                    .then_inc(d_sm.h, 16); d_sm.n += 16
                g.dma_start(b2rs[:, :].rearrange("p (q h) -> p q h", h=D),
                            b2rd.ap().rearrange("(q p) h -> p q h", p=128))\
                    .then_inc(d_b2.h, 16); d_b2.n += 16

            @blk.tensor
            def _(te):
                _w(te, d_sm, rsm + 48)
                _w(te, ac_a, ra)
                for tc in range(2):
                    for hh in range(HT):
                        ins = te.matmul(plog[:, tc * 512:(tc + 1) * 512],
                                        wgts[:, hh * 8:(hh + 1) * 8],
                                        xnTo_s(hh, tc), start=(hh == 0), stop=(hh == HT - 1))
                _inc(ins, pe_a)
                _w(te, ve_a, rv + 1)
                for tt in range(NTT):
                    ins = te.transpose(pbig[3][:, tt * 8:(tt + 1) * 8],
                                       lg[:, tt * 128:(tt + 1) * 128], identf[0:8, 0:8])
                _inc(ins, pe_a)
                _w(te, ve_a, rv + 10)
                for tt in range(NTT):
                    ins = te.transpose(poh[:, tt * 128:(tt + 1) * 128],
                                       oh[0][:, tt * 8:(tt + 1) * 8], identf[:, :])
                _inc(ins, pe_a)
                _w(te, ve_a, rv + 11)
                for tt in range(NTT):
                    pb = pbig[2 + tt // 4]
                    ins = te.transpose(pb[0:1, (tt % 4) * 128:(tt % 4 + 1) * 128],
                                       rwa[0][:, tt:tt + 1], identf[:, :])
                _inc(ins, pe_a)
                _w(te, ve_a, rv + 12)
                for tt in range(NTT):
                    ins = te.transpose(poh[:, tt * 128:(tt + 1) * 128],
                                       oh[1][:, tt * 8:(tt + 1) * 8], identf[:, :])
                _inc(ins, pe_a)
                _w(te, ve_a, rv + 13)
                for tt in range(NTT):
                    pb = pbig[2 + tt // 4]
                    ins = te.transpose(pb[0:1, (tt % 4) * 128:(tt % 4 + 1) * 128],
                                       rwa[1][:, tt:tt + 1], identf[:, :])
                _inc(ins, pe_a)
                for k in range(2):
                    _w(te, ve_a, rv + 15 + k)
                    for q in range(2):
                        for tc in range(2):
                            ins = te.matmul(pbig[q * 2 + tc][:, :],
                                            e8s[0:8, q * 128:(q + 1) * 128],
                                            ohT[k][0:8, tc * 512:(tc + 1) * 512],
                                            start=True, stop=True)
                    _inc(ins, pe_a)
                for k in range(2):
                    _w(te, ve_a, rv + 17)
                    for tc in range(2):
                        ins = te.matmul(pbig[k * 2 + tc][:, :], ones1[:, :],
                                        rwT[k][0:1, tc * 512:(tc + 1) * 512],
                                        start=True, stop=True)
                    _inc(ins, pe_a)
                _w(te, d_sm, rsm + 48)
                for si, src in enumerate((a1ts, a3ts)):
                    _w(te, ve_a, rv + 19 + si)
                    for q in range(2):
                        for tc in range(2):
                            for hh in range(HT):
                                ins = te.matmul(pbig[q * 2 + tc][:, :],
                                                src[:, hh * ER + q * 128:hh * ER + q * 128 + 128],
                                                xnTo_s(hh, tc),
                                                start=(hh == 0), stop=(hh == HT - 1))
                    _inc(ins, pe_a)

            @blk.vector
            def _(v):
                _w(v, pe_a, rp + 1)
                _inc(v.tensor_copy(lg[:, :], plog[:, :]), ve_a)
                _w(v, pe_a, rp + 2)
                _inc(v.tensor_copy(lgt[:, :], pbig[3][:, 0:64]), ve_a)
                _w(v, ve_a, rv + 2)
                for tt in range(NTT):
                    sl_ = lgt[:, tt * 8:(tt + 1) * 8]
                    if tt >= 1:
                        _w(v, ve_a, rv + 2 + tt)
                    _w(v, ve_s, _inc(v.reduce_max(mcol[:, 0:1], sl_, axis=AX.X), ve_s))
                    _w(v, ve_s, _inc(v.tensor_scalar(oh[0][:, tt * 8:(tt + 1) * 8], sl_,
                                                     mcol[:, 0:1], None, ALU.is_ge), ve_s))
                    _w(v, ve_s, _inc(v.scalar_tensor_tensor(lg2[:, :], oh[0][:, tt * 8:(tt + 1) * 8],
                                                            -1.0e30, sl_, ALU.mult, ALU.add), ve_s))
                    _w(v, ve_s, _inc(v.reduce_max(mcol[:, 1:2], lg2[:, :], axis=AX.X), ve_s))
                    v.tensor_scalar(oh[1][:, tt * 8:(tt + 1) * 8], lg2[:, :], mcol[:, 1:2],
                                    None, ALU.is_ge)
                    _inc(v.tensor_sub(dall[:, tt:tt + 1], mcol[:, 1:2], mcol[:, 0:1]),
                         ve_a)
                _w(v, ac_a, ra + 1)
                _w(v, ve_s, _inc(v.tensor_scalar(eall[:, :], eall[:, :], 1.0, None,
                                                 ALU.add), ve_s))
                _w(v, ve_s, _inc(v.reciprocal(rwa[0][:, :], eall[:, :]), ve_s))
                _inc(v.tensor_scalar(rwa[1][:, :], rwa[0][:, :], -1.0, 1.0,
                                     ALU.mult, ALU.add), ve_a)
                _w(v, pe_a, rp + 3)
                _inc(v.tensor_copy(ohT[0][0:8, :], poh[0:8, :]), ve_a)
                _w(v, pe_a, rp + 4)
                v.tensor_copy(rwT[0][0:1, 0:512], pbig[2][0:1, :])
                _inc(v.tensor_copy(rwT[0][0:1, 512:1024], pbig[3][0:1, :]), ve_a)
                _w(v, pe_a, rp + 5)
                _inc(v.tensor_copy(ohT[1][0:8, :], poh[0:8, :]), ve_a)
                _w(v, pe_a, rp + 6)
                v.tensor_copy(rwT[1][0:1, 0:512], pbig[2][0:1, :])
                _inc(v.tensor_copy(rwT[1][0:1, 512:1024], pbig[3][0:1, :]), ve_a)
                for k in range(2):
                    _w(v, pe_a, rp + 7 + k)
                    for q in range(2):
                        for tc in range(2):
                            ins = v.tensor_copy(
                                mex[k][:, q * TOK + tc * 512:q * TOK + (tc + 1) * 512],
                                pbig[q * 2 + tc][:, :])
                    _inc(ins, ve_a)
                for k in range(2):
                    _w(v, pe_a, rp + 9 + k)
                    for tc in range(2):
                        ins = v.tensor_copy(rwb[k][:, tc * 512:(tc + 1) * 512],
                                            pbig[k * 2 + tc][:, :])
                    _inc(ins, ve_a)
                _w(v, ve_a, rv + 17)
                for si, mb in enumerate((m1b, m3b)):
                    _w(v, pe_a, rp + 11 + si)
                    for k in range(2):
                        for q in range(2):
                            for tc in range(2):
                                sl_ = slice(q * TOK + tc * 512, q * TOK + (tc + 1) * 512)
                                ins = v.tensor_tensor(mb[k][:, sl_], pbig[q * 2 + tc][:, :],
                                                      mex[k][:, sl_], ALU.mult)
                    _inc(ins, ve_a)
                v.memset(a2acc[0][:, :], 0.0)
                v.memset(a2acc[1][:, :], 0.0)

            @blk.scalar
            def _(sc):
                _w(sc, ve_a, rv + 10)
                _inc(sc.activation(eall[:, :], dall[:, :], AF.Exp, bias=0.0, scale=1.0),
                     ac_a)

        B2RS_N = d_b2.n
        rt_ctx.close()
        if upto <= 5:
            ffn_ctx.close()
            emit_final()
            nc._dbg = dict(out=outd, xnTo=xnTo, big0=big0, sumT=sumT, xkv=xkv,
                           m1b=m1b, m3b=m3b, mex=mex, rwb=rwb)
            return nc

        f_ctx = ExitStack()
        w1b = [f_ctx.enter_context(nc.sbuf_tensor(f"w1b{i}", [128, HT * 128], BF16))
               for i in range(2)]
        w3b = [f_ctx.enter_context(nc.sbuf_tensor(f"w3b{i}", [128, HT * 128], BF16))
               for i in range(2)]
        b1re = [f_ctx.enter_context(nc.sbuf_tensor(f"b1re{i}", [128, 2 * 512], BF16))
                for i in range(2)]
        b3re = [f_ctx.enter_context(nc.sbuf_tensor(f"b3re{i}", [128, 2 * 512], BF16))
                for i in range(2)]
        a2te = [f_ctx.enter_context(nc.sbuf_tensor(f"a2te{i}", [128, SUBS * ER], BF16))
                for i in range(2)]
        w2x = f_ctx.enter_context(nc.sbuf_tensor("w2x", [128, 2 * D], BF16))
        w2slot_ap[2] = w2x[:, :]
        b1sb = [f_ctx.enter_context(nc.sbuf_tensor(f"b1sb{i}", [128, 512], F32))
                for i in range(2)]
        b3sb = [f_ctx.enter_context(nc.sbuf_tensor(f"b3sb{i}", [128, 512], F32))
                for i in range(2)]
        x1sb = [f_ctx.enter_context(nc.sbuf_tensor(f"x1sb{i}", [128, 512], F32))
                for i in range(2)]
        s1sb = [f_ctx.enter_context(nc.sbuf_tensor(f"s1sb{i}", [128, 512], F32))
                for i in range(2)]
        x3sb = [f_ctx.enter_context(nc.sbuf_tensor(f"x3sb{i}", [128, 512], F32))
                for i in range(2)]
        pfa = [f_ctx.enter_context(nc.psum_tensor(f"pfa{i}", [128, 512], F32))
               for i in range(8)]

        be_wait = {}
        w2_wait = {}

        w1_wait = {}
        w3_wait = {}

        def _ffn_w13(g, c):
            sl = c % 2
            g.dma_start(w1b[sl][:, :].rearrange("p (h f) -> p h f", f=128),
                        w1td.ap().rearrange("(h p) f -> p h f", p=128)[:, :, c * 128:(c + 1) * 128])\
                .then_inc(d_w1[sl].h, 16)
            d_w1[sl].n += 16
            w1_wait[c] = d_w1[sl].n
            g.dma_start(w3b[sl][:, :].rearrange("p (h f) -> p h f", f=128),
                        w3td.ap().rearrange("(h p) f -> p h f", p=128)[:, :, c * 128:(c + 1) * 128])\
                .then_inc(d_w3[sl].h, 16)
            d_w3[sl].n += 16
            w3_wait[c] = d_w3[sl].n

        def _ffn_bundle(g, e):
            sl = e % 2
            g.dma_start(b1re[sl][:, :].rearrange("p (q f) -> p q f", f=512),
                        b1rd.ap().rearrange("(q p) f -> p q f", p=128)[:, :, e * 512:(e + 1) * 512])\
                .then_inc(d_be[sl].h, 16)
            d_be[sl].n += 16
            g.dma_start(b3re[sl][:, :].rearrange("p (q f) -> p q f", f=512),
                        b3rd.ap().rearrange("(q p) f -> p q f", p=128)[:, :, e * 512:(e + 1) * 512])\
                .then_inc(d_be[sl].h, 16)
            d_be[sl].n += 16
            g.dma_start(a2te[sl][:, :].rearrange("p (j e2) -> p j e2", e2=ER),
                        a2td.ap().rearrange("(f p) e2 -> p f e2", p=128)[:, e * SUBS:(e + 1) * SUBS, :])\
                .then_inc(d_be[sl].h, 16)
            d_be[sl].n += 16
            be_wait[e] = d_be[sl].n

        def _ffn_w2(g, he):
            sl = he % 3
            g.dma_start(w2slot_ap[sl].rearrange("p (j h) -> p j h", h=D),
                        w2td.ap().rearrange("(f p) h -> p f h", p=128)[:, he * 2:(he + 1) * 2, :])\
                .then_inc(d_w2[sl].h, 16)
            d_w2[sl].n += 16
            w2_wait[he] = d_w2[sl].n


        NFC_RUN = NFC if upto > 7 else 1
        EMIT_FD = upto != 6

        FB, FD = [], []
        _cur = dict(pe_a=pe_a.n, pe_b=pe_b.n, ve_a=ve_a.n, ve_b=ve_b.n,
                    ve_c=ve_c.n, ac_a=ac_a.n, ac_b=ac_b.n, po_a=po_a.n)
        for e in range(NFC_RUN):
            FB.append(dict(_cur))
            _cur["pe_a"] += 8; _cur["pe_b"] += 16; _cur["ve_a"] += 16
            _cur["ac_a"] += 8; _cur["ac_b"] += 16; _cur["po_a"] += 16
            FD.append(dict(_cur))
            if EMIT_FD:
                _cur["pe_a"] += 8; _cur["pe_b"] += 32; _cur["ve_a"] += 8
                _cur["ve_b"] += 32
                if e == NFC - 1:
                    _cur["ve_c"] += 2

        def fb_gpsimd(g, e):
            bp1, bp2, bva = FB[e]["pe_a"], FB[e]["pe_b"], FB[e]["ve_a"]
            if e == 0:
                _w(g, pe_a, bp1)
                _w(g, ve_a, bva)
                _ffn_bundle(g, 0)
                _ffn_w13(g, 0)
                _ffn_w13(g, 1)
                _ffn_w2(g, 0)
                _ffn_w2(g, 1)
            if e > 0:
                _w(g, pe_b, bp2)
                _ffn_w2(g, 2 * e)
                _ffn_w2(g, 2 * e + 1)
            if e + 1 < NFC:
                _ffn_bundle(g, e + 1)
            _w(g, pe_a, bp1 + 2)
            _ffn_w13(g, e * SUBS + 2)
            _w(g, pe_a, bp1 + 4)
            _ffn_w13(g, e * SUBS + 3)
            if e + 1 < NFC:
                _w(g, pe_a, bp1 + 6)
                _ffn_w13(g, (e + 1) * SUBS)
                _w(g, pe_a, bp1 + 8)
                _ffn_w13(g, (e + 1) * SUBS + 1)

        def fb_tensor(te, e):
            bva, bac = FB[e]["ve_a"], FB[e]["ac_a"]
            for m in range(SUBS * 2):
                j, tc = divmod(m, 2)
                c = e * SUBS + j
                sl = c % 2
                if m == 0:
                    _w(te, ve_a, bva)
                    _w(te, ve_b, FB[e]["ve_b"])
                if tc == 0:
                    _w(te, d_w1[sl], w1_wait[c])
                    _w(te, d_w3[sl], w3_wait[c])
                    if j == 0:
                        _w(te, d_be[e % 2], be_wait[e])
                _w(te, ac_a, bac + (m - 2) + 1)
                for hh in range(HT):
                    te.matmul(pfa[m % 2][:, :], w1b[sl][:, hh * 128:(hh + 1) * 128],
                              xnTo_s(hh, tc), start=(hh == 0), stop=(hh == HT - 1))
                for hh in range(HT):
                    ins = te.matmul(pfa[2 + m % 2][:, :],
                                    w3b[sl][:, hh * 128:(hh + 1) * 128],
                                    xnTo_s(hh, tc), start=(hh == 0), stop=(hh == HT - 1))
                _inc(ins, pe_a)
                for k in range(2):
                    mk = m * 2 + k
                    _w(te, ve_a, bva + (mk - 2) + 1)
                    for q in range(2):
                        te.matmul(pfa[4 + k][:, :],
                                  b1re[e % 2][:, q * 512 + j * 128:q * 512 + (j + 1) * 128],
                                  m1b[k][:, q * TOK + tc * 512:q * TOK + (tc + 1) * 512],
                                  start=(q == 0), stop=(q == 1))
                    for q in range(2):
                        ins = te.matmul(pfa[6 + k][:, :],
                                        b3re[e % 2][:, q * 512 + j * 128:q * 512 + (j + 1) * 128],
                                        m3b[k][:, q * TOK + tc * 512:q * TOK + (tc + 1) * 512],
                                        start=(q == 0), stop=(q == 1))
                    _inc(ins, pe_b)

        def fb_scalar(sc, e):
            bp1, bva, bpo = FB[e]["pe_a"], FB[e]["ve_a"], FB[e]["po_a"]
            for m in range(SUBS * 2):
                j, tc = divmod(m, 2)
                _w(sc, pe_a, bp1 + m + 1)
                _w(sc, ve_a, bva + (m - 2) * 2 + 2)
                sc.activation(b1sb[m % 2][:, :], pfa[m % 2][:, :], AF.Identity,
                              bias=0.0, scale=1.0)
                _inc(sc.activation(b3sb[m % 2][:, :], pfa[2 + m % 2][:, :],
                                   AF.Identity, bias=0.0, scale=1.0), ac_a)
                for k in range(2):
                    mk = m * 2 + k
                    _w(sc, ve_a, bva + mk + 1)
                    _w(sc, po_a, bpo + (mk - 2) + 1)
                    _inc(sc.activation(s1sb[k][:, :], x1sb[k][:, :], AF.Silu,
                                       bias=0.0, scale=1.0), ac_b)

        def fb_vector(v, e):
            bp2, bac, bab, bpo = (FB[e]["pe_b"], FB[e]["ac_a"], FB[e]["ac_b"],
                                  FB[e]["po_a"])

            def pool_part(mk2):
                m2, k2 = divmod(mk2, 2)
                j2, tc2 = divmod(m2, 2)
                _w(v, ac_b, bab + mk2 + 1)
                _w(v, po_s, _inc(v.tensor_tensor(
                    x3sb[k2][:, :], x3sb[k2][:, :],
                    rwb[k2][:, tc2 * 512:(tc2 + 1) * 512], ALU.mult), po_s))
                _inc(v.tensor_tensor(
                    x2v(k2)[:, j2 * TOK + tc2 * 512:j2 * TOK + (tc2 + 1) * 512],
                    s1sb[k2][:, :], x3sb[k2][:, :], ALU.mult), po_a)

            for m in range(SUBS * 2):
                j, tc = divmod(m, 2)
                for k in range(2):
                    mk = m * 2 + k
                    _w(v, pe_b, bp2 + mk + 1)
                    _w(v, ac_a, bac + m + 1)
                    _w(v, ac_b, bab + (mk - 2) + 1)
                    v.tensor_tensor(x1sb[k][:, :], pfa[4 + k][:, :],
                                    b1sb[m % 2][:, :], ALU.add)
                    if mk >= 2:
                        pool_part(mk - 2)
                    _w(v, po_a, bpo + (mk - 2) + 1)
                    _inc(v.tensor_tensor(x3sb[k][:, :], pfa[6 + k][:, :],
                                         b3sb[m % 2][:, :], ALU.add), ve_a)
            pool_part(14)
            pool_part(15)

        def fd_tensor(te, e):
            dva, dvb, dvc, dpo = (FD[e]["ve_a"], FD[e]["ve_b"], FD[e]["ve_c"],
                                  FD[e]["po_a"])
            for idx in range(8):
                k, r = divmod(idx, 4)
                q, tc = divmod(r, 2)
                if idx == 0:
                    _w(te, po_a, dpo)
                for j in range(SUBS):
                    ins = te.matmul(
                        pfa[idx][:, :],
                        a2te[e % 2][:, j * ER + q * 128:j * ER + q * 128 + 128],
                        x2v(k)[:, j * TOK + tc * 512:j * TOK + (tc + 1) * 512],
                        start=(j == 0), stop=(j == SUBS - 1))
                _inc(ins, pe_a)
            for gidx in range(32):
                h, tc = divmod(gidx, 2)
                if gidx == 0:
                    _w(te, ve_a, dva + 8)
                    _w(te, d_w2[(2 * e) % 3], w2_wait[2 * e])
                    _w(te, d_w2[(2 * e + 1) % 3], w2_wait[2 * e + 1])
                    if e == NFC - 1:
                        _w(te, ve_c, dvc + 2)
                        _w(te, d_b2, B2RS_N)
                _w(te, ve_b, dvb + (gidx - 4) + 1)
                notlast = (e < NFC - 1)
                for k in range(2):
                    for j in range(SUBS):
                        half = j // 2
                        ap_ = w2slot_ap[(2 * e + half) % 3]
                        ins = te.matmul(
                            pfa[gidx % 4][:, :],
                            ap_[:, (j % 2) * D + h * 128:(j % 2) * D + (h + 1) * 128],
                            x2v(k)[:, j * TOK + tc * 512:j * TOK + (tc + 1) * 512],
                            start=(k == 0 and j == 0),
                            stop=(notlast and k == 1 and j == SUBS - 1))
                if e == NFC - 1:
                    for k in range(2):
                        for q in range(2):
                            ins = te.matmul(
                                pfa[gidx % 4][:, :],
                                b2rs[:, q * D + h * 128:q * D + (h + 1) * 128],
                                m1b[k][:, q * TOK + tc * 512:q * TOK + (tc + 1) * 512],
                                start=False, stop=(k == 1 and q == 1))
                _inc(ins, pe_b)

        def fd_vector(v, e):
            dp1, dp2, dva = FD[e]["pe_a"], FD[e]["pe_b"], FD[e]["ve_a"]
            for idx in range(8):
                k, r = divmod(idx, 4)
                q, tc = divmod(r, 2)
                _w(v, pe_a, dp1 + idx + 1)
                dst = a2acc[k][:, q * TOK + tc * 512:q * TOK + (tc + 1) * 512]
                _inc(v.tensor_tensor(dst, pfa[idx][:, :], dst, ALU.add), ve_a)
            if e == NFC - 1:
                _w(v, ve_a, dva + 8)
                _inc(v.tensor_tensor(m1b[0][:, :], a2acc[0][:, :], mex[0][:, :],
                                     ALU.mult), ve_c)
                _inc(v.tensor_tensor(m1b[1][:, :], a2acc[1][:, :], mex[1][:, :],
                                     ALU.mult), ve_c)
            for gidx in range(32):
                h, tc = divmod(gidx, 2)
                _w(v, pe_b, dp2 + gidx + 1)
                dst = sumT[:, h * TOK + tc * 512:h * TOK + (tc + 1) * 512]
                _inc(v.tensor_tensor(dst, pfa[gidx % 4][:, :], dst, ALU.add), ve_b)

        for e0 in range(0, NFC_RUN, 2):
            es = [e for e in (e0, e0 + 1) if e < NFC_RUN]
            with nc.Block(f"f{e0}") as blk:
                @blk.gpsimd
                def _(g, es=es):
                    for e in es:
                        fb_gpsimd(g, e)

                @blk.tensor
                def _(te, es=es):
                    for e in es:
                        fb_tensor(te, e)
                        if EMIT_FD:
                            fd_tensor(te, e)

                @blk.scalar
                def _(sc, es=es):
                    for e in es:
                        fb_scalar(sc, e)

                @blk.vector
                def _(v, es=es):
                    for e in es:
                        fb_vector(v, e)
                        if EMIT_FD:
                            fd_vector(v, e)

        f_ctx.close()
        ffn_ctx.close()

        emit_final()
        nc._dbg = dict(out=outd, xnTo=xnTo, big0=big0, sumT=sumT, xkv=xkv)
    return nc


def _host_arrays(inputs):
    f32 = np.float32
    g = {}
    g["wq"] = np.ascontiguousarray(inputs["L0_attn_Wq"], f32)
    g["wk"] = np.ascontiguousarray(inputs["L0_attn_Wk"], f32)
    g["wv"] = np.ascontiguousarray(inputs["L0_attn_Wv"], f32)
    g["wo"] = np.ascontiguousarray(inputs["L0_attn_Wo"], f32)
    g["w1t"] = np.ascontiguousarray(inputs["L0_ffn_W1"].T, f32)
    g["w3t"] = np.ascontiguousarray(inputs["L0_ffn_W3"].T, f32)
    g["w2t"] = np.ascontiguousarray(inputs["L0_ffn_W2"].T, f32)
    g["a1t"] = np.ascontiguousarray(inputs["L0_ffn_A1"].reshape(ER, D).T, f32)
    g["a3t"] = np.ascontiguousarray(inputs["L0_ffn_A3"].reshape(ER, D).T, f32)
    g["a2t"] = np.ascontiguousarray(inputs["L0_ffn_A2"].reshape(ER, FF).T, f32)
    g["b1r"] = np.ascontiguousarray(inputs["L0_ffn_B1"].transpose(0, 2, 1).reshape(ER, FF), f32)
    g["b3r"] = np.ascontiguousarray(inputs["L0_ffn_B3"].transpose(0, 2, 1).reshape(ER, FF), f32)
    g["b2r"] = np.ascontiguousarray(inputs["L0_ffn_B2"].transpose(0, 2, 1).reshape(ER, D), f32)
    wgT = np.ascontiguousarray(inputs["L0_ffn_Wg"].T, f32)
    g["wgt"] = np.ascontiguousarray(
        wgT.reshape(HT, 128, 8).transpose(1, 0, 2).reshape(128, 128), f32)
    bq = np.asarray(inputs["L0_attn_bq"], f32)
    bk = np.asarray(inputs["L0_attn_bk"], f32)
    bv = np.asarray(inputs["L0_attn_bv"], f32)
    bo = np.asarray(inputs["L0_attn_bo"], f32)
    g["bqb"] = np.ascontiguousarray(np.tile(bq[None, :], (128, 1))).astype(BF)
    g["bkb"] = np.ascontiguousarray(np.tile(bk[None, :], (128, 1))).astype(BF)
    g["bvb"] = np.ascontiguousarray(np.tile(bv[None, :], (128, 1))).astype(BF)
    g["bot"] = np.ascontiguousarray(bo.reshape(16, 128).T, f32)
    e8 = np.zeros((8, ER), BF)
    for e in range(8):
        e8[e, e * 32:(e + 1) * 32] = 1
    g["e8"] = e8
    g["identb"] = np.eye(128, dtype=BF)
    g["identf"] = np.eye(128, dtype=f32)
    g["ones1"] = np.ones((1, 128), BF)
    return g


def _rope_tables(h):
    f32 = np.float32
    inv = 1.0 / (10000.0 ** (np.arange(64, dtype=np.float64) * 2.0 / 128.0))
    pos_own = h * TOK + np.arange(TOK, dtype=np.float64)
    ang_q = pos_own[:, None] * inv[None, :]
    cq = np.cos(ang_q).astype(f32) * np.float32(SCALE)
    sq = np.sin(ang_q).astype(f32) * np.float32(SCALE)
    cq = cq.reshape(NTT, 128, 64).transpose(1, 0, 2).reshape(128, NTT * 64)
    sq = sq.reshape(NTT, 128, 64).transpose(1, 0, 2).reshape(128, NTT * 64)
    pos_kv = np.zeros(NKT * 128, np.float64)
    pos_kv[:1536] = h * TOK - 512 + np.arange(1536)
    pos_kv[1536:1540] = np.arange(4)
    ang_k = pos_kv[:, None] * inv[None, :]
    ck = np.cos(ang_k).astype(f32)
    sk = np.sin(ang_k).astype(f32)
    ck = ck.reshape(NKT, 128, 64).transpose(1, 0, 2).reshape(128, NKT * 64)
    sk = sk.reshape(NKT, 128, 64).transpose(1, 0, 2).reshape(128, NKT * 64)
    return (np.ascontiguousarray(cq), np.ascontiguousarray(sq),
            np.ascontiguousarray(ck), np.ascontiguousarray(sk))


def _mask_for(h):
    m = np.full((NTT, SLOTS, 128, 128), MASKNEG, np.float32)
    for qb in range(NTT):
        i = (h * TOK + qb * 128 + np.arange(128))[:, None]
        for j in range(SLOTS):
            if j < 5:
                s = (qb + j) * 128 + np.arange(128)[None, :]
                gk = h * TOK - 512 + s
                valid = np.ones_like(gk, bool) if h == 1 else (s >= 512)
                allow = valid & (gk <= i) & ((gk >= i - 512) | (gk < 4))
            else:
                c = np.arange(128)[None, :]
                if h == 0 and qb <= 4:
                    allow = np.zeros((128, 128), bool)
                else:
                    allow = (c < 4) & (c <= i)
            m[qb, j][allow] = 0.0
    return np.ascontiguousarray(
        m.transpose(2, 0, 1, 3).reshape(128, NTT * SLOTS * 128)).astype(BF)


def kernel(**inputs):
    if "nc" not in _CACHE:
        _CACHE["nc"] = build_program()
    nc = _CACHE["nc"]

    x = np.asarray(inputs["L0_attn_x"], np.float32)
    g = _host_arrays(inputs)
    per_h = {}
    for h in range(2):
        cq, sq, ck, sk = _rope_tables(h)
        per_h[h] = dict(cqt=cq, sqt=sq, ckt=ck, skt=sk, maskb=_mask_for(h))

    in_maps = []
    for core in range(8):
        b, h = divmod(core, 2)
        halo = x[b, 512:1024] if h == 1 else np.zeros((512, D), np.float32)
        xkv = np.concatenate(
            [halo, x[b, h * TOK:(h + 1) * TOK], x[b, 0:4],
             np.zeros((124, D), np.float32)], axis=0)
        m = dict(g)
        m.update(per_h[h])
        m["xkv"] = np.ascontiguousarray(xkv)
        in_maps.append(m)

    res = run_bass_kernel_spmd(nc, in_maps, core_ids=list(range(8)))
    out = np.empty((4, D, D), np.float32)
    for core in range(8):
        b, h = divmod(core, 2)
        out[b, h * TOK:(h + 1) * TOK] = res.results[core]["out"]
    return out


# version 15
# speedup vs baseline: 1.4304x; 1.4304x over previous
import math
import numpy as np
import ml_dtypes
from contextlib import ExitStack

import concourse.bass as bass
import concourse.mybir as mybir
from concourse.bass_utils import run_bass_kernel_spmd

F32 = mybir.dt.float32
BF16 = mybir.dt.bfloat16
AF = mybir.ActivationFunctionType
ALU = mybir.AluOpType
AX = mybir.AxisListType
BF = ml_dtypes.bfloat16

D = 2048
HT = 16
TOK = 1024
NTT = 8
KV = 1664
NKT = 13
NH = 16
NG = 4
DH = 128
FF = 7168
NFC = 14
SUBS = 4
ER = 256
SLOTS = 6
EPS = 1e-5
SCALE = 1.0 / math.sqrt(DH)
MASKNEG = -1.0e9

_CACHE = {}


class _Sem:
    __slots__ = ("h", "n")

    def __init__(self, ctx, nc, name):
        self.h = ctx.enter_context(nc.semaphore(name))
        self.n = 0


def _w(eng, sem, v):
    if v > 0:
        eng.wait_ge(sem.h, v)


def _inc(ins, sem, d=1):
    ins.then_inc(sem.h, d)
    sem.n += d
    return sem.n


def build_program(upto=99):
    nc = bass.Bass("TRN2", target_bir_lowering=False)

    xkv = nc.dram_tensor("xkv", [KV, D], F32, kind="ExternalInput")
    maskd = nc.dram_tensor("maskb", [128, NTT * SLOTS * 128], BF16, kind="ExternalInput")
    cqtd = nc.dram_tensor("cqt", [128, NTT * 64], F32, kind="ExternalInput")
    sqtd = nc.dram_tensor("sqt", [128, NTT * 64], F32, kind="ExternalInput")
    cktd = nc.dram_tensor("ckt", [128, NKT * 64], F32, kind="ExternalInput")
    sktd = nc.dram_tensor("skt", [128, NKT * 64], F32, kind="ExternalInput")
    wqd = nc.dram_tensor("wq", [D, D], F32, kind="ExternalInput")
    wkd = nc.dram_tensor("wk", [D, 512], F32, kind="ExternalInput")
    wvd = nc.dram_tensor("wv", [D, 512], F32, kind="ExternalInput")
    wod = nc.dram_tensor("wo", [D, D], F32, kind="ExternalInput")
    bqbd = nc.dram_tensor("bqb", [128, D], BF16, kind="ExternalInput")
    bkbd = nc.dram_tensor("bkb", [128, 512], BF16, kind="ExternalInput")
    bvbd = nc.dram_tensor("bvb", [128, 512], BF16, kind="ExternalInput")
    botd = nc.dram_tensor("bot", [128, 16], F32, kind="ExternalInput")
    wgtd = nc.dram_tensor("wgt", [128, 128], F32, kind="ExternalInput")
    a1td = nc.dram_tensor("a1t", [D, ER], F32, kind="ExternalInput")
    a3td = nc.dram_tensor("a3t", [D, ER], F32, kind="ExternalInput")
    w1td = nc.dram_tensor("w1t", [D, FF], F32, kind="ExternalInput")
    w3td = nc.dram_tensor("w3t", [D, FF], F32, kind="ExternalInput")
    w2td = nc.dram_tensor("w2t", [FF, D], F32, kind="ExternalInput")
    a2td = nc.dram_tensor("a2t", [FF, ER], F32, kind="ExternalInput")
    b1rd = nc.dram_tensor("b1r", [ER, FF], F32, kind="ExternalInput")
    b3rd = nc.dram_tensor("b3r", [ER, FF], F32, kind="ExternalInput")
    b2rd = nc.dram_tensor("b2r", [ER, D], F32, kind="ExternalInput")
    e8d = nc.dram_tensor("e8", [8, ER], BF16, kind="ExternalInput")
    identbd = nc.dram_tensor("identb", [128, 128], BF16, kind="ExternalInput")
    identfd = nc.dram_tensor("identf", [128, 128], F32, kind="ExternalInput")
    ones1d = nc.dram_tensor("ones1", [1, 128], BF16, kind="ExternalInput")
    outd = nc.dram_tensor("out", [TOK, D], F32, kind="ExternalOutput")

    ctx = ExitStack()
    with ctx:
        d_const = _Sem(ctx, nc, "d_const")
        d_x = [_Sem(ctx, nc, f"d_x{i}") for i in range(2)]
        d_wr = [_Sem(ctx, nc, f"d_wr{i}") for i in range(3)]
        d_w1 = [_Sem(ctx, nc, f"d_w1{i}") for i in range(2)]
        d_w3 = [_Sem(ctx, nc, f"d_w3{i}") for i in range(2)]
        d_w2 = [_Sem(ctx, nc, f"d_w2{i}") for i in range(3)]
        d_be = [_Sem(ctx, nc, f"d_be{i}") for i in range(2)]
        d_sm = _Sem(ctx, nc, "d_sm")
        d_b2 = _Sem(ctx, nc, "d_b2")
        d_out = _Sem(ctx, nc, "d_out")
        d_out2 = _Sem(ctx, nc, "d_out2")
        pe_a = _Sem(ctx, nc, "pe_a")
        pe_b = _Sem(ctx, nc, "pe_b")
        pe_c = _Sem(ctx, nc, "pe_c")
        ve_a = _Sem(ctx, nc, "ve_a")
        ve_b = _Sem(ctx, nc, "ve_b")
        ve_c = _Sem(ctx, nc, "ve_c")
        ac_a = _Sem(ctx, nc, "ac_a")
        ac_b = _Sem(ctx, nc, "ac_b")
        ac_c = _Sem(ctx, nc, "ac_c")
        po_a = _Sem(ctx, nc, "po_a")
        ve_s = _Sem(ctx, nc, "ve_s")
        po_s = _Sem(ctx, nc, "po_s")
        SEMS = [d_const, d_sm, d_b2, d_out, d_out2, pe_a, pe_b, pe_c, ve_a, ve_b,
                ve_c, ac_a, ac_b, ac_c, po_a, ve_s, po_s] + d_x + d_wr + d_w1 + d_w3 + d_w2 + d_be

        xnTo = ctx.enter_context(nc.sbuf_tensor("xnTo", [128, HT * TOK], BF16))
        big0 = ctx.enter_context(nc.sbuf_tensor("big0", [128, HT * TOK], BF16))
        sumT = ctx.enter_context(nc.sbuf_tensor("sumT", [128, HT * TOK], BF16))
        identb = ctx.enter_context(nc.sbuf_tensor("identb_s", [128, 128], BF16))
        identf = ctx.enter_context(nc.sbuf_tensor("identf_s", [128, 128], F32))
        ones1 = ctx.enter_context(nc.sbuf_tensor("ones1_s", [1, 128], BF16))
        e8s = ctx.enter_context(nc.sbuf_tensor("e8_s", [8, ER], BF16))
        bots = ctx.enter_context(nc.sbuf_tensor("bot_s", [128, 16], F32))

        def emit_final():
            fin_ctx = ExitStack()
            xr = [fin_ctx.enter_context(nc.sbuf_tensor(f"xr{i}", [128, D], F32)) for i in range(2)]
            osb = [fin_ctx.enter_context(nc.sbuf_tensor(f"osb{i}", [128, D], F32)) for i in range(2)]
            pfin = [fin_ctx.enter_context(nc.psum_tensor(f"pfin{i}", [128, D], BF16))
                    for i in range(2)]
            fp, fv = pe_a.n, ve_a.n
            d_st = [d_out, d_out2]
            xr_wait = []
            st_wait = []
            fin_gate = [(sm, sm.n) for sm in SEMS]
            with nc.Block("final") as blk:
                @blk.sync
                def _(s):
                    for sm, n in fin_gate:
                        _w(s, sm, n)
                    for tt in range(NTT + 1):
                        if tt < NTT:
                            sl = tt % 2
                            _w(s, ve_a, fv + (tt - 2) + 1)
                            s.dma_start(xr[sl][:, :],
                                        xkv.ap()[512 + tt * 128:512 + (tt + 1) * 128, :])\
                                .then_inc(d_x[sl].h, 16)
                            d_x[sl].n += 16
                            xr_wait.append(d_x[sl].n)
                        if tt >= 1:
                            osl = (tt - 1) % 2
                            _w(s, ve_a, fv + tt)
                            s.dma_start(outd.ap()[(tt - 1) * 128:tt * 128, :],
                                        osb[osl][:, :]).then_inc(d_st[osl].h, 16)
                            d_st[osl].n += 16
                            st_wait.append(d_st[osl].n)
                    s.wait_ge(d_out.h, d_out.n)
                    s.wait_ge(d_out2.h, d_out2.n)

                @blk.tensor
                def _(te):
                    for sm, n in fin_gate:
                        if sm not in (d_out,):
                            _w(te, sm, n)
                    for tt in range(NTT):
                        _w(te, ve_a, fv + (tt - 2) + 1)
                        for hh in range(HT):
                            ins = te.transpose(pfin[tt % 2][:, hh * 128:(hh + 1) * 128],
                                               sumT[:, hh * TOK + tt * 128:hh * TOK + (tt + 1) * 128],
                                               identb[:, :])
                        _inc(ins, pe_a)

                @blk.vector
                def _(v):
                    for tt in range(NTT):
                        sl = tt % 2
                        _w(v, pe_a, fp + tt + 1)
                        _w(v, d_x[sl], xr_wait[tt])
                        if tt >= 2:
                            _w(v, d_st[sl], st_wait[tt - 2])
                        _inc(v.tensor_add(osb[sl][:, :], pfin[sl][:, :], xr[sl][:, :]), ve_a)

            fin_ctx.close()


        def xnTo_t(h):
            return xnTo[:, h * TOK:(h + 1) * TOK]

        def xnTo_s(h, tc):
            return xnTo[:, h * TOK + tc * 512:h * TOK + (tc + 1) * 512]

        def x2v(k):
            return big0[:, k * SUBS * TOK:(k + 1) * SUBS * TOK]

        w2slot_ap = [big0[:, 8192:12288], big0[:, 12288:16384], None]

        attn_ctx = ExitStack()
        KTs = attn_ctx.enter_context(nc.sbuf_tensor("KTs", [128, NG * KV], BF16))
        Vs = attn_ctx.enter_context(nc.sbuf_tensor("Vs", [128, NKT * 512], BF16))

        lnqkv_ctx = ExitStack()
        xnTh = lnqkv_ctx.enter_context(nc.sbuf_tensor("xnTh", [128, HT * 640], BF16))

        def xnTh_t(h):
            return xnTh[:, h * 640:(h + 1) * 640]

        def xn_kv(h, kt):
            if kt < 4:
                return xnTh_t(h)[:, kt * 128:(kt + 1) * 128]
            if kt < 12:
                return xnTo_t(h)[:, (kt - 4) * 128:(kt - 3) * 128]
            return xnTh_t(h)[:, 512:640]

        with nc.Block("consts") as blk:
            @blk.sync
            def _(s):
                for dst, src in ((identb, identbd), (identf, identfd), (ones1, ones1d),
                                 (e8s, e8d), (bots, botd)):
                    s.dma_start(dst[:, :], src.ap()).then_inc(d_const.h, 16)
                    d_const.n += 16
        CONST_N = d_const.n

        ln_ctx = ExitStack()
        xin = [ln_ctx.enter_context(nc.sbuf_tensor(f"xin{i}", [128, D], F32)) for i in range(2)]
        sqs = ln_ctx.enter_context(nc.sbuf_tensor("sqs", [128, D], F32))
        sms = ln_ctx.enter_context(nc.sbuf_tensor("sms", [128, D], F32))
        xnt = [ln_ctx.enter_context(nc.sbuf_tensor(f"xnt{i}", [128, D], BF16)) for i in range(2)]
        lst = [ln_ctx.enter_context(nc.sbuf_tensor(f"lst{i}", [128, 8], F32)) for i in range(2)]
        pt_ln = [ln_ctx.enter_context(nc.psum_tensor(f"pt_ln{i}", [128, 2048], BF16))
                 for i in range(2)]

        lnv, lna, lnp, lne = ve_a.n, ac_a.n, pe_a.n, ve_b.n

        def _ln_evac(v, t):
            _w(v, pe_a, lnp + t + 1)
            if t < 4:
                dst = xnTh[:, :].rearrange("p (h c) -> p h c", c=640)[:, :, t * 128:(t + 1) * 128]
            elif t < 12:
                dst = xnTo[:, :].rearrange("p (h c) -> p h c", c=TOK)[:, :, (t - 4) * 128:(t - 3) * 128]
            else:
                dst = xnTh[:, :].rearrange("p (h c) -> p h c", c=640)[:, :, 512:640]
            src = pt_ln[t % 2][:, :].rearrange("p (h c) -> p h c", c=128)
            _inc(v.tensor_copy(dst, src), ve_b)

        with nc.Block("ln") as blk:
            @blk.sync
            def _(s):
                for t in range(NKT):
                    sl = t % 2
                    _w(s, ac_a, lna + (t - 2) * 2 + 2)
                    s.dma_start(xin[sl][:, :], xkv.ap()[t * 128:(t + 1) * 128, :])\
                        .then_inc(d_x[sl].h, 16)
                    d_x[sl].n += 16

            @blk.vector
            def _(v):
                for t in range(NKT):
                    sl = t % 2
                    st = lst[sl]
                    _w(v, d_x[sl], 16 * (t // 2 + 1))
                    _inc(v.scalar_tensor_tensor(sqs[:, :], xin[sl][:, :], 1.0, xin[sl][:, :],
                                                ALU.mult, ALU.mult, accum_out=st[:, 1:2]), ve_s)
                    _inc(v.tensor_scalar(sms[:, :], xin[sl][:, :], 1.0, 0.0, ALU.mult,
                                         ALU.add, accum_out=st[:, 0:1]), ve_s)
                    _w(v, ve_s, ve_s.n)
                    v.tensor_scalar(st[:, 2:3], st[:, 0:1], 1.0 / D, None, ALU.mult)
                    _w(v, ve_s, _inc(v.tensor_scalar(st[:, 3:4], st[:, 1:2], 1.0 / D,
                                                     EPS, ALU.mult, ALU.add), ve_s))
                    _w(v, ve_s, _inc(v.scalar_tensor_tensor(st[:, 4:5], st[:, 2:3], 1.0,
                                                            st[:, 2:3], ALU.mult,
                                                            ALU.mult), ve_s))
                    _inc(v.scalar_tensor_tensor(st[:, 5:6], st[:, 4:5], -1.0, st[:, 3:4],
                                                ALU.mult, ALU.add), ve_a)
                    _w(v, ac_a, lna + t * 2 + 1)
                    _w(v, ve_s, _inc(v.reciprocal(st[:, 6:7], st[:, 6:7]), ve_s))
                    _inc(v.scalar_tensor_tensor(st[:, 7:8], st[:, 2:3], -1.0, st[:, 6:7],
                                                ALU.mult, ALU.mult), ve_a)
                    if t >= 1:
                        _ln_evac(v, t - 1)
                _ln_evac(v, NKT - 1)

            @blk.scalar
            def _(sc):
                for t in range(NKT):
                    sl = t % 2
                    st = lst[sl]
                    _w(sc, ve_a, lnv + t * 2 + 1)
                    _inc(sc.activation(st[:, 6:7], st[:, 5:6], AF.Sqrt, bias=0.0, scale=1.0),
                         ac_a)
                    _w(sc, ve_a, lnv + t * 2 + 2)
                    _inc(sc.activation(xnt[sl][:, :], xin[sl][:, :], AF.Identity,
                                       bias=st[:, 7:8], scale=st[:, 6:7]), ac_a)

            @blk.tensor
            def _(te):
                for t in range(NKT):
                    sl = t % 2
                    if t == 0:
                        _w(te, d_const, CONST_N)
                    _w(te, ac_a, lna + t * 2 + 2)
                    _w(te, ve_b, lne + (t - 2) + 1)
                    for hh in range(HT):
                        ins = te.transpose(pt_ln[sl][:, hh * 128:(hh + 1) * 128],
                                           xnt[sl][:, hh * 128:(hh + 1) * 128], identb[:, :])
                    _inc(ins, pe_a)

        LN_END = [(pe_a, pe_a.n), (ve_a, ve_a.n), (ve_b, ve_b.n), (ac_a, ac_a.n)]
        ln_ctx.close()
        if upto <= 1:
            qkv_ctx = None
            lnqkv_ctx.close(); attn_ctx.close()
            emit_final()
            nc._dbg = dict(out=outd, xnTo=xnTo, big0=big0, sumT=sumT, xkv=xkv)
            return nc

        qkv_ctx = ExitStack()
        wr = [qkv_ctx.enter_context(nc.sbuf_tensor(f"wr{i}", [128, HT * 256], BF16))
              for i in range(3)]
        cqs = qkv_ctx.enter_context(nc.sbuf_tensor("cqs", [128, NTT * 64], F32))
        sqss = qkv_ctx.enter_context(nc.sbuf_tensor("sqss", [128, NTT * 64], F32))
        cks = qkv_ctx.enter_context(nc.sbuf_tensor("cks", [128, NKT * 64], F32))
        sks = qkv_ctx.enter_context(nc.sbuf_tensor("sks", [128, NKT * 64], F32))
        bqbs = qkv_ctx.enter_context(nc.sbuf_tensor("bqbs", [128, D], BF16))
        bkbs = qkv_ctx.enter_context(nc.sbuf_tensor("bkbs", [128, 512], BF16))
        bvbs = qkv_ctx.enter_context(nc.sbuf_tensor("bvbs", [128, 512], BF16))
        qscr = qkv_ctx.enter_context(nc.sbuf_tensor("qscr", [128, 256], F32))
        rop = qkv_ctx.enter_context(nc.sbuf_tensor("rop", [128, 512], F32))
        qrot = [qkv_ctx.enter_context(nc.sbuf_tensor(f"qrot{i}", [128, 256], BF16))
                for i in range(2)]
        pq = [qkv_ctx.enter_context(nc.psum_tensor(f"pq{i}", [128, 256], F32))
              for i in range(4)]
        ptq = [qkv_ctx.enter_context(nc.psum_tensor(f"ptq{i}", [128, 256], BF16))
               for i in range(2)]

        qkv_items = []
        for c in range(10):
            if c < 8:
                for tt in range(NTT):
                    qkv_items.append(("q", c, tt))
            else:
                for kt in range(NKT):
                    qkv_items.append(("k", c - 8, kt))
        for c in range(2):
            for kt in range(NKT):
                qkv_items.append(("v", c, kt))
        NQKV = len(qkv_items)
        thru = []
        acc = 0
        for c in range(12):
            acc += NTT if c < 8 else NKT
            thru.append(acc)

        qp, qv, qv2, qv3, qp2 = pe_a.n, ve_a.n, ve_b.n, ve_c.n, pe_b.n
        qd = d_const.n
        with nc.Block("qkv") as blk:
            wr_wait = []
            rope_ves = {}

            @blk.gpsimd
            def _(g):
                for c in range(12):
                    sl = c % 3
                    if c == 0:
                        for sm, n in LN_END:
                            _w(g, sm, n)
                    if c >= 3:
                        _w(g, pe_a, qp + thru[c - 3])
                    if c < 8:
                        src = wqd.ap().rearrange("(h p) q -> p h q", p=128)[:, :, c * 256:(c + 1) * 256]
                    elif c < 10:
                        src = wkd.ap().rearrange("(h p) q -> p h q", p=128)[:, :, (c - 8) * 256:(c - 7) * 256]
                    else:
                        src = wvd.ap().rearrange("(h p) q -> p h q", p=128)[:, :, (c - 10) * 256:(c - 9) * 256]
                    g.dma_start(wr[sl][:, :].rearrange("p (h q) -> p h q", q=256), src)\
                        .then_inc(d_wr[sl].h, 16)
                    d_wr[sl].n += 16
                    wr_wait.append(d_wr[sl].n)

            @blk.sync
            def _(s):
                for sm, n in LN_END:
                    _w(s, sm, n)
                for dst, src in ((cqs, cqtd), (sqss, sqtd), (cks, cktd), (sks, sktd),
                                 (bqbs, bqbd), (bkbs, bkbd), (bvbs, bvbd)):
                    s.dma_start(dst[:, :], src.ap()).then_inc(d_const.h, 16)
                    d_const.n += 16

            def _qkv_tr(te, j):
                if qkv_items[j][0] == "v":
                    return
                _w(te, ve_c, qv3 + j + 1)
                _w(te, ve_b, qv2 + (j - 2) + 1)
                for u in range(2):
                    ins = te.transpose(ptq[j % 2][:, u * 128:(u + 1) * 128],
                                       qrot[j % 2][:, u * 128:(u + 1) * 128],
                                       identb[:, :])
                _inc(ins, pe_b)

            @blk.tensor
            def _(te):
                for i in range(NQKV + 2):
                    if i < NQKV:
                        kind, c, sub = qkv_items[i]
                        cg = {"q": c, "k": 8 + c, "v": 10 + c}[kind]
                        sl = cg % 3
                        if i == 0:
                            _w(te, ve_b, lne + NKT)
                        if sub == 0:
                            _w(te, d_wr[sl], wr_wait[cg])
                        _w(te, ve_a, qv + (i - 4) + 1)
                        ps = pq[i % 4]
                        for hh in range(HT):
                            lhsT = (xnTo_t(hh)[:, sub * 128:(sub + 1) * 128] if kind == "q"
                                    else xn_kv(hh, sub))
                            ins = te.matmul(ps[:, :], lhsT, wr[sl][:, hh * 256:(hh + 1) * 256],
                                            start=(hh == 0), stop=(hh == HT - 1))
                        _inc(ins, pe_a)
                    if i >= 2:
                        _qkv_tr(te, i - 2)

            @blk.vector
            def _(v):
                for i, (kind, c, sub) in enumerate(qkv_items):
                    if i == 0:
                        _w(v, d_const, qd + 7 * 16)
                    ps = pq[i % 4]
                    _w(v, pe_a, qp + i + 1)
                    if kind == "v":
                        _inc(v.tensor_tensor(
                            Vs[:, sub * 512 + c * 256:sub * 512 + (c + 1) * 256],
                            ps[:, :], bvbs[:, c * 256:(c + 1) * 256], ALU.add), ve_a)
                        continue
                    bias = (bqbs if kind == "q" else bkbs)[:, c * 256:(c + 1) * 256]
                    if i >= 1:
                        _w(v, ve_s, rope_ves[i - 1])
                        _w(v, ve_c, qv3 + i)
                    _inc(v.tensor_tensor(qscr[:, :], ps[:, :], bias, ALU.add), ve_a)
                    _w(v, ve_a, qv + i + 1)
                    if kind == "q":
                        ctab = cqs[:, sub * 64:(sub + 1) * 64]
                        stab = sqss[:, sub * 64:(sub + 1) * 64]
                    else:
                        ctab = cks[:, sub * 64:(sub + 1) * 64]
                        stab = sks[:, sub * 64:(sub + 1) * 64]
                    _w(v, pe_b, qp2 + (i - 2) + 1)
                    qr = qrot[i % 2]
                    for hd in range(2):
                        o = hd * 128
                        r = hd * 256
                        q1 = qscr[:, o:o + 64]
                        q2 = qscr[:, o + 64:o + 128]
                        v.tensor_tensor(rop[:, r:r + 64], q1, ctab, ALU.mult)
                        v.tensor_tensor(rop[:, r + 64:r + 128], q2, stab, ALU.mult)
                        v.tensor_tensor(rop[:, r + 128:r + 192], q1, stab, ALU.mult)
                        ins = v.tensor_tensor(rop[:, r + 192:r + 256], q2, ctab, ALU.mult)
                    rope_ves[i] = _inc(ins, ve_s)
                    _w(v, ve_s, rope_ves[i])
                    for hd in range(2):
                        o = hd * 128
                        r = hd * 256
                        v.tensor_sub(qr[:, o:o + 64], rop[:, r:r + 64], rop[:, r + 64:r + 128])
                        ins = v.tensor_add(qr[:, o + 64:o + 128], rop[:, r + 128:r + 192],
                                           rop[:, r + 192:r + 256])
                    _inc(ins, ve_c)
                    _w(v, pe_b, qp2 + i + 1)
                    src3 = ptq[i % 2][:, :].rearrange("p (u c2) -> p u c2", c2=128)
                    if kind == "q":
                        dst3 = big0[:, :].rearrange("p (hh t) -> p hh t", t=TOK)[
                            :, 2 * c:2 * c + 2, sub * 128:(sub + 1) * 128]
                    else:
                        dst3 = KTs[:, :].rearrange("p (gg t) -> p gg t", t=KV)[
                            :, 2 * c:2 * c + 2, sub * 128:(sub + 1) * 128]
                    _inc(v.tensor_copy(dst3, src3), ve_b)

        qkv_ctx.close()
        lnqkv_ctx.close()
        if upto <= 2:
            attn_ctx.close()
            emit_final()
            nc._dbg = dict(out=outd, xnTo=xnTo, big0=big0, sumT=sumT, xkv=xkv)
            return nc

        sc_ctx = ExitStack()
        attnT = sc_ctx.enter_context(nc.sbuf_tensor("attnT", [128, HT * TOK], BF16))
        sm_ctx = ExitStack()
        masks = sm_ctx.enter_context(nc.sbuf_tensor("masks", [128, NTT * SLOTS * 128], BF16))
        ssb = [sm_ctx.enter_context(nc.sbuf_tensor(f"ssb{i}", [128, SLOTS * 128], F32))
               for i in range(2)]
        pnb = [sm_ctx.enter_context(nc.sbuf_tensor(f"pnb{i}", [128, SLOTS * 128], BF16))
               for i in range(4)]
        pnT = [sm_ctx.enter_context(nc.sbuf_tensor(f"pnT{i}", [128, SLOTS * 128], BF16))
               for i in range(4)]
        nmz = sm_ctx.enter_context(nc.sbuf_tensor("nmz", [128, 12], F32))
        psc = [sm_ctx.enter_context(nc.psum_tensor(f"psc{i}", [128, SLOTS * 128], F32))
               for i in range(2)]
        ppn = [sm_ctx.enter_context(nc.psum_tensor(f"ppn{i}", [128, SLOTS * 128], BF16))
               for i in range(2)]
        pwv = [sm_ctx.enter_context(nc.psum_tensor(f"pwv{i}", [128, 512], F32))
               for i in range(2)]

        sp1, sp2, sp3 = pe_a.n, pe_b.n, pe_c.n
        sv1, sv2, sv3 = ve_a.n, ve_b.n, ve_c.n
        sa1, sa2, sa3 = ac_a.n, ac_b.n, ac_c.n
        sxd = d_x[0].n

        def slot_kt(qb, j):
            return qb + j if j < 5 else 12

        with nc.Block("scores") as blk:
            @blk.sync
            def _(s):
                _w(s, pe_a, sp1)
                _w(s, pe_b, sp2)
                _w(s, ve_b, sv2)
                s.dma_start(masks[:, :], maskd.ap()).then_inc(d_x[0].h, 16)
                d_x[0].n += 16

            def _sc_mm(te, i):
                hh, qb = divmod(i, NTT)
                g = hh // 4
                _w(te, ve_a, sv1 + (i - 2) + 1)
                qtile = big0[:, hh * TOK + qb * 128:hh * TOK + (qb + 1) * 128]
                base = g * KV
                te.matmul(psc[i % 2][:, 0:512], qtile,
                          KTs[:, base + qb * 128:base + qb * 128 + 512],
                          start=True, stop=True)
                te.matmul(psc[i % 2][:, 512:640], qtile,
                          KTs[:, base + (qb + 4) * 128:base + (qb + 5) * 128],
                          start=True, stop=True)
                ins = te.matmul(psc[i % 2][:, 640:768], qtile,
                                KTs[:, base + 12 * 128:base + 13 * 128],
                                start=True, stop=True)
                _inc(ins, pe_a)

            def _sc_tr(te, j):
                _w(te, ac_b, sa2 + j + 1)
                _w(te, ve_c, sv3 + (j - 2) + 1)
                for u in range(SLOTS):
                    ins = te.transpose(ppn[j % 2][:, u * 128:(u + 1) * 128],
                                       pnb[j % 4][:, u * 128:(u + 1) * 128],
                                       identb[:, :])
                _inc(ins, pe_b)

            def _sc_wv(te, w):
                hh, qb = divmod(w, NTT)
                g = hh // 4
                q4 = w // 4
                _w(te, ve_c, sv3 + w + 1)
                if w % 4 == 0:
                    _w(te, ac_c, sa3 + (q4 - 2) + 1)
                for j in range(SLOTS):
                    kt = slot_kt(qb, j)
                    ins = te.matmul(
                        pwv[q4 % 2][:, (w % 4) * 128:(w % 4 + 1) * 128],
                        Vs[:, kt * 512 + g * 128:kt * 512 + (g + 1) * 128],
                        pnT[w % 4][:, j * 128:(j + 1) * 128],
                        start=(j == 0), stop=(j == SLOTS - 1))
                _inc(ins, pe_c)

            @blk.tensor
            def _(te):
                _w(te, ve_a, sv1)
                _w(te, ve_b, sv2)
                for i in range(NH * NTT + 6):
                    if i < NH * NTT:
                        _sc_mm(te, i)
                    if i >= 4 and i - 4 < NH * NTT:
                        _sc_tr(te, i - 4)
                    if i >= 6:
                        _sc_wv(te, i - 6)

            @blk.vector
            def _(v):
                def _norm(j):
                    jl = j % 2
                    _w(v, ac_b, sa2 + (j - 2) + 1)
                    _w(v, ve_b, _inc(v.reciprocal(nmz[:, 4 + jl:5 + jl],
                                                  nmz[:, 2 + jl:3 + jl]), ve_b))
                    _w(v, pe_b, sp2 + (j - 4) + 1)
                    _inc(v.tensor_scalar(pnb[j % 4][:, :], pnb[j % 4][:, :],
                                         nmz[:, 4 + jl:5 + jl], None, ALU.mult), ac_b)

                for i in range(NH * NTT):
                    hh, qb = divmod(i, NTT)
                    sl = i % 2
                    if i == 0:
                        _w(v, d_x[0], sxd + 16)
                    _w(v, pe_a, sp1 + i + 1)
                    _w(v, ac_a, sa1 + (i - 2) + 1)
                    _w(v, ve_s, _inc(v.tensor_tensor(ssb[sl][:, :], psc[sl][:, :],
                                                     masks[:, qb * 768:(qb + 1) * 768],
                                                     ALU.add), ve_s))
                    _inc(v.reduce_max(nmz[:, sl:sl + 1], ssb[sl][:, :], axis=AX.X,
                                      negate=True), ve_a)
                    if i >= 2:
                        _norm(i - 2)
                _w(v, ac_a, sa1 + NH * NTT)
                _norm(NH * NTT - 2)
                _norm(NH * NTT - 1)

            @blk.scalar
            def _(sc):
                def _evac(g):
                    il = 4 * g + 3
                    ghh, gqb = divmod(il, NTT)
                    _w(sc, pe_c, sp3 + il + 1)
                    _inc(sc.activation(
                        attnT[:, ghh * TOK + (gqb - 3) * 128:ghh * TOK + (gqb + 1) * 128],
                        pwv[g % 2][:, :], AF.Identity, bias=0.0, scale=1.0), ac_c)

                for i in range(NH * NTT):
                    hh, qb = divmod(i, NTT)
                    sl = i % 2
                    _w(sc, ve_a, sv1 + i + 1)
                    _w(sc, ve_b, sv2 + (i - 2) + 1)
                    _w(sc, pe_b, sp2 + (i - 4) + 1)
                    _inc(sc.activation(pnb[i % 4][:, :], ssb[sl][:, :], AF.Exp,
                                       bias=nmz[:, sl:sl + 1], scale=1.0,
                                       accum_out=nmz[:, 2 + sl:3 + sl]), ac_a)
                    if i >= 2:
                        j = i - 2
                        _w(sc, pe_b, sp2 + j + 1)
                        _w(sc, pe_c, sp3 + (j - 4) + 1)
                        _inc(sc.activation(pnT[j % 4][:, :], ppn[j % 2][:, :],
                                           AF.Identity, bias=0.0, scale=1.0), ve_c)
                    if i % 4 == 3 and i >= 7:
                        _evac(i // 4 - 1)
                for j in (NH * NTT - 2, NH * NTT - 1):
                    _w(sc, pe_b, sp2 + j + 1)
                    _w(sc, pe_c, sp3 + (j - 4) + 1)
                    _inc(sc.activation(pnT[j % 4][:, :], ppn[j % 2][:, :],
                                       AF.Identity, bias=0.0, scale=1.0), ve_c)
                _evac(NH * NTT // 4 - 1)

        sm_ctx.close()
        if upto <= 3:
            sc_ctx.close(); attn_ctx.close()
            emit_final()
            nc._dbg = dict(out=outd, xnTo=xnTo, big0=big0, sumT=sumT, xkv=xkv, attnT=attnT)
            return nc

        wo_ctx = ExitStack()
        wow = [wo_ctx.enter_context(nc.sbuf_tensor(f"wow{i}", [128, HT * 256], BF16))
               for i in range(3)]
        pwo = [wo_ctx.enter_context(nc.psum_tensor(f"pwo{i}", [128, 512], F32))
               for i in range(4)]
        op, oa = pe_a.n, ac_a.n
        WO_GATE = [(pe_b, pe_b.n), (pe_c, pe_c.n), (ve_a, ve_a.n),
                   (ve_c, ve_c.n), (ac_b, ac_b.n), (ac_c, ac_c.n)]
        with nc.Block("wo") as blk:
            wo_wait = []

            @blk.gpsimd
            def _(g):
                for c in range(8):
                    sl = c % 3
                    if c == 0:
                        for sm, n in WO_GATE:
                            _w(g, sm, n)
                    if c >= 3:
                        _w(g, pe_a, op + (c - 3 + 1) * 4)
                    g.dma_start(
                        wow[sl][:, :].rearrange("p (h q) -> p h q", q=256),
                        wod.ap().rearrange("(h p) q -> p h q", p=128)[:, :, c * 256:(c + 1) * 256]
                    ).then_inc(d_wr[sl].h, 16)
                    d_wr[sl].n += 16
                    wo_wait.append(d_wr[sl].n)

            @blk.tensor
            def _(te):
                for m in range(32):
                    co, tc = divmod(m, 2)
                    c = co // 2
                    sl = c % 3
                    if m == 0:
                        for sm, n in WO_GATE:
                            _w(te, sm, n)
                    if m % 4 == 0:
                        _w(te, d_wr[sl], wo_wait[c])
                    _w(te, ac_a, oa + (m - 4) + 1)
                    for ci in range(HT):
                        ins = te.matmul(
                            pwo[m % 4][:, :],
                            wow[sl][:, ci * 256 + (co % 2) * 128:ci * 256 + (co % 2) * 128 + 128],
                            attnT[:, ci * TOK + tc * 512:ci * TOK + (tc + 1) * 512],
                            start=(ci == 0), stop=(ci == HT - 1))
                    _inc(ins, pe_a)

            @blk.scalar
            def _(sc):
                for m in range(32):
                    co, tc = divmod(m, 2)
                    _w(sc, pe_a, op + m + 1)
                    _inc(sc.activation(sumT[:, co * TOK + tc * 512:co * TOK + (tc + 1) * 512],
                                       pwo[m % 4][:, :], AF.Identity,
                                       bias=bots[:, co:co + 1], scale=1.0), ac_a)

        wo_ctx.close()
        sc_ctx.close()
        attn_ctx.close()
        if upto <= 4:
            emit_final()
            nc._dbg = dict(out=outd, xnTo=xnTo, big0=big0, sumT=sumT, xkv=xkv)
            return nc

        ffn_ctx = ExitStack()
        m1b = [ffn_ctx.enter_context(nc.sbuf_tensor(f"m1b{i}", [128, 2 * TOK], BF16))
               for i in range(2)]
        m3b = [ffn_ctx.enter_context(nc.sbuf_tensor(f"m3b{i}", [128, 2 * TOK], BF16))
               for i in range(2)]
        mex = [ffn_ctx.enter_context(nc.sbuf_tensor(f"mex{i}", [128, 2 * TOK], BF16))
               for i in range(2)]
        rwb = [ffn_ctx.enter_context(nc.sbuf_tensor(f"rwb{i}", [128, TOK], F32))
               for i in range(2)]
        a2acc = [ffn_ctx.enter_context(nc.sbuf_tensor(f"a2acc{i}", [128, 2 * TOK], BF16))
                 for i in range(2)]
        b2rs = ffn_ctx.enter_context(nc.sbuf_tensor("b2rs", [128, 2 * D], BF16))

        rt_ctx = ExitStack()
        wgts = rt_ctx.enter_context(nc.sbuf_tensor("wgts", [128, 128], BF16))
        a1ts = rt_ctx.enter_context(nc.sbuf_tensor("a1ts", [128, HT * ER], BF16))
        a3ts = rt_ctx.enter_context(nc.sbuf_tensor("a3ts", [128, HT * ER], BF16))
        lg = rt_ctx.enter_context(nc.sbuf_tensor("lg", [8, TOK], F32))
        lgt = rt_ctx.enter_context(nc.sbuf_tensor("lgt", [128, 64], F32))
        oh = [rt_ctx.enter_context(nc.sbuf_tensor(f"oh{i}", [128, 64], F32)) for i in range(2)]
        lg2 = rt_ctx.enter_context(nc.sbuf_tensor("lg2", [128, 8], F32))
        mcol = rt_ctx.enter_context(nc.sbuf_tensor("mcol", [128, 4], F32))
        dall = rt_ctx.enter_context(nc.sbuf_tensor("dall", [128, 8], F32))
        eall = rt_ctx.enter_context(nc.sbuf_tensor("eall", [128, 8], F32))
        rwa = [rt_ctx.enter_context(nc.sbuf_tensor(f"rwa{i}", [128, 8], F32)) for i in range(2)]
        ohT = [rt_ctx.enter_context(nc.sbuf_tensor(f"ohT{i}", [8, TOK], BF16)) for i in range(2)]
        rwT = [rt_ctx.enter_context(nc.sbuf_tensor(f"rwT{i}", [1, TOK], BF16)) for i in range(2)]
        pbig = [rt_ctx.enter_context(nc.psum_tensor(f"pbig{i}", [128, 512], F32))
                for i in range(4)]
        plog = rt_ctx.enter_context(nc.psum_tensor("plog", [8, TOK], F32))
        poh = rt_ctx.enter_context(nc.psum_tensor("poh", [8, TOK], F32))

        rp, rv, ra = pe_a.n, ve_a.n, ac_a.n
        rsm = d_sm.n
        with nc.Block("routing") as blk:
            @blk.gpsimd
            def _(g):
                _w(g, pe_a, rp)
                g.dma_start(wgts[:, :], wgtd.ap()).then_inc(d_sm.h, 16); d_sm.n += 16
                g.dma_start(a1ts[:, :].rearrange("p (h e) -> p h e", e=ER),
                            a1td.ap().rearrange("(h p) e -> p h e", p=128))\
                    .then_inc(d_sm.h, 16); d_sm.n += 16
                g.dma_start(a3ts[:, :].rearrange("p (h e) -> p h e", e=ER),
                            a3td.ap().rearrange("(h p) e -> p h e", p=128))\
                    .then_inc(d_sm.h, 16); d_sm.n += 16
                g.dma_start(b2rs[:, :].rearrange("p (q h) -> p q h", h=D),
                            b2rd.ap().rearrange("(q p) h -> p q h", p=128))\
                    .then_inc(d_b2.h, 16); d_b2.n += 16

            @blk.tensor
            def _(te):
                _w(te, d_sm, rsm + 48)
                _w(te, ac_a, ra)
                for tc in range(2):
                    for hh in range(HT):
                        ins = te.matmul(plog[:, tc * 512:(tc + 1) * 512],
                                        wgts[:, hh * 8:(hh + 1) * 8],
                                        xnTo_s(hh, tc), start=(hh == 0), stop=(hh == HT - 1))
                _inc(ins, pe_a)
                _w(te, ve_a, rv + 1)
                for tt in range(NTT):
                    ins = te.transpose(pbig[3][:, tt * 8:(tt + 1) * 8],
                                       lg[:, tt * 128:(tt + 1) * 128], identf[0:8, 0:8])
                _inc(ins, pe_a)
                _w(te, ve_a, rv + 10)
                for tt in range(NTT):
                    ins = te.transpose(poh[:, tt * 128:(tt + 1) * 128],
                                       oh[0][:, tt * 8:(tt + 1) * 8], identf[:, :])
                _inc(ins, pe_a)
                _w(te, ve_a, rv + 11)
                for tt in range(NTT):
                    pb = pbig[2 + tt // 4]
                    ins = te.transpose(pb[0:1, (tt % 4) * 128:(tt % 4 + 1) * 128],
                                       rwa[0][:, tt:tt + 1], identf[:, :])
                _inc(ins, pe_a)
                _w(te, ve_a, rv + 12)
                for tt in range(NTT):
                    ins = te.transpose(poh[:, tt * 128:(tt + 1) * 128],
                                       oh[1][:, tt * 8:(tt + 1) * 8], identf[:, :])
                _inc(ins, pe_a)
                _w(te, ve_a, rv + 13)
                for tt in range(NTT):
                    pb = pbig[2 + tt // 4]
                    ins = te.transpose(pb[0:1, (tt % 4) * 128:(tt % 4 + 1) * 128],
                                       rwa[1][:, tt:tt + 1], identf[:, :])
                _inc(ins, pe_a)
                for k in range(2):
                    _w(te, ve_a, rv + 15 + k)
                    for q in range(2):
                        for tc in range(2):
                            ins = te.matmul(pbig[q * 2 + tc][:, :],
                                            e8s[0:8, q * 128:(q + 1) * 128],
                                            ohT[k][0:8, tc * 512:(tc + 1) * 512],
                                            start=True, stop=True)
                    _inc(ins, pe_a)
                for k in range(2):
                    _w(te, ve_a, rv + 17)
                    for tc in range(2):
                        ins = te.matmul(pbig[k * 2 + tc][:, :], ones1[:, :],
                                        rwT[k][0:1, tc * 512:(tc + 1) * 512],
                                        start=True, stop=True)
                    _inc(ins, pe_a)
                _w(te, d_sm, rsm + 48)
                for si, src in enumerate((a1ts, a3ts)):
                    _w(te, ve_a, rv + 19 + si)
                    for q in range(2):
                        for tc in range(2):
                            for hh in range(HT):
                                ins = te.matmul(pbig[q * 2 + tc][:, :],
                                                src[:, hh * ER + q * 128:hh * ER + q * 128 + 128],
                                                xnTo_s(hh, tc),
                                                start=(hh == 0), stop=(hh == HT - 1))
                    _inc(ins, pe_a)

            @blk.vector
            def _(v):
                _w(v, pe_a, rp + 1)
                _inc(v.tensor_copy(lg[:, :], plog[:, :]), ve_a)
                _w(v, pe_a, rp + 2)
                _inc(v.tensor_copy(lgt[:, :], pbig[3][:, 0:64]), ve_a)
                _w(v, ve_a, rv + 2)
                for tt in range(NTT):
                    sl_ = lgt[:, tt * 8:(tt + 1) * 8]
                    if tt >= 1:
                        _w(v, ve_a, rv + 2 + tt)
                    _w(v, ve_s, _inc(v.reduce_max(mcol[:, 0:1], sl_, axis=AX.X), ve_s))
                    _w(v, ve_s, _inc(v.tensor_scalar(oh[0][:, tt * 8:(tt + 1) * 8], sl_,
                                                     mcol[:, 0:1], None, ALU.is_ge), ve_s))
                    _w(v, ve_s, _inc(v.scalar_tensor_tensor(lg2[:, :], oh[0][:, tt * 8:(tt + 1) * 8],
                                                            -1.0e30, sl_, ALU.mult, ALU.add), ve_s))
                    _w(v, ve_s, _inc(v.reduce_max(mcol[:, 1:2], lg2[:, :], axis=AX.X), ve_s))
                    v.tensor_scalar(oh[1][:, tt * 8:(tt + 1) * 8], lg2[:, :], mcol[:, 1:2],
                                    None, ALU.is_ge)
                    _inc(v.tensor_sub(dall[:, tt:tt + 1], mcol[:, 1:2], mcol[:, 0:1]),
                         ve_a)
                _w(v, ac_a, ra + 1)
                _w(v, ve_s, _inc(v.tensor_scalar(eall[:, :], eall[:, :], 1.0, None,
                                                 ALU.add), ve_s))
                _w(v, ve_s, _inc(v.reciprocal(rwa[0][:, :], eall[:, :]), ve_s))
                _inc(v.tensor_scalar(rwa[1][:, :], rwa[0][:, :], -1.0, 1.0,
                                     ALU.mult, ALU.add), ve_a)
                _w(v, pe_a, rp + 3)
                _inc(v.tensor_copy(ohT[0][0:8, :], poh[0:8, :]), ve_a)
                _w(v, pe_a, rp + 4)
                v.tensor_copy(rwT[0][0:1, 0:512], pbig[2][0:1, :])
                _inc(v.tensor_copy(rwT[0][0:1, 512:1024], pbig[3][0:1, :]), ve_a)
                _w(v, pe_a, rp + 5)
                _inc(v.tensor_copy(ohT[1][0:8, :], poh[0:8, :]), ve_a)
                _w(v, pe_a, rp + 6)
                v.tensor_copy(rwT[1][0:1, 0:512], pbig[2][0:1, :])
                _inc(v.tensor_copy(rwT[1][0:1, 512:1024], pbig[3][0:1, :]), ve_a)
                for k in range(2):
                    _w(v, pe_a, rp + 7 + k)
                    for q in range(2):
                        for tc in range(2):
                            ins = v.tensor_copy(
                                mex[k][:, q * TOK + tc * 512:q * TOK + (tc + 1) * 512],
                                pbig[q * 2 + tc][:, :])
                    _inc(ins, ve_a)
                for k in range(2):
                    _w(v, pe_a, rp + 9 + k)
                    for tc in range(2):
                        ins = v.tensor_copy(rwb[k][:, tc * 512:(tc + 1) * 512],
                                            pbig[k * 2 + tc][:, :])
                    _inc(ins, ve_a)
                _w(v, ve_a, rv + 17)
                for si, mb in enumerate((m1b, m3b)):
                    _w(v, pe_a, rp + 11 + si)
                    for k in range(2):
                        for q in range(2):
                            for tc in range(2):
                                sl_ = slice(q * TOK + tc * 512, q * TOK + (tc + 1) * 512)
                                ins = v.tensor_tensor(mb[k][:, sl_], pbig[q * 2 + tc][:, :],
                                                      mex[k][:, sl_], ALU.mult)
                    _inc(ins, ve_a)
                v.memset(a2acc[0][:, :], 0.0)
                v.memset(a2acc[1][:, :], 0.0)

            @blk.scalar
            def _(sc):
                _w(sc, ve_a, rv + 10)
                _inc(sc.activation(eall[:, :], dall[:, :], AF.Exp, bias=0.0, scale=1.0),
                     ac_a)

        B2RS_N = d_b2.n
        rt_ctx.close()
        if upto <= 5:
            ffn_ctx.close()
            emit_final()
            nc._dbg = dict(out=outd, xnTo=xnTo, big0=big0, sumT=sumT, xkv=xkv,
                           m1b=m1b, m3b=m3b, mex=mex, rwb=rwb)
            return nc

        f_ctx = ExitStack()
        w1b = [f_ctx.enter_context(nc.sbuf_tensor(f"w1b{i}", [128, HT * 128], BF16))
               for i in range(2)]
        w3b = [f_ctx.enter_context(nc.sbuf_tensor(f"w3b{i}", [128, HT * 128], BF16))
               for i in range(2)]
        b1re = [f_ctx.enter_context(nc.sbuf_tensor(f"b1re{i}", [128, 2 * 512], BF16))
                for i in range(2)]
        b3re = [f_ctx.enter_context(nc.sbuf_tensor(f"b3re{i}", [128, 2 * 512], BF16))
                for i in range(2)]
        a2te = [f_ctx.enter_context(nc.sbuf_tensor(f"a2te{i}", [128, SUBS * ER], BF16))
                for i in range(2)]
        w2x = f_ctx.enter_context(nc.sbuf_tensor("w2x", [128, 2 * D], BF16))
        w2slot_ap[2] = w2x[:, :]
        b1sb = [f_ctx.enter_context(nc.sbuf_tensor(f"b1sb{i}", [128, 512], F32))
                for i in range(2)]
        b3sb = [f_ctx.enter_context(nc.sbuf_tensor(f"b3sb{i}", [128, 512], F32))
                for i in range(2)]
        x1sb = [f_ctx.enter_context(nc.sbuf_tensor(f"x1sb{i}", [128, 512], F32))
                for i in range(2)]
        s1sb = [f_ctx.enter_context(nc.sbuf_tensor(f"s1sb{i}", [128, 512], F32))
                for i in range(2)]
        x3sb = [f_ctx.enter_context(nc.sbuf_tensor(f"x3sb{i}", [128, 512], F32))
                for i in range(2)]
        pfa = [f_ctx.enter_context(nc.psum_tensor(f"pfa{i}", [128, 512], F32))
               for i in range(8)]

        be_wait = {}
        w2_wait = {}

        w1_wait = {}
        w3_wait = {}

        def _ffn_w13(g, c):
            sl = c % 2
            g.dma_start(w1b[sl][:, :].rearrange("p (h f) -> p h f", f=128),
                        w1td.ap().rearrange("(h p) f -> p h f", p=128)[:, :, c * 128:(c + 1) * 128])\
                .then_inc(d_w1[sl].h, 16)
            d_w1[sl].n += 16
            w1_wait[c] = d_w1[sl].n
            g.dma_start(w3b[sl][:, :].rearrange("p (h f) -> p h f", f=128),
                        w3td.ap().rearrange("(h p) f -> p h f", p=128)[:, :, c * 128:(c + 1) * 128])\
                .then_inc(d_w3[sl].h, 16)
            d_w3[sl].n += 16
            w3_wait[c] = d_w3[sl].n

        def _ffn_bundle(g, e):
            sl = e % 2
            g.dma_start(b1re[sl][:, :].rearrange("p (q f) -> p q f", f=512),
                        b1rd.ap().rearrange("(q p) f -> p q f", p=128)[:, :, e * 512:(e + 1) * 512])\
                .then_inc(d_be[sl].h, 16)
            d_be[sl].n += 16
            g.dma_start(b3re[sl][:, :].rearrange("p (q f) -> p q f", f=512),
                        b3rd.ap().rearrange("(q p) f -> p q f", p=128)[:, :, e * 512:(e + 1) * 512])\
                .then_inc(d_be[sl].h, 16)
            d_be[sl].n += 16
            g.dma_start(a2te[sl][:, :].rearrange("p (j e2) -> p j e2", e2=ER),
                        a2td.ap().rearrange("(f p) e2 -> p f e2", p=128)[:, e * SUBS:(e + 1) * SUBS, :])\
                .then_inc(d_be[sl].h, 16)
            d_be[sl].n += 16
            be_wait[e] = d_be[sl].n

        def _ffn_w2(g, he):
            sl = he % 3
            g.dma_start(w2slot_ap[sl].rearrange("p (j h) -> p j h", h=D),
                        w2td.ap().rearrange("(f p) h -> p f h", p=128)[:, he * 2:(he + 1) * 2, :])\
                .then_inc(d_w2[sl].h, 16)
            d_w2[sl].n += 16
            w2_wait[he] = d_w2[sl].n


        NFC_RUN = NFC if upto > 7 else 1
        EMIT_FD = upto != 6

        FB, FD = [], []
        _cur = dict(pe_a=pe_a.n, pe_b=pe_b.n, ve_a=ve_a.n, ve_b=ve_b.n,
                    ve_c=ve_c.n, ac_a=ac_a.n, ac_b=ac_b.n, po_a=po_a.n)
        for e in range(NFC_RUN):
            FB.append(dict(_cur))
            _cur["pe_a"] += 8; _cur["pe_b"] += 16; _cur["ve_a"] += 16
            _cur["ac_a"] += 8; _cur["ac_b"] += 16; _cur["po_a"] += 16
            FD.append(dict(_cur))
            if EMIT_FD:
                _cur["pe_a"] += 8; _cur["pe_b"] += 32; _cur["ve_a"] += 8
                _cur["ve_b"] += 32
                if e == NFC - 1:
                    _cur["ve_c"] += 2

        def fb_gpsimd(g, e):
            bp1, bp2, bva = FB[e]["pe_a"], FB[e]["pe_b"], FB[e]["ve_a"]
            if e == 0:
                _w(g, pe_a, bp1)
                _w(g, ve_a, bva)
                _ffn_bundle(g, 0)
                _ffn_w13(g, 0)
                _ffn_w13(g, 1)
                _ffn_w2(g, 0)
                _ffn_w2(g, 1)
            if e > 0:
                _w(g, pe_b, bp2)
                _ffn_w2(g, 2 * e)
                _ffn_w2(g, 2 * e + 1)
            if e + 1 < NFC:
                _ffn_bundle(g, e + 1)
            _w(g, pe_a, bp1 + 2)
            _ffn_w13(g, e * SUBS + 2)
            _w(g, pe_a, bp1 + 4)
            _ffn_w13(g, e * SUBS + 3)
            if e + 1 < NFC:
                _w(g, pe_a, bp1 + 6)
                _ffn_w13(g, (e + 1) * SUBS)
                _w(g, pe_a, bp1 + 8)
                _ffn_w13(g, (e + 1) * SUBS + 1)

        def fb_tensor(te, e):
            bva, bac = FB[e]["ve_a"], FB[e]["ac_a"]
            for m in range(SUBS * 2):
                j, tc = divmod(m, 2)
                c = e * SUBS + j
                sl = c % 2
                if m == 0:
                    _w(te, ve_a, bva)
                    _w(te, ve_b, FB[e]["ve_b"])
                if tc == 0:
                    _w(te, d_w1[sl], w1_wait[c])
                    _w(te, d_w3[sl], w3_wait[c])
                    if j == 0:
                        _w(te, d_be[e % 2], be_wait[e])
                _w(te, ac_a, bac + (m - 2) + 1)
                for hh in range(HT):
                    te.matmul(pfa[m % 2][:, :], w1b[sl][:, hh * 128:(hh + 1) * 128],
                              xnTo_s(hh, tc), start=(hh == 0), stop=(hh == HT - 1))
                for hh in range(HT):
                    ins = te.matmul(pfa[2 + m % 2][:, :],
                                    w3b[sl][:, hh * 128:(hh + 1) * 128],
                                    xnTo_s(hh, tc), start=(hh == 0), stop=(hh == HT - 1))
                _inc(ins, pe_a)
                for k in range(2):
                    mk = m * 2 + k
                    _w(te, ve_a, bva + (mk - 2) + 1)
                    for q in range(2):
                        te.matmul(pfa[4 + k][:, :],
                                  b1re[e % 2][:, q * 512 + j * 128:q * 512 + (j + 1) * 128],
                                  m1b[k][:, q * TOK + tc * 512:q * TOK + (tc + 1) * 512],
                                  start=(q == 0), stop=(q == 1))
                    for q in range(2):
                        ins = te.matmul(pfa[6 + k][:, :],
                                        b3re[e % 2][:, q * 512 + j * 128:q * 512 + (j + 1) * 128],
                                        m3b[k][:, q * TOK + tc * 512:q * TOK + (tc + 1) * 512],
                                        start=(q == 0), stop=(q == 1))
                    _inc(ins, pe_b)

        def fb_scalar(sc, e):
            bp1, bva, bpo = FB[e]["pe_a"], FB[e]["ve_a"], FB[e]["po_a"]
            for m in range(SUBS * 2):
                j, tc = divmod(m, 2)
                _w(sc, pe_a, bp1 + m + 1)
                _w(sc, ve_a, bva + (m - 2) * 2 + 2)
                sc.activation(b1sb[m % 2][:, :], pfa[m % 2][:, :], AF.Identity,
                              bias=0.0, scale=1.0)
                _inc(sc.activation(b3sb[m % 2][:, :], pfa[2 + m % 2][:, :],
                                   AF.Identity, bias=0.0, scale=1.0), ac_a)
                for k in range(2):
                    mk = m * 2 + k
                    _w(sc, ve_a, bva + mk + 1)
                    _w(sc, po_a, bpo + (mk - 2) + 1)
                    _inc(sc.activation(s1sb[k][:, :], x1sb[k][:, :], AF.Silu,
                                       bias=0.0, scale=1.0), ac_b)

        def fb_vector(v, e):
            bp2, bac, bab, bpo = (FB[e]["pe_b"], FB[e]["ac_a"], FB[e]["ac_b"],
                                  FB[e]["po_a"])

            def pool_part(mk2):
                m2, k2 = divmod(mk2, 2)
                j2, tc2 = divmod(m2, 2)
                _w(v, ac_b, bab + mk2 + 1)
                _w(v, po_s, _inc(v.tensor_tensor(
                    x3sb[k2][:, :], x3sb[k2][:, :],
                    rwb[k2][:, tc2 * 512:(tc2 + 1) * 512], ALU.mult), po_s))
                _inc(v.tensor_tensor(
                    x2v(k2)[:, j2 * TOK + tc2 * 512:j2 * TOK + (tc2 + 1) * 512],
                    s1sb[k2][:, :], x3sb[k2][:, :], ALU.mult), po_a)

            for m in range(SUBS * 2):
                j, tc = divmod(m, 2)
                for k in range(2):
                    mk = m * 2 + k
                    _w(v, pe_b, bp2 + mk + 1)
                    _w(v, ac_a, bac + m + 1)
                    _w(v, ac_b, bab + (mk - 2) + 1)
                    v.tensor_tensor(x1sb[k][:, :], pfa[4 + k][:, :],
                                    b1sb[m % 2][:, :], ALU.add)
                    if mk >= 2:
                        pool_part(mk - 2)
                    _w(v, po_a, bpo + (mk - 2) + 1)
                    _inc(v.tensor_tensor(x3sb[k][:, :], pfa[6 + k][:, :],
                                         b3sb[m % 2][:, :], ALU.add), ve_a)
            pool_part(14)
            pool_part(15)

        FD_ORDER = [(0, 0, 0), (0, 1, 0), (1, 0, 0), (1, 1, 0),
                    (0, 0, 1), (0, 1, 1), (1, 0, 1), (1, 1, 1)]

        def fd_tensor(te, e):
            dva, dvb, dvc, dpo = (FD[e]["ve_a"], FD[e]["ve_b"], FD[e]["ve_c"],
                                  FD[e]["po_a"])
            bpo = dpo - 16
            last_po = -1
            for idx, (k, q, tc) in enumerate(FD_ORDER):
                need = bpo + (6 + tc) * 2 + k + 1
                if need > last_po:
                    _w(te, po_a, need)
                    last_po = need
                for j in range(SUBS):
                    ins = te.matmul(
                        pfa[idx][:, :],
                        a2te[e % 2][:, j * ER + q * 128:j * ER + q * 128 + 128],
                        x2v(k)[:, j * TOK + tc * 512:j * TOK + (tc + 1) * 512],
                        start=(j == 0), stop=(j == SUBS - 1))
                _inc(ins, pe_a)
            for gidx in range(32):
                h, tc = divmod(gidx, 2)
                if gidx == 0:
                    _w(te, ve_a, dva + 8)
                    _w(te, d_w2[(2 * e) % 3], w2_wait[2 * e])
                    _w(te, d_w2[(2 * e + 1) % 3], w2_wait[2 * e + 1])
                    if e == NFC - 1:
                        _w(te, ve_c, dvc + 2)
                        _w(te, d_b2, B2RS_N)
                _w(te, ve_b, dvb + (gidx - 4) + 1)
                notlast = (e < NFC - 1)
                for k in range(2):
                    for j in range(SUBS):
                        half = j // 2
                        ap_ = w2slot_ap[(2 * e + half) % 3]
                        ins = te.matmul(
                            pfa[gidx % 4][:, :],
                            ap_[:, (j % 2) * D + h * 128:(j % 2) * D + (h + 1) * 128],
                            x2v(k)[:, j * TOK + tc * 512:j * TOK + (tc + 1) * 512],
                            start=(k == 0 and j == 0),
                            stop=(notlast and k == 1 and j == SUBS - 1))
                if e == NFC - 1:
                    for k in range(2):
                        for q in range(2):
                            ins = te.matmul(
                                pfa[gidx % 4][:, :],
                                b2rs[:, q * D + h * 128:q * D + (h + 1) * 128],
                                m1b[k][:, q * TOK + tc * 512:q * TOK + (tc + 1) * 512],
                                start=False, stop=(k == 1 and q == 1))
                _inc(ins, pe_b)

        def fd_vector(v, e):
            dp1, dp2, dva = FD[e]["pe_a"], FD[e]["pe_b"], FD[e]["ve_a"]
            for idx, (k, q, tc) in enumerate(FD_ORDER):
                _w(v, pe_a, dp1 + idx + 1)
                dst = a2acc[k][:, q * TOK + tc * 512:q * TOK + (tc + 1) * 512]
                _inc(v.tensor_tensor(dst, pfa[idx][:, :], dst, ALU.add), ve_a)
            if e == NFC - 1:
                _w(v, ve_a, dva + 8)
                _inc(v.tensor_tensor(m1b[0][:, :], a2acc[0][:, :], mex[0][:, :],
                                     ALU.mult), ve_c)
                _inc(v.tensor_tensor(m1b[1][:, :], a2acc[1][:, :], mex[1][:, :],
                                     ALU.mult), ve_c)
            for gidx in range(32):
                h, tc = divmod(gidx, 2)
                _w(v, pe_b, dp2 + gidx + 1)
                dst = sumT[:, h * TOK + tc * 512:h * TOK + (tc + 1) * 512]
                _inc(v.tensor_tensor(dst, pfa[gidx % 4][:, :], dst, ALU.add), ve_b)

        for e0 in range(0, NFC_RUN, 2):
            es = [e for e in (e0, e0 + 1) if e < NFC_RUN]
            with nc.Block(f"f{e0}") as blk:
                @blk.gpsimd
                def _(g, es=es):
                    for e in es:
                        fb_gpsimd(g, e)

                @blk.tensor
                def _(te, es=es):
                    for e in es:
                        fb_tensor(te, e)
                        if EMIT_FD:
                            fd_tensor(te, e)

                @blk.scalar
                def _(sc, es=es):
                    for e in es:
                        fb_scalar(sc, e)

                @blk.vector
                def _(v, es=es):
                    for e in es:
                        fb_vector(v, e)
                        if EMIT_FD:
                            fd_vector(v, e)

        f_ctx.close()
        ffn_ctx.close()

        emit_final()
        nc._dbg = dict(out=outd, xnTo=xnTo, big0=big0, sumT=sumT, xkv=xkv)
    return nc


def _host_arrays(inputs):
    f32 = np.float32
    g = {}
    g["wq"] = np.ascontiguousarray(inputs["L0_attn_Wq"], f32)
    g["wk"] = np.ascontiguousarray(inputs["L0_attn_Wk"], f32)
    g["wv"] = np.ascontiguousarray(inputs["L0_attn_Wv"], f32)
    g["wo"] = np.ascontiguousarray(inputs["L0_attn_Wo"], f32)
    g["w1t"] = np.ascontiguousarray(inputs["L0_ffn_W1"].T, f32)
    g["w3t"] = np.ascontiguousarray(inputs["L0_ffn_W3"].T, f32)
    g["w2t"] = np.ascontiguousarray(inputs["L0_ffn_W2"].T, f32)
    g["a1t"] = np.ascontiguousarray(inputs["L0_ffn_A1"].reshape(ER, D).T, f32)
    g["a3t"] = np.ascontiguousarray(inputs["L0_ffn_A3"].reshape(ER, D).T, f32)
    g["a2t"] = np.ascontiguousarray(inputs["L0_ffn_A2"].reshape(ER, FF).T, f32)
    g["b1r"] = np.ascontiguousarray(inputs["L0_ffn_B1"].transpose(0, 2, 1).reshape(ER, FF), f32)
    g["b3r"] = np.ascontiguousarray(inputs["L0_ffn_B3"].transpose(0, 2, 1).reshape(ER, FF), f32)
    g["b2r"] = np.ascontiguousarray(inputs["L0_ffn_B2"].transpose(0, 2, 1).reshape(ER, D), f32)
    wgT = np.ascontiguousarray(inputs["L0_ffn_Wg"].T, f32)
    g["wgt"] = np.ascontiguousarray(
        wgT.reshape(HT, 128, 8).transpose(1, 0, 2).reshape(128, 128), f32)
    bq = np.asarray(inputs["L0_attn_bq"], f32)
    bk = np.asarray(inputs["L0_attn_bk"], f32)
    bv = np.asarray(inputs["L0_attn_bv"], f32)
    bo = np.asarray(inputs["L0_attn_bo"], f32)
    g["bqb"] = np.ascontiguousarray(np.tile(bq[None, :], (128, 1))).astype(BF)
    g["bkb"] = np.ascontiguousarray(np.tile(bk[None, :], (128, 1))).astype(BF)
    g["bvb"] = np.ascontiguousarray(np.tile(bv[None, :], (128, 1))).astype(BF)
    g["bot"] = np.ascontiguousarray(bo.reshape(16, 128).T, f32)
    e8 = np.zeros((8, ER), BF)
    for e in range(8):
        e8[e, e * 32:(e + 1) * 32] = 1
    g["e8"] = e8
    g["identb"] = np.eye(128, dtype=BF)
    g["identf"] = np.eye(128, dtype=f32)
    g["ones1"] = np.ones((1, 128), BF)
    return g


def _rope_tables(h):
    f32 = np.float32
    inv = 1.0 / (10000.0 ** (np.arange(64, dtype=np.float64) * 2.0 / 128.0))
    pos_own = h * TOK + np.arange(TOK, dtype=np.float64)
    ang_q = pos_own[:, None] * inv[None, :]
    cq = np.cos(ang_q).astype(f32) * np.float32(SCALE)
    sq = np.sin(ang_q).astype(f32) * np.float32(SCALE)
    cq = cq.reshape(NTT, 128, 64).transpose(1, 0, 2).reshape(128, NTT * 64)
    sq = sq.reshape(NTT, 128, 64).transpose(1, 0, 2).reshape(128, NTT * 64)
    pos_kv = np.zeros(NKT * 128, np.float64)
    pos_kv[:1536] = h * TOK - 512 + np.arange(1536)
    pos_kv[1536:1540] = np.arange(4)
    ang_k = pos_kv[:, None] * inv[None, :]
    ck = np.cos(ang_k).astype(f32)
    sk = np.sin(ang_k).astype(f32)
    ck = ck.reshape(NKT, 128, 64).transpose(1, 0, 2).reshape(128, NKT * 64)
    sk = sk.reshape(NKT, 128, 64).transpose(1, 0, 2).reshape(128, NKT * 64)
    return (np.ascontiguousarray(cq), np.ascontiguousarray(sq),
            np.ascontiguousarray(ck), np.ascontiguousarray(sk))


def _mask_for(h):
    m = np.full((NTT, SLOTS, 128, 128), MASKNEG, np.float32)
    for qb in range(NTT):
        i = (h * TOK + qb * 128 + np.arange(128))[:, None]
        for j in range(SLOTS):
            if j < 5:
                s = (qb + j) * 128 + np.arange(128)[None, :]
                gk = h * TOK - 512 + s
                valid = np.ones_like(gk, bool) if h == 1 else (s >= 512)
                allow = valid & (gk <= i) & ((gk >= i - 512) | (gk < 4))
            else:
                c = np.arange(128)[None, :]
                if h == 0 and qb <= 4:
                    allow = np.zeros((128, 128), bool)
                else:
                    allow = (c < 4) & (c <= i)
            m[qb, j][allow] = 0.0
    return np.ascontiguousarray(
        m.transpose(2, 0, 1, 3).reshape(128, NTT * SLOTS * 128)).astype(BF)


def kernel(**inputs):
    if "nc" not in _CACHE:
        _CACHE["nc"] = build_program()
    nc = _CACHE["nc"]

    x = np.asarray(inputs["L0_attn_x"], np.float32)
    g = _host_arrays(inputs)
    per_h = {}
    for h in range(2):
        cq, sq, ck, sk = _rope_tables(h)
        per_h[h] = dict(cqt=cq, sqt=sq, ckt=ck, skt=sk, maskb=_mask_for(h))

    in_maps = []
    for core in range(8):
        b, h = divmod(core, 2)
        halo = x[b, 512:1024] if h == 1 else np.zeros((512, D), np.float32)
        xkv = np.concatenate(
            [halo, x[b, h * TOK:(h + 1) * TOK], x[b, 0:4],
             np.zeros((124, D), np.float32)], axis=0)
        m = dict(g)
        m.update(per_h[h])
        m["xkv"] = np.ascontiguousarray(xkv)
        in_maps.append(m)

    res = run_bass_kernel_spmd(nc, in_maps, core_ids=list(range(8)))
    out = np.empty((4, D, D), np.float32)
    for core in range(8):
        b, h = divmod(core, 2)
        out[b, h * TOK:(h + 1) * TOK] = res.results[core]["out"]
    return out


# version 16
# speedup vs baseline: 1.4412x; 1.0076x over previous
import math
import numpy as np
import ml_dtypes
from contextlib import ExitStack

import concourse.bass as bass
import concourse.mybir as mybir
from concourse.bass_utils import run_bass_kernel_spmd

F32 = mybir.dt.float32
BF16 = mybir.dt.bfloat16
AF = mybir.ActivationFunctionType
ALU = mybir.AluOpType
AX = mybir.AxisListType
BF = ml_dtypes.bfloat16

D = 2048
HT = 16
TOK = 1024
NTT = 8
KV = 1664
NKT = 13
NH = 16
NG = 4
DH = 128
FF = 7168
NFC = 14
SUBS = 4
ER = 256
SLOTS = 6
EPS = 1e-5
SCALE = 1.0 / math.sqrt(DH)
MASKNEG = -1.0e9

_CACHE = {}


class _Sem:
    __slots__ = ("h", "n")

    def __init__(self, ctx, nc, name):
        self.h = ctx.enter_context(nc.semaphore(name))
        self.n = 0


def _w(eng, sem, v):
    if v > 0:
        eng.wait_ge(sem.h, v)


def _inc(ins, sem, d=1):
    ins.then_inc(sem.h, d)
    sem.n += d
    return sem.n


def build_program(upto=99):
    nc = bass.Bass("TRN2", target_bir_lowering=False)

    xkv = nc.dram_tensor("xkv", [KV, D], F32, kind="ExternalInput")
    maskd = nc.dram_tensor("maskb", [128, NTT * SLOTS * 128], BF16, kind="ExternalInput")
    cqtd = nc.dram_tensor("cqt", [128, NTT * 320], F32, kind="ExternalInput")
    cktd = nc.dram_tensor("ckt", [128, NKT * 320], F32, kind="ExternalInput")
    wqd = nc.dram_tensor("wq", [D, D], F32, kind="ExternalInput")
    wkd = nc.dram_tensor("wk", [D, 512], F32, kind="ExternalInput")
    wvd = nc.dram_tensor("wv", [D, 512], F32, kind="ExternalInput")
    wod = nc.dram_tensor("wo", [D, D], F32, kind="ExternalInput")
    bqbd = nc.dram_tensor("bqb", [128, D], BF16, kind="ExternalInput")
    bkbd = nc.dram_tensor("bkb", [128, 512], BF16, kind="ExternalInput")
    bvbd = nc.dram_tensor("bvb", [128, 512], BF16, kind="ExternalInput")
    botd = nc.dram_tensor("bot", [128, 16], F32, kind="ExternalInput")
    wgtd = nc.dram_tensor("wgt", [128, 128], F32, kind="ExternalInput")
    a1td = nc.dram_tensor("a1t", [D, ER], F32, kind="ExternalInput")
    a3td = nc.dram_tensor("a3t", [D, ER], F32, kind="ExternalInput")
    w1td = nc.dram_tensor("w1t", [D, FF], F32, kind="ExternalInput")
    w3td = nc.dram_tensor("w3t", [D, FF], F32, kind="ExternalInput")
    w2td = nc.dram_tensor("w2t", [FF, D], F32, kind="ExternalInput")
    a2td = nc.dram_tensor("a2t", [FF, ER], F32, kind="ExternalInput")
    b1rd = nc.dram_tensor("b1r", [ER, FF], F32, kind="ExternalInput")
    b3rd = nc.dram_tensor("b3r", [ER, FF], F32, kind="ExternalInput")
    b2rd = nc.dram_tensor("b2r", [ER, D], F32, kind="ExternalInput")
    e8d = nc.dram_tensor("e8", [8, ER], BF16, kind="ExternalInput")
    identbd = nc.dram_tensor("identb", [128, 128], BF16, kind="ExternalInput")
    identfd = nc.dram_tensor("identf", [128, 128], F32, kind="ExternalInput")
    ones1d = nc.dram_tensor("ones1", [1, 128], BF16, kind="ExternalInput")
    outd = nc.dram_tensor("out", [TOK, D], F32, kind="ExternalOutput")

    ctx = ExitStack()
    with ctx:
        d_const = _Sem(ctx, nc, "d_const")
        d_x = [_Sem(ctx, nc, f"d_x{i}") for i in range(2)]
        d_wr = [_Sem(ctx, nc, f"d_wr{i}") for i in range(3)]
        d_w1 = [_Sem(ctx, nc, f"d_w1{i}") for i in range(2)]
        d_w3 = [_Sem(ctx, nc, f"d_w3{i}") for i in range(2)]
        d_w2 = [_Sem(ctx, nc, f"d_w2{i}") for i in range(3)]
        d_be = [_Sem(ctx, nc, f"d_be{i}") for i in range(2)]
        d_sm = _Sem(ctx, nc, "d_sm")
        d_b2 = _Sem(ctx, nc, "d_b2")
        d_out = _Sem(ctx, nc, "d_out")
        d_out2 = _Sem(ctx, nc, "d_out2")
        pe_a = _Sem(ctx, nc, "pe_a")
        pe_b = _Sem(ctx, nc, "pe_b")
        pe_c = _Sem(ctx, nc, "pe_c")
        ve_a = _Sem(ctx, nc, "ve_a")
        ve_b = _Sem(ctx, nc, "ve_b")
        ve_c = _Sem(ctx, nc, "ve_c")
        ac_a = _Sem(ctx, nc, "ac_a")
        ac_b = _Sem(ctx, nc, "ac_b")
        ac_c = _Sem(ctx, nc, "ac_c")
        po_a = _Sem(ctx, nc, "po_a")
        ve_s = _Sem(ctx, nc, "ve_s")
        po_s = _Sem(ctx, nc, "po_s")
        SEMS = [d_const, d_sm, d_b2, d_out, d_out2, pe_a, pe_b, pe_c, ve_a, ve_b,
                ve_c, ac_a, ac_b, ac_c, po_a, ve_s, po_s] + d_x + d_wr + d_w1 + d_w3 + d_w2 + d_be

        xnTo = ctx.enter_context(nc.sbuf_tensor("xnTo", [128, HT * TOK], BF16))
        big0 = ctx.enter_context(nc.sbuf_tensor("big0", [128, HT * TOK], BF16))
        sumT = ctx.enter_context(nc.sbuf_tensor("sumT", [128, HT * TOK], BF16))
        identb = ctx.enter_context(nc.sbuf_tensor("identb_s", [128, 128], BF16))
        identf = ctx.enter_context(nc.sbuf_tensor("identf_s", [128, 128], F32))
        ones1 = ctx.enter_context(nc.sbuf_tensor("ones1_s", [1, 128], BF16))
        e8s = ctx.enter_context(nc.sbuf_tensor("e8_s", [8, ER], BF16))
        bots = ctx.enter_context(nc.sbuf_tensor("bot_s", [128, 16], F32))

        def emit_final():
            fin_ctx = ExitStack()
            xr = [fin_ctx.enter_context(nc.sbuf_tensor(f"xr{i}", [128, D], F32)) for i in range(2)]
            osb = [fin_ctx.enter_context(nc.sbuf_tensor(f"osb{i}", [128, D], F32)) for i in range(2)]
            pfin = [fin_ctx.enter_context(nc.psum_tensor(f"pfin{i}", [128, D], BF16))
                    for i in range(2)]
            fp, fv = pe_a.n, ve_a.n
            d_st = [d_out, d_out2]
            xr_wait = []
            st_wait = []
            fin_gate = [(sm, sm.n) for sm in SEMS]
            with nc.Block("final") as blk:
                @blk.sync
                def _(s):
                    for sm, n in fin_gate:
                        _w(s, sm, n)
                    for tt in range(NTT + 1):
                        if tt < NTT:
                            sl = tt % 2
                            _w(s, ve_a, fv + (tt - 2) + 1)
                            s.dma_start(xr[sl][:, :],
                                        xkv.ap()[512 + tt * 128:512 + (tt + 1) * 128, :])\
                                .then_inc(d_x[sl].h, 16)
                            d_x[sl].n += 16
                            xr_wait.append(d_x[sl].n)
                        if tt >= 1:
                            osl = (tt - 1) % 2
                            _w(s, ve_a, fv + tt)
                            s.dma_start(outd.ap()[(tt - 1) * 128:tt * 128, :],
                                        osb[osl][:, :]).then_inc(d_st[osl].h, 16)
                            d_st[osl].n += 16
                            st_wait.append(d_st[osl].n)
                    s.wait_ge(d_out.h, d_out.n)
                    s.wait_ge(d_out2.h, d_out2.n)

                @blk.tensor
                def _(te):
                    for sm, n in fin_gate:
                        if sm not in (d_out,):
                            _w(te, sm, n)
                    for tt in range(NTT):
                        _w(te, ve_a, fv + (tt - 2) + 1)
                        for hh in range(HT):
                            ins = te.transpose(pfin[tt % 2][:, hh * 128:(hh + 1) * 128],
                                               sumT[:, hh * TOK + tt * 128:hh * TOK + (tt + 1) * 128],
                                               identb[:, :])
                        _inc(ins, pe_a)

                @blk.vector
                def _(v):
                    for tt in range(NTT):
                        sl = tt % 2
                        _w(v, pe_a, fp + tt + 1)
                        _w(v, d_x[sl], xr_wait[tt])
                        if tt >= 2:
                            _w(v, d_st[sl], st_wait[tt - 2])
                        _inc(v.tensor_add(osb[sl][:, :], pfin[sl][:, :], xr[sl][:, :]), ve_a)

            fin_ctx.close()


        def xnTo_t(h):
            return xnTo[:, h * TOK:(h + 1) * TOK]

        def xnTo_s(h, tc):
            return xnTo[:, h * TOK + tc * 512:h * TOK + (tc + 1) * 512]

        def x2v(k):
            return big0[:, k * SUBS * TOK:(k + 1) * SUBS * TOK]

        w2slot_ap = [big0[:, 8192:12288], big0[:, 12288:16384], None]

        attn_ctx = ExitStack()
        KTs = attn_ctx.enter_context(nc.sbuf_tensor("KTs", [128, NG * KV], BF16))
        Vs = attn_ctx.enter_context(nc.sbuf_tensor("Vs", [128, NKT * 512], BF16))

        lnqkv_ctx = ExitStack()
        xnTh = lnqkv_ctx.enter_context(nc.sbuf_tensor("xnTh", [128, HT * 640], BF16))

        def xnTh_t(h):
            return xnTh[:, h * 640:(h + 1) * 640]

        def xn_kv(h, kt):
            if kt < 4:
                return xnTh_t(h)[:, kt * 128:(kt + 1) * 128]
            if kt < 12:
                return xnTo_t(h)[:, (kt - 4) * 128:(kt - 3) * 128]
            return xnTh_t(h)[:, 512:640]

        with nc.Block("consts") as blk:
            @blk.sync
            def _(s):
                for dst, src in ((identb, identbd), (identf, identfd), (ones1, ones1d),
                                 (e8s, e8d), (bots, botd)):
                    s.dma_start(dst[:, :], src.ap()).then_inc(d_const.h, 16)
                    d_const.n += 16
        CONST_N = d_const.n

        ln_ctx = ExitStack()
        xin = [ln_ctx.enter_context(nc.sbuf_tensor(f"xin{i}", [128, D], F32)) for i in range(2)]
        sqs = ln_ctx.enter_context(nc.sbuf_tensor("sqs", [128, D], F32))
        sms = ln_ctx.enter_context(nc.sbuf_tensor("sms", [128, D], F32))
        xnt = [ln_ctx.enter_context(nc.sbuf_tensor(f"xnt{i}", [128, D], BF16)) for i in range(2)]
        lst = [ln_ctx.enter_context(nc.sbuf_tensor(f"lst{i}", [128, 8], F32)) for i in range(2)]
        pt_ln = [ln_ctx.enter_context(nc.psum_tensor(f"pt_ln{i}", [128, 2048], BF16))
                 for i in range(2)]

        lnv, lna, lnp, lne = ve_a.n, ac_a.n, pe_a.n, ve_b.n

        def _ln_evac(v, t):
            _w(v, pe_a, lnp + t + 1)
            if t < 4:
                dst = xnTh[:, :].rearrange("p (h c) -> p h c", c=640)[:, :, t * 128:(t + 1) * 128]
            elif t < 12:
                dst = xnTo[:, :].rearrange("p (h c) -> p h c", c=TOK)[:, :, (t - 4) * 128:(t - 3) * 128]
            else:
                dst = xnTh[:, :].rearrange("p (h c) -> p h c", c=640)[:, :, 512:640]
            src = pt_ln[t % 2][:, :].rearrange("p (h c) -> p h c", c=128)
            _inc(v.tensor_copy(dst, src), ve_b)

        with nc.Block("ln") as blk:
            @blk.sync
            def _(s):
                for t in range(NKT):
                    sl = t % 2
                    _w(s, ac_a, lna + (t - 2) * 2 + 2)
                    s.dma_start(xin[sl][:, :], xkv.ap()[t * 128:(t + 1) * 128, :])\
                        .then_inc(d_x[sl].h, 16)
                    d_x[sl].n += 16

            @blk.vector
            def _(v):
                for t in range(NKT):
                    sl = t % 2
                    st = lst[sl]
                    _w(v, d_x[sl], 16 * (t // 2 + 1))
                    _inc(v.scalar_tensor_tensor(sqs[:, :], xin[sl][:, :], 1.0, xin[sl][:, :],
                                                ALU.mult, ALU.mult, accum_out=st[:, 1:2]), ve_s)
                    _inc(v.tensor_scalar(sms[:, :], xin[sl][:, :], 1.0, 0.0, ALU.mult,
                                         ALU.add, accum_out=st[:, 0:1]), ve_s)
                    _w(v, ve_s, ve_s.n)
                    v.tensor_scalar(st[:, 2:3], st[:, 0:1], 1.0 / D, None, ALU.mult)
                    _w(v, ve_s, _inc(v.tensor_scalar(st[:, 3:4], st[:, 1:2], 1.0 / D,
                                                     EPS, ALU.mult, ALU.add), ve_s))
                    _w(v, ve_s, _inc(v.scalar_tensor_tensor(st[:, 4:5], st[:, 2:3], 1.0,
                                                            st[:, 2:3], ALU.mult,
                                                            ALU.mult), ve_s))
                    _inc(v.scalar_tensor_tensor(st[:, 5:6], st[:, 4:5], -1.0, st[:, 3:4],
                                                ALU.mult, ALU.add), ve_a)
                    _w(v, ac_a, lna + t * 2 + 1)
                    _w(v, ve_s, _inc(v.reciprocal(st[:, 6:7], st[:, 6:7]), ve_s))
                    _inc(v.scalar_tensor_tensor(st[:, 7:8], st[:, 2:3], -1.0, st[:, 6:7],
                                                ALU.mult, ALU.mult), ve_a)
                    if t >= 1:
                        _ln_evac(v, t - 1)
                _ln_evac(v, NKT - 1)

            @blk.scalar
            def _(sc):
                for t in range(NKT):
                    sl = t % 2
                    st = lst[sl]
                    _w(sc, ve_a, lnv + t * 2 + 1)
                    _inc(sc.activation(st[:, 6:7], st[:, 5:6], AF.Sqrt, bias=0.0, scale=1.0),
                         ac_a)
                    _w(sc, ve_a, lnv + t * 2 + 2)
                    _inc(sc.activation(xnt[sl][:, :], xin[sl][:, :], AF.Identity,
                                       bias=st[:, 7:8], scale=st[:, 6:7]), ac_a)

            @blk.tensor
            def _(te):
                for t in range(NKT):
                    sl = t % 2
                    if t == 0:
                        _w(te, d_const, CONST_N)
                    _w(te, ac_a, lna + t * 2 + 2)
                    _w(te, ve_b, lne + (t - 2) + 1)
                    for hh in range(HT):
                        ins = te.transpose(pt_ln[sl][:, hh * 128:(hh + 1) * 128],
                                           xnt[sl][:, hh * 128:(hh + 1) * 128], identb[:, :])
                    _inc(ins, pe_a)

        LN_END = [(pe_a, pe_a.n), (ve_a, ve_a.n), (ve_b, ve_b.n), (ac_a, ac_a.n)]
        ln_ctx.close()
        if upto <= 1:
            qkv_ctx = None
            lnqkv_ctx.close(); attn_ctx.close()
            emit_final()
            nc._dbg = dict(out=outd, xnTo=xnTo, big0=big0, sumT=sumT, xkv=xkv)
            return nc

        qkv_ctx = ExitStack()
        wr = [qkv_ctx.enter_context(nc.sbuf_tensor(f"wr{i}", [128, HT * 256], BF16))
              for i in range(3)]
        cqs = qkv_ctx.enter_context(nc.sbuf_tensor("cqs", [128, NTT * 320], F32))
        cks = qkv_ctx.enter_context(nc.sbuf_tensor("cks", [128, NKT * 320], F32))
        bqbs = qkv_ctx.enter_context(nc.sbuf_tensor("bqbs", [128, D], BF16))
        bkbs = qkv_ctx.enter_context(nc.sbuf_tensor("bkbs", [128, 512], BF16))
        bvbs = qkv_ctx.enter_context(nc.sbuf_tensor("bvbs", [128, 512], BF16))
        qscr = qkv_ctx.enter_context(nc.sbuf_tensor("qscr", [128, 256], F32))
        rop = qkv_ctx.enter_context(nc.sbuf_tensor("rop", [128, 512], F32))
        qrot = [qkv_ctx.enter_context(nc.sbuf_tensor(f"qrot{i}", [128, 256], BF16))
                for i in range(2)]
        pq = [qkv_ctx.enter_context(nc.psum_tensor(f"pq{i}", [128, 256], F32))
              for i in range(4)]
        ptq = [qkv_ctx.enter_context(nc.psum_tensor(f"ptq{i}", [128, 256], BF16))
               for i in range(2)]

        qkv_items = []
        for c in range(10):
            if c < 8:
                for tt in range(NTT):
                    qkv_items.append(("q", c, tt))
            else:
                for kt in range(NKT):
                    qkv_items.append(("k", c - 8, kt))
        for c in range(2):
            for kt in range(NKT):
                qkv_items.append(("v", c, kt))
        NQKV = len(qkv_items)
        thru = []
        acc = 0
        for c in range(12):
            acc += NTT if c < 8 else NKT
            thru.append(acc)

        qp, qv, qv2, qv3, qp2 = pe_a.n, ve_a.n, ve_b.n, ve_c.n, pe_b.n
        qd = d_const.n
        with nc.Block("qkv") as blk:
            wr_wait = []
            rope_ves = {}

            @blk.gpsimd
            def _(g):
                for c in range(12):
                    sl = c % 3
                    if c == 0:
                        for sm, n in LN_END:
                            _w(g, sm, n)
                    if c >= 3:
                        _w(g, pe_a, qp + thru[c - 3])
                    if c < 8:
                        src = wqd.ap().rearrange("(h p) q -> p h q", p=128)[:, :, c * 256:(c + 1) * 256]
                    elif c < 10:
                        src = wkd.ap().rearrange("(h p) q -> p h q", p=128)[:, :, (c - 8) * 256:(c - 7) * 256]
                    else:
                        src = wvd.ap().rearrange("(h p) q -> p h q", p=128)[:, :, (c - 10) * 256:(c - 9) * 256]
                    g.dma_start(wr[sl][:, :].rearrange("p (h q) -> p h q", q=256), src)\
                        .then_inc(d_wr[sl].h, 16)
                    d_wr[sl].n += 16
                    wr_wait.append(d_wr[sl].n)

            @blk.sync
            def _(s):
                for sm, n in LN_END:
                    _w(s, sm, n)
                for dst, src in ((cqs, cqtd), (cks, cktd),
                                 (bqbs, bqbd), (bkbs, bkbd), (bvbs, bvbd)):
                    s.dma_start(dst[:, :], src.ap()).then_inc(d_const.h, 16)
                    d_const.n += 16

            def _qkv_tr(te, j):
                if qkv_items[j][0] == "v":
                    return
                _w(te, ve_c, qv3 + j + 1)
                _w(te, ve_b, qv2 + (j - 2) + 1)
                for u in range(2):
                    ins = te.transpose(ptq[j % 2][:, u * 128:(u + 1) * 128],
                                       qrot[j % 2][:, u * 128:(u + 1) * 128],
                                       identb[:, :])
                _inc(ins, pe_b)

            @blk.tensor
            def _(te):
                for i in range(NQKV + 2):
                    if i < NQKV:
                        kind, c, sub = qkv_items[i]
                        cg = {"q": c, "k": 8 + c, "v": 10 + c}[kind]
                        sl = cg % 3
                        if i == 0:
                            _w(te, ve_b, lne + NKT)
                        if sub == 0:
                            _w(te, d_wr[sl], wr_wait[cg])
                        _w(te, ve_a, qv + (i - 4) + 1)
                        ps = pq[i % 4]
                        for hh in range(HT):
                            lhsT = (xnTo_t(hh)[:, sub * 128:(sub + 1) * 128] if kind == "q"
                                    else xn_kv(hh, sub))
                            ins = te.matmul(ps[:, :], lhsT, wr[sl][:, hh * 256:(hh + 1) * 256],
                                            start=(hh == 0), stop=(hh == HT - 1))
                        _inc(ins, pe_a)
                    if i >= 2:
                        _qkv_tr(te, i - 2)

            @blk.vector
            def _(v):
                for i, (kind, c, sub) in enumerate(qkv_items):
                    if i == 0:
                        _w(v, d_const, qd + 5 * 16)
                    ps = pq[i % 4]
                    _w(v, pe_a, qp + i + 1)
                    if kind == "v":
                        _inc(v.tensor_tensor(
                            Vs[:, sub * 512 + c * 256:sub * 512 + (c + 1) * 256],
                            ps[:, :], bvbs[:, c * 256:(c + 1) * 256], ALU.add), ve_a)
                        continue
                    bias = (bqbs if kind == "q" else bkbs)[:, c * 256:(c + 1) * 256]
                    if i >= 1:
                        _w(v, ve_s, rope_ves[i - 1])
                        _w(v, ve_c, qv3 + i)
                    _inc(v.tensor_tensor(qscr[:, :], ps[:, :], bias, ALU.add), ve_a)
                    _w(v, ve_a, qv + i + 1)
                    tab = cqs if kind == "q" else cks
                    base = sub * 320
                    _w(v, pe_b, qp2 + (i - 2) + 1)
                    qr = qrot[i % 2]
                    v.tensor_tensor(rop[:, 0:256], qscr[:, :],
                                    tab[:, base:base + 256], ALU.mult)
                    ins = v.tensor_tensor(rop[:, 256:512], qscr[:, :],
                                          tab[:, base + 64:base + 320], ALU.mult)
                    rope_ves[i] = _inc(ins, ve_s)
                    _w(v, ve_s, rope_ves[i])
                    for hd in range(2):
                        o = hd * 128
                        v.tensor_sub(qr[:, o:o + 64], rop[:, o:o + 64],
                                     rop[:, o + 64:o + 128])
                        ins = v.tensor_add(qr[:, o + 64:o + 128], rop[:, 256 + o:256 + o + 64],
                                           rop[:, 256 + o + 64:256 + o + 128])
                    _inc(ins, ve_c)
                    _w(v, pe_b, qp2 + i + 1)
                    src3 = ptq[i % 2][:, :].rearrange("p (u c2) -> p u c2", c2=128)
                    if kind == "q":
                        dst3 = big0[:, :].rearrange("p (hh t) -> p hh t", t=TOK)[
                            :, 2 * c:2 * c + 2, sub * 128:(sub + 1) * 128]
                    else:
                        dst3 = KTs[:, :].rearrange("p (gg t) -> p gg t", t=KV)[
                            :, 2 * c:2 * c + 2, sub * 128:(sub + 1) * 128]
                    _inc(v.tensor_copy(dst3, src3), ve_b)

        qkv_ctx.close()
        lnqkv_ctx.close()
        if upto <= 2:
            attn_ctx.close()
            emit_final()
            nc._dbg = dict(out=outd, xnTo=xnTo, big0=big0, sumT=sumT, xkv=xkv)
            return nc

        sc_ctx = ExitStack()
        attnT = sc_ctx.enter_context(nc.sbuf_tensor("attnT", [128, HT * TOK], BF16))
        sm_ctx = ExitStack()
        masks = sm_ctx.enter_context(nc.sbuf_tensor("masks", [128, NTT * SLOTS * 128], BF16))
        ssb = [sm_ctx.enter_context(nc.sbuf_tensor(f"ssb{i}", [128, SLOTS * 128], F32))
               for i in range(2)]
        pnb = [sm_ctx.enter_context(nc.sbuf_tensor(f"pnb{i}", [128, SLOTS * 128], BF16))
               for i in range(4)]
        pnT = [sm_ctx.enter_context(nc.sbuf_tensor(f"pnT{i}", [128, SLOTS * 128], BF16))
               for i in range(4)]
        nmz = sm_ctx.enter_context(nc.sbuf_tensor("nmz", [128, 12], F32))
        psc = [sm_ctx.enter_context(nc.psum_tensor(f"psc{i}", [128, SLOTS * 128], F32))
               for i in range(2)]
        ppn = [sm_ctx.enter_context(nc.psum_tensor(f"ppn{i}", [128, SLOTS * 128], BF16))
               for i in range(2)]
        pwv = [sm_ctx.enter_context(nc.psum_tensor(f"pwv{i}", [128, 512], F32))
               for i in range(2)]

        sp1, sp2, sp3 = pe_a.n, pe_b.n, pe_c.n
        sv1, sv2, sv3 = ve_a.n, ve_b.n, ve_c.n
        sa1, sa2, sa3 = ac_a.n, ac_b.n, ac_c.n
        sxd = d_x[0].n

        def slot_kt(qb, j):
            return qb + j if j < 5 else 12

        with nc.Block("scores") as blk:
            @blk.sync
            def _(s):
                _w(s, pe_a, sp1)
                _w(s, pe_b, sp2)
                _w(s, ve_b, sv2)
                s.dma_start(masks[:, :], maskd.ap()).then_inc(d_x[0].h, 16)
                d_x[0].n += 16

            def _sc_mm(te, i):
                hh, qb = divmod(i, NTT)
                g = hh // 4
                _w(te, ve_a, sv1 + (i - 2) + 1)
                qtile = big0[:, hh * TOK + qb * 128:hh * TOK + (qb + 1) * 128]
                base = g * KV
                te.matmul(psc[i % 2][:, 0:512], qtile,
                          KTs[:, base + qb * 128:base + qb * 128 + 512],
                          start=True, stop=True)
                te.matmul(psc[i % 2][:, 512:640], qtile,
                          KTs[:, base + (qb + 4) * 128:base + (qb + 5) * 128],
                          start=True, stop=True)
                ins = te.matmul(psc[i % 2][:, 640:768], qtile,
                                KTs[:, base + 12 * 128:base + 13 * 128],
                                start=True, stop=True)
                _inc(ins, pe_a)

            def _sc_tr(te, j):
                _w(te, ac_b, sa2 + j + 1)
                _w(te, ve_c, sv3 + (j - 2) + 1)
                for u in range(SLOTS):
                    ins = te.transpose(ppn[j % 2][:, u * 128:(u + 1) * 128],
                                       pnb[j % 4][:, u * 128:(u + 1) * 128],
                                       identb[:, :])
                _inc(ins, pe_b)

            def _sc_wv(te, w):
                hh, qb = divmod(w, NTT)
                g = hh // 4
                q4 = w // 4
                _w(te, ve_c, sv3 + w + 1)
                if w % 4 == 0:
                    _w(te, ac_c, sa3 + (q4 - 2) + 1)
                for j in range(SLOTS):
                    kt = slot_kt(qb, j)
                    ins = te.matmul(
                        pwv[q4 % 2][:, (w % 4) * 128:(w % 4 + 1) * 128],
                        Vs[:, kt * 512 + g * 128:kt * 512 + (g + 1) * 128],
                        pnT[w % 4][:, j * 128:(j + 1) * 128],
                        start=(j == 0), stop=(j == SLOTS - 1))
                _inc(ins, pe_c)

            @blk.tensor
            def _(te):
                _w(te, ve_a, sv1)
                _w(te, ve_b, sv2)
                for i in range(NH * NTT + 6):
                    if i < NH * NTT:
                        _sc_mm(te, i)
                    if i >= 4 and i - 4 < NH * NTT:
                        _sc_tr(te, i - 4)
                    if i >= 6:
                        _sc_wv(te, i - 6)

            @blk.vector
            def _(v):
                def _norm(j):
                    jl = j % 2
                    _w(v, ac_b, sa2 + (j - 2) + 1)
                    _w(v, ve_b, _inc(v.reciprocal(nmz[:, 4 + jl:5 + jl],
                                                  nmz[:, 2 + jl:3 + jl]), ve_b))
                    _w(v, pe_b, sp2 + (j - 4) + 1)
                    _inc(v.tensor_scalar(pnb[j % 4][:, :], pnb[j % 4][:, :],
                                         nmz[:, 4 + jl:5 + jl], None, ALU.mult), ac_b)

                for i in range(NH * NTT):
                    hh, qb = divmod(i, NTT)
                    sl = i % 2
                    if i == 0:
                        _w(v, d_x[0], sxd + 16)
                    _w(v, pe_a, sp1 + i + 1)
                    _w(v, ac_a, sa1 + (i - 2) + 1)
                    _w(v, ve_s, _inc(v.tensor_tensor(ssb[sl][:, :], psc[sl][:, :],
                                                     masks[:, qb * 768:(qb + 1) * 768],
                                                     ALU.add), ve_s))
                    _inc(v.reduce_max(nmz[:, sl:sl + 1], ssb[sl][:, :], axis=AX.X,
                                      negate=True), ve_a)
                    if i >= 2:
                        _norm(i - 2)
                _w(v, ac_a, sa1 + NH * NTT)
                _norm(NH * NTT - 2)
                _norm(NH * NTT - 1)

            @blk.scalar
            def _(sc):
                def _evac(g):
                    il = 4 * g + 3
                    ghh, gqb = divmod(il, NTT)
                    _w(sc, pe_c, sp3 + il + 1)
                    _inc(sc.activation(
                        attnT[:, ghh * TOK + (gqb - 3) * 128:ghh * TOK + (gqb + 1) * 128],
                        pwv[g % 2][:, :], AF.Identity, bias=0.0, scale=1.0), ac_c)

                for i in range(NH * NTT):
                    hh, qb = divmod(i, NTT)
                    sl = i % 2
                    _w(sc, ve_a, sv1 + i + 1)
                    _w(sc, ve_b, sv2 + (i - 2) + 1)
                    _w(sc, pe_b, sp2 + (i - 4) + 1)
                    _inc(sc.activation(pnb[i % 4][:, :], ssb[sl][:, :], AF.Exp,
                                       bias=nmz[:, sl:sl + 1], scale=1.0,
                                       accum_out=nmz[:, 2 + sl:3 + sl]), ac_a)
                    if i >= 2:
                        j = i - 2
                        _w(sc, pe_b, sp2 + j + 1)
                        _w(sc, pe_c, sp3 + (j - 4) + 1)
                        _inc(sc.activation(pnT[j % 4][:, :], ppn[j % 2][:, :],
                                           AF.Identity, bias=0.0, scale=1.0), ve_c)
                    if i % 4 == 3 and i >= 7:
                        _evac(i // 4 - 1)
                for j in (NH * NTT - 2, NH * NTT - 1):
                    _w(sc, pe_b, sp2 + j + 1)
                    _w(sc, pe_c, sp3 + (j - 4) + 1)
                    _inc(sc.activation(pnT[j % 4][:, :], ppn[j % 2][:, :],
                                       AF.Identity, bias=0.0, scale=1.0), ve_c)
                _evac(NH * NTT // 4 - 1)

        sm_ctx.close()
        if upto <= 3:
            sc_ctx.close(); attn_ctx.close()
            emit_final()
            nc._dbg = dict(out=outd, xnTo=xnTo, big0=big0, sumT=sumT, xkv=xkv, attnT=attnT)
            return nc

        wo_ctx = ExitStack()
        wow = [wo_ctx.enter_context(nc.sbuf_tensor(f"wow{i}", [128, HT * 256], BF16))
               for i in range(3)]
        pwo = [wo_ctx.enter_context(nc.psum_tensor(f"pwo{i}", [128, 512], F32))
               for i in range(4)]
        op, oa = pe_a.n, ac_a.n
        WO_GATE = [(pe_b, pe_b.n), (pe_c, pe_c.n), (ve_a, ve_a.n),
                   (ve_c, ve_c.n), (ac_b, ac_b.n), (ac_c, ac_c.n)]
        with nc.Block("wo") as blk:
            wo_wait = []

            @blk.gpsimd
            def _(g):
                for c in range(8):
                    sl = c % 3
                    if c == 0:
                        for sm, n in WO_GATE:
                            _w(g, sm, n)
                    if c >= 3:
                        _w(g, pe_a, op + (c - 3 + 1) * 4)
                    g.dma_start(
                        wow[sl][:, :].rearrange("p (h q) -> p h q", q=256),
                        wod.ap().rearrange("(h p) q -> p h q", p=128)[:, :, c * 256:(c + 1) * 256]
                    ).then_inc(d_wr[sl].h, 16)
                    d_wr[sl].n += 16
                    wo_wait.append(d_wr[sl].n)

            @blk.tensor
            def _(te):
                for m in range(32):
                    co, tc = divmod(m, 2)
                    c = co // 2
                    sl = c % 3
                    if m == 0:
                        for sm, n in WO_GATE:
                            _w(te, sm, n)
                    if m % 4 == 0:
                        _w(te, d_wr[sl], wo_wait[c])
                    _w(te, ac_a, oa + (m - 4) + 1)
                    for ci in range(HT):
                        ins = te.matmul(
                            pwo[m % 4][:, :],
                            wow[sl][:, ci * 256 + (co % 2) * 128:ci * 256 + (co % 2) * 128 + 128],
                            attnT[:, ci * TOK + tc * 512:ci * TOK + (tc + 1) * 512],
                            start=(ci == 0), stop=(ci == HT - 1))
                    _inc(ins, pe_a)

            @blk.scalar
            def _(sc):
                for m in range(32):
                    co, tc = divmod(m, 2)
                    _w(sc, pe_a, op + m + 1)
                    _inc(sc.activation(sumT[:, co * TOK + tc * 512:co * TOK + (tc + 1) * 512],
                                       pwo[m % 4][:, :], AF.Identity,
                                       bias=bots[:, co:co + 1], scale=1.0), ac_a)

        wo_ctx.close()
        sc_ctx.close()
        attn_ctx.close()
        if upto <= 4:
            emit_final()
            nc._dbg = dict(out=outd, xnTo=xnTo, big0=big0, sumT=sumT, xkv=xkv)
            return nc

        ffn_ctx = ExitStack()
        m1b = [ffn_ctx.enter_context(nc.sbuf_tensor(f"m1b{i}", [128, 2 * TOK], BF16))
               for i in range(2)]
        m3b = [ffn_ctx.enter_context(nc.sbuf_tensor(f"m3b{i}", [128, 2 * TOK], BF16))
               for i in range(2)]
        mex = [ffn_ctx.enter_context(nc.sbuf_tensor(f"mex{i}", [128, 2 * TOK], BF16))
               for i in range(2)]
        rwb = [ffn_ctx.enter_context(nc.sbuf_tensor(f"rwb{i}", [128, TOK], F32))
               for i in range(2)]
        a2acc = [ffn_ctx.enter_context(nc.sbuf_tensor(f"a2acc{i}", [128, 2 * TOK], BF16))
                 for i in range(2)]
        b2rs = ffn_ctx.enter_context(nc.sbuf_tensor("b2rs", [128, 2 * D], BF16))

        rt_ctx = ExitStack()
        wgts = rt_ctx.enter_context(nc.sbuf_tensor("wgts", [128, 128], BF16))
        a1ts = rt_ctx.enter_context(nc.sbuf_tensor("a1ts", [128, HT * ER], BF16))
        a3ts = rt_ctx.enter_context(nc.sbuf_tensor("a3ts", [128, HT * ER], BF16))
        lg = rt_ctx.enter_context(nc.sbuf_tensor("lg", [8, TOK], F32))
        lgt = rt_ctx.enter_context(nc.sbuf_tensor("lgt", [128, 64], F32))
        oh = [rt_ctx.enter_context(nc.sbuf_tensor(f"oh{i}", [128, 64], F32)) for i in range(2)]
        lg2 = rt_ctx.enter_context(nc.sbuf_tensor("lg2", [128, 8], F32))
        mcol = rt_ctx.enter_context(nc.sbuf_tensor("mcol", [128, 4], F32))
        dall = rt_ctx.enter_context(nc.sbuf_tensor("dall", [128, 8], F32))
        eall = rt_ctx.enter_context(nc.sbuf_tensor("eall", [128, 8], F32))
        rwa = [rt_ctx.enter_context(nc.sbuf_tensor(f"rwa{i}", [128, 8], F32)) for i in range(2)]
        ohT = [rt_ctx.enter_context(nc.sbuf_tensor(f"ohT{i}", [8, TOK], BF16)) for i in range(2)]
        rwT = [rt_ctx.enter_context(nc.sbuf_tensor(f"rwT{i}", [1, TOK], BF16)) for i in range(2)]
        pbig = [rt_ctx.enter_context(nc.psum_tensor(f"pbig{i}", [128, 512], F32))
                for i in range(4)]
        plog = rt_ctx.enter_context(nc.psum_tensor("plog", [8, TOK], F32))
        poh = rt_ctx.enter_context(nc.psum_tensor("poh", [8, TOK], F32))

        rp, rv, ra = pe_a.n, ve_a.n, ac_a.n
        rsm = d_sm.n
        with nc.Block("routing") as blk:
            @blk.gpsimd
            def _(g):
                _w(g, pe_a, rp)
                g.dma_start(wgts[:, :], wgtd.ap()).then_inc(d_sm.h, 16); d_sm.n += 16
                g.dma_start(a1ts[:, :].rearrange("p (h e) -> p h e", e=ER),
                            a1td.ap().rearrange("(h p) e -> p h e", p=128))\
                    .then_inc(d_sm.h, 16); d_sm.n += 16
                g.dma_start(a3ts[:, :].rearrange("p (h e) -> p h e", e=ER),
                            a3td.ap().rearrange("(h p) e -> p h e", p=128))\
                    .then_inc(d_sm.h, 16); d_sm.n += 16
                g.dma_start(b2rs[:, :].rearrange("p (q h) -> p q h", h=D),
                            b2rd.ap().rearrange("(q p) h -> p q h", p=128))\
                    .then_inc(d_b2.h, 16); d_b2.n += 16

            @blk.tensor
            def _(te):
                _w(te, d_sm, rsm + 48)
                _w(te, ac_a, ra)
                for tc in range(2):
                    for hh in range(HT):
                        ins = te.matmul(plog[:, tc * 512:(tc + 1) * 512],
                                        wgts[:, hh * 8:(hh + 1) * 8],
                                        xnTo_s(hh, tc), start=(hh == 0), stop=(hh == HT - 1))
                _inc(ins, pe_a)
                _w(te, ve_a, rv + 1)
                for tt in range(NTT):
                    ins = te.transpose(pbig[3][:, tt * 8:(tt + 1) * 8],
                                       lg[:, tt * 128:(tt + 1) * 128], identf[0:8, 0:8])
                _inc(ins, pe_a)
                _w(te, ve_a, rv + 10)
                for tt in range(NTT):
                    ins = te.transpose(poh[:, tt * 128:(tt + 1) * 128],
                                       oh[0][:, tt * 8:(tt + 1) * 8], identf[:, :])
                _inc(ins, pe_a)
                _w(te, ve_a, rv + 11)
                for tt in range(NTT):
                    pb = pbig[2 + tt // 4]
                    ins = te.transpose(pb[0:1, (tt % 4) * 128:(tt % 4 + 1) * 128],
                                       rwa[0][:, tt:tt + 1], identf[:, :])
                _inc(ins, pe_a)
                _w(te, ve_a, rv + 12)
                for tt in range(NTT):
                    ins = te.transpose(poh[:, tt * 128:(tt + 1) * 128],
                                       oh[1][:, tt * 8:(tt + 1) * 8], identf[:, :])
                _inc(ins, pe_a)
                _w(te, ve_a, rv + 13)
                for tt in range(NTT):
                    pb = pbig[2 + tt // 4]
                    ins = te.transpose(pb[0:1, (tt % 4) * 128:(tt % 4 + 1) * 128],
                                       rwa[1][:, tt:tt + 1], identf[:, :])
                _inc(ins, pe_a)
                for k in range(2):
                    _w(te, ve_a, rv + 15 + k)
                    for q in range(2):
                        for tc in range(2):
                            ins = te.matmul(pbig[q * 2 + tc][:, :],
                                            e8s[0:8, q * 128:(q + 1) * 128],
                                            ohT[k][0:8, tc * 512:(tc + 1) * 512],
                                            start=True, stop=True)
                    _inc(ins, pe_a)
                for k in range(2):
                    _w(te, ve_a, rv + 17)
                    for tc in range(2):
                        ins = te.matmul(pbig[k * 2 + tc][:, :], ones1[:, :],
                                        rwT[k][0:1, tc * 512:(tc + 1) * 512],
                                        start=True, stop=True)
                    _inc(ins, pe_a)
                _w(te, d_sm, rsm + 48)
                for si, src in enumerate((a1ts, a3ts)):
                    _w(te, ve_a, rv + 19 + si)
                    for q in range(2):
                        for tc in range(2):
                            for hh in range(HT):
                                ins = te.matmul(pbig[q * 2 + tc][:, :],
                                                src[:, hh * ER + q * 128:hh * ER + q * 128 + 128],
                                                xnTo_s(hh, tc),
                                                start=(hh == 0), stop=(hh == HT - 1))
                    _inc(ins, pe_a)

            @blk.vector
            def _(v):
                _w(v, pe_a, rp + 1)
                _inc(v.tensor_copy(lg[:, :], plog[:, :]), ve_a)
                _w(v, pe_a, rp + 2)
                _inc(v.tensor_copy(lgt[:, :], pbig[3][:, 0:64]), ve_a)
                _w(v, ve_a, rv + 2)
                for tt in range(NTT):
                    sl_ = lgt[:, tt * 8:(tt + 1) * 8]
                    if tt >= 1:
                        _w(v, ve_a, rv + 2 + tt)
                    _w(v, ve_s, _inc(v.reduce_max(mcol[:, 0:1], sl_, axis=AX.X), ve_s))
                    _w(v, ve_s, _inc(v.tensor_scalar(oh[0][:, tt * 8:(tt + 1) * 8], sl_,
                                                     mcol[:, 0:1], None, ALU.is_ge), ve_s))
                    _w(v, ve_s, _inc(v.scalar_tensor_tensor(lg2[:, :], oh[0][:, tt * 8:(tt + 1) * 8],
                                                            -1.0e30, sl_, ALU.mult, ALU.add), ve_s))
                    _w(v, ve_s, _inc(v.reduce_max(mcol[:, 1:2], lg2[:, :], axis=AX.X), ve_s))
                    v.tensor_scalar(oh[1][:, tt * 8:(tt + 1) * 8], lg2[:, :], mcol[:, 1:2],
                                    None, ALU.is_ge)
                    _inc(v.tensor_sub(dall[:, tt:tt + 1], mcol[:, 1:2], mcol[:, 0:1]),
                         ve_a)
                _w(v, ac_a, ra + 1)
                _w(v, ve_s, _inc(v.tensor_scalar(eall[:, :], eall[:, :], 1.0, None,
                                                 ALU.add), ve_s))
                _w(v, ve_s, _inc(v.reciprocal(rwa[0][:, :], eall[:, :]), ve_s))
                _inc(v.tensor_scalar(rwa[1][:, :], rwa[0][:, :], -1.0, 1.0,
                                     ALU.mult, ALU.add), ve_a)
                _w(v, pe_a, rp + 3)
                _inc(v.tensor_copy(ohT[0][0:8, :], poh[0:8, :]), ve_a)
                _w(v, pe_a, rp + 4)
                v.tensor_copy(rwT[0][0:1, 0:512], pbig[2][0:1, :])
                _inc(v.tensor_copy(rwT[0][0:1, 512:1024], pbig[3][0:1, :]), ve_a)
                _w(v, pe_a, rp + 5)
                _inc(v.tensor_copy(ohT[1][0:8, :], poh[0:8, :]), ve_a)
                _w(v, pe_a, rp + 6)
                v.tensor_copy(rwT[1][0:1, 0:512], pbig[2][0:1, :])
                _inc(v.tensor_copy(rwT[1][0:1, 512:1024], pbig[3][0:1, :]), ve_a)
                for k in range(2):
                    _w(v, pe_a, rp + 7 + k)
                    for q in range(2):
                        for tc in range(2):
                            ins = v.tensor_copy(
                                mex[k][:, q * TOK + tc * 512:q * TOK + (tc + 1) * 512],
                                pbig[q * 2 + tc][:, :])
                    _inc(ins, ve_a)
                for k in range(2):
                    _w(v, pe_a, rp + 9 + k)
                    for tc in range(2):
                        ins = v.tensor_copy(rwb[k][:, tc * 512:(tc + 1) * 512],
                                            pbig[k * 2 + tc][:, :])
                    _inc(ins, ve_a)
                _w(v, ve_a, rv + 17)
                for si, mb in enumerate((m1b, m3b)):
                    _w(v, pe_a, rp + 11 + si)
                    for k in range(2):
                        for q in range(2):
                            for tc in range(2):
                                sl_ = slice(q * TOK + tc * 512, q * TOK + (tc + 1) * 512)
                                ins = v.tensor_tensor(mb[k][:, sl_], pbig[q * 2 + tc][:, :],
                                                      mex[k][:, sl_], ALU.mult)
                    _inc(ins, ve_a)
                v.memset(a2acc[0][:, :], 0.0)
                v.memset(a2acc[1][:, :], 0.0)

            @blk.scalar
            def _(sc):
                _w(sc, ve_a, rv + 10)
                _inc(sc.activation(eall[:, :], dall[:, :], AF.Exp, bias=0.0, scale=1.0),
                     ac_a)

        B2RS_N = d_b2.n
        rt_ctx.close()
        if upto <= 5:
            ffn_ctx.close()
            emit_final()
            nc._dbg = dict(out=outd, xnTo=xnTo, big0=big0, sumT=sumT, xkv=xkv,
                           m1b=m1b, m3b=m3b, mex=mex, rwb=rwb)
            return nc

        f_ctx = ExitStack()
        w1b = [f_ctx.enter_context(nc.sbuf_tensor(f"w1b{i}", [128, HT * 128], BF16))
               for i in range(2)]
        w3b = [f_ctx.enter_context(nc.sbuf_tensor(f"w3b{i}", [128, HT * 128], BF16))
               for i in range(2)]
        b1re = [f_ctx.enter_context(nc.sbuf_tensor(f"b1re{i}", [128, 2 * 512], BF16))
                for i in range(2)]
        b3re = [f_ctx.enter_context(nc.sbuf_tensor(f"b3re{i}", [128, 2 * 512], BF16))
                for i in range(2)]
        a2te = [f_ctx.enter_context(nc.sbuf_tensor(f"a2te{i}", [128, SUBS * ER], BF16))
                for i in range(2)]
        w2x = f_ctx.enter_context(nc.sbuf_tensor("w2x", [128, 2 * D], BF16))
        w2slot_ap[2] = w2x[:, :]
        b1sb = [f_ctx.enter_context(nc.sbuf_tensor(f"b1sb{i}", [128, 512], F32))
                for i in range(2)]
        b3sb = [f_ctx.enter_context(nc.sbuf_tensor(f"b3sb{i}", [128, 512], F32))
                for i in range(2)]
        x1sb = [f_ctx.enter_context(nc.sbuf_tensor(f"x1sb{i}", [128, 512], F32))
                for i in range(2)]
        s1sb = [f_ctx.enter_context(nc.sbuf_tensor(f"s1sb{i}", [128, 512], F32))
                for i in range(2)]
        x3sb = [f_ctx.enter_context(nc.sbuf_tensor(f"x3sb{i}", [128, 512], F32))
                for i in range(2)]
        pfa = [f_ctx.enter_context(nc.psum_tensor(f"pfa{i}", [128, 512], F32))
               for i in range(8)]

        be_wait = {}
        w2_wait = {}

        w1_wait = {}
        w3_wait = {}

        def _ffn_w13(g, c):
            sl = c % 2
            g.dma_start(w1b[sl][:, :].rearrange("p (h f) -> p h f", f=128),
                        w1td.ap().rearrange("(h p) f -> p h f", p=128)[:, :, c * 128:(c + 1) * 128])\
                .then_inc(d_w1[sl].h, 16)
            d_w1[sl].n += 16
            w1_wait[c] = d_w1[sl].n
            g.dma_start(w3b[sl][:, :].rearrange("p (h f) -> p h f", f=128),
                        w3td.ap().rearrange("(h p) f -> p h f", p=128)[:, :, c * 128:(c + 1) * 128])\
                .then_inc(d_w3[sl].h, 16)
            d_w3[sl].n += 16
            w3_wait[c] = d_w3[sl].n

        def _ffn_bundle(g, e):
            sl = e % 2
            g.dma_start(b1re[sl][:, :].rearrange("p (q f) -> p q f", f=512),
                        b1rd.ap().rearrange("(q p) f -> p q f", p=128)[:, :, e * 512:(e + 1) * 512])\
                .then_inc(d_be[sl].h, 16)
            d_be[sl].n += 16
            g.dma_start(b3re[sl][:, :].rearrange("p (q f) -> p q f", f=512),
                        b3rd.ap().rearrange("(q p) f -> p q f", p=128)[:, :, e * 512:(e + 1) * 512])\
                .then_inc(d_be[sl].h, 16)
            d_be[sl].n += 16
            g.dma_start(a2te[sl][:, :].rearrange("p (j e2) -> p j e2", e2=ER),
                        a2td.ap().rearrange("(f p) e2 -> p f e2", p=128)[:, e * SUBS:(e + 1) * SUBS, :])\
                .then_inc(d_be[sl].h, 16)
            d_be[sl].n += 16
            be_wait[e] = d_be[sl].n

        def _ffn_w2(g, he):
            sl = he % 3
            g.dma_start(w2slot_ap[sl].rearrange("p (j h) -> p j h", h=D),
                        w2td.ap().rearrange("(f p) h -> p f h", p=128)[:, he * 2:(he + 1) * 2, :])\
                .then_inc(d_w2[sl].h, 16)
            d_w2[sl].n += 16
            w2_wait[he] = d_w2[sl].n


        NFC_RUN = NFC if upto > 7 else 1
        EMIT_FD = upto != 6

        FB, FD = [], []
        _cur = dict(pe_a=pe_a.n, pe_b=pe_b.n, ve_a=ve_a.n, ve_b=ve_b.n,
                    ve_c=ve_c.n, ac_a=ac_a.n, ac_b=ac_b.n, po_a=po_a.n)
        for e in range(NFC_RUN):
            FB.append(dict(_cur))
            _cur["pe_a"] += 8; _cur["pe_b"] += 16; _cur["ve_a"] += 16
            _cur["ac_a"] += 8; _cur["ac_b"] += 16; _cur["po_a"] += 16
            FD.append(dict(_cur))
            if EMIT_FD:
                _cur["pe_a"] += 8; _cur["pe_b"] += 32; _cur["ve_a"] += 8
                _cur["ve_b"] += 32
                if e == NFC - 1:
                    _cur["ve_c"] += 2

        def fb_gpsimd(g, e):
            bp1, bp2, bva = FB[e]["pe_a"], FB[e]["pe_b"], FB[e]["ve_a"]
            if e == 0:
                _w(g, pe_a, bp1)
                _w(g, ve_a, bva)
                _ffn_bundle(g, 0)
                _ffn_w13(g, 0)
                _ffn_w13(g, 1)
                _ffn_w2(g, 0)
                _ffn_w2(g, 1)
            if e > 0:
                _w(g, pe_b, bp2)
                _ffn_w2(g, 2 * e)
                _ffn_w2(g, 2 * e + 1)
            if e + 1 < NFC:
                _ffn_bundle(g, e + 1)
            _w(g, pe_a, bp1 + 2)
            _ffn_w13(g, e * SUBS + 2)
            _w(g, pe_a, bp1 + 4)
            _ffn_w13(g, e * SUBS + 3)
            if e + 1 < NFC:
                _w(g, pe_a, bp1 + 6)
                _ffn_w13(g, (e + 1) * SUBS)
                _w(g, pe_a, bp1 + 8)
                _ffn_w13(g, (e + 1) * SUBS + 1)

        def fb_tensor(te, e):
            bva, bac = FB[e]["ve_a"], FB[e]["ac_a"]
            for m in range(SUBS * 2):
                j, tc = divmod(m, 2)
                c = e * SUBS + j
                sl = c % 2
                if m == 0:
                    _w(te, ve_a, bva)
                    _w(te, ve_b, FB[e]["ve_b"])
                if tc == 0:
                    _w(te, d_w1[sl], w1_wait[c])
                    _w(te, d_w3[sl], w3_wait[c])
                    if j == 0:
                        _w(te, d_be[e % 2], be_wait[e])
                _w(te, ac_a, bac + (m - 2) + 1)
                for hh in range(HT):
                    te.matmul(pfa[m % 2][:, :], w1b[sl][:, hh * 128:(hh + 1) * 128],
                              xnTo_s(hh, tc), start=(hh == 0), stop=(hh == HT - 1))
                for hh in range(HT):
                    ins = te.matmul(pfa[2 + m % 2][:, :],
                                    w3b[sl][:, hh * 128:(hh + 1) * 128],
                                    xnTo_s(hh, tc), start=(hh == 0), stop=(hh == HT - 1))
                _inc(ins, pe_a)
                for k in range(2):
                    mk = m * 2 + k
                    _w(te, ve_a, bva + (mk - 2) + 1)
                    for q in range(2):
                        te.matmul(pfa[4 + k][:, :],
                                  b1re[e % 2][:, q * 512 + j * 128:q * 512 + (j + 1) * 128],
                                  m1b[k][:, q * TOK + tc * 512:q * TOK + (tc + 1) * 512],
                                  start=(q == 0), stop=(q == 1))
                    for q in range(2):
                        ins = te.matmul(pfa[6 + k][:, :],
                                        b3re[e % 2][:, q * 512 + j * 128:q * 512 + (j + 1) * 128],
                                        m3b[k][:, q * TOK + tc * 512:q * TOK + (tc + 1) * 512],
                                        start=(q == 0), stop=(q == 1))
                    _inc(ins, pe_b)

        def fb_scalar(sc, e):
            bp1, bva, bpo = FB[e]["pe_a"], FB[e]["ve_a"], FB[e]["po_a"]
            for m in range(SUBS * 2):
                j, tc = divmod(m, 2)
                _w(sc, pe_a, bp1 + m + 1)
                _w(sc, ve_a, bva + (m - 2) * 2 + 2)
                sc.activation(b1sb[m % 2][:, :], pfa[m % 2][:, :], AF.Identity,
                              bias=0.0, scale=1.0)
                _inc(sc.activation(b3sb[m % 2][:, :], pfa[2 + m % 2][:, :],
                                   AF.Identity, bias=0.0, scale=1.0), ac_a)
                for k in range(2):
                    mk = m * 2 + k
                    _w(sc, ve_a, bva + mk + 1)
                    _w(sc, po_a, bpo + (mk - 2) + 1)
                    _inc(sc.activation(s1sb[k][:, :], x1sb[k][:, :], AF.Silu,
                                       bias=0.0, scale=1.0), ac_b)

        def fb_vector(v, e):
            bp2, bac, bab, bpo = (FB[e]["pe_b"], FB[e]["ac_a"], FB[e]["ac_b"],
                                  FB[e]["po_a"])

            def pool_part(mk2):
                m2, k2 = divmod(mk2, 2)
                j2, tc2 = divmod(m2, 2)
                _w(v, ac_b, bab + mk2 + 1)
                _w(v, po_s, _inc(v.tensor_tensor(
                    x3sb[k2][:, :], x3sb[k2][:, :],
                    rwb[k2][:, tc2 * 512:(tc2 + 1) * 512], ALU.mult), po_s))
                _inc(v.tensor_tensor(
                    x2v(k2)[:, j2 * TOK + tc2 * 512:j2 * TOK + (tc2 + 1) * 512],
                    s1sb[k2][:, :], x3sb[k2][:, :], ALU.mult), po_a)

            for m in range(SUBS * 2):
                j, tc = divmod(m, 2)
                for k in range(2):
                    mk = m * 2 + k
                    _w(v, pe_b, bp2 + mk + 1)
                    _w(v, ac_a, bac + m + 1)
                    _w(v, ac_b, bab + (mk - 2) + 1)
                    v.tensor_tensor(x1sb[k][:, :], pfa[4 + k][:, :],
                                    b1sb[m % 2][:, :], ALU.add)
                    if mk >= 2:
                        pool_part(mk - 2)
                    _w(v, po_a, bpo + (mk - 2) + 1)
                    _inc(v.tensor_tensor(x3sb[k][:, :], pfa[6 + k][:, :],
                                         b3sb[m % 2][:, :], ALU.add), ve_a)
            pool_part(14)
            pool_part(15)

        FD_ORDER = [(0, 0, 0), (0, 1, 0), (1, 0, 0), (1, 1, 0),
                    (0, 0, 1), (0, 1, 1), (1, 0, 1), (1, 1, 1)]

        def fd_tensor(te, e):
            dva, dvb, dvc, dpo = (FD[e]["ve_a"], FD[e]["ve_b"], FD[e]["ve_c"],
                                  FD[e]["po_a"])
            bpo = dpo - 16
            last_po = -1
            for idx, (k, q, tc) in enumerate(FD_ORDER):
                need = bpo + (6 + tc) * 2 + k + 1
                if need > last_po:
                    _w(te, po_a, need)
                    last_po = need
                for j in range(SUBS):
                    ins = te.matmul(
                        pfa[idx][:, :],
                        a2te[e % 2][:, j * ER + q * 128:j * ER + q * 128 + 128],
                        x2v(k)[:, j * TOK + tc * 512:j * TOK + (tc + 1) * 512],
                        start=(j == 0), stop=(j == SUBS - 1))
                _inc(ins, pe_a)
            for gidx in range(32):
                h, tc = divmod(gidx, 2)
                if gidx == 0:
                    _w(te, ve_a, dva + 8)
                    _w(te, d_w2[(2 * e) % 3], w2_wait[2 * e])
                    _w(te, d_w2[(2 * e + 1) % 3], w2_wait[2 * e + 1])
                    if e == NFC - 1:
                        _w(te, ve_c, dvc + 2)
                        _w(te, d_b2, B2RS_N)
                _w(te, ve_b, dvb + (gidx - 4) + 1)
                notlast = (e < NFC - 1)
                for k in range(2):
                    for j in range(SUBS):
                        half = j // 2
                        ap_ = w2slot_ap[(2 * e + half) % 3]
                        ins = te.matmul(
                            pfa[gidx % 4][:, :],
                            ap_[:, (j % 2) * D + h * 128:(j % 2) * D + (h + 1) * 128],
                            x2v(k)[:, j * TOK + tc * 512:j * TOK + (tc + 1) * 512],
                            start=(k == 0 and j == 0),
                            stop=(notlast and k == 1 and j == SUBS - 1))
                if e == NFC - 1:
                    for k in range(2):
                        for q in range(2):
                            ins = te.matmul(
                                pfa[gidx % 4][:, :],
                                b2rs[:, q * D + h * 128:q * D + (h + 1) * 128],
                                m1b[k][:, q * TOK + tc * 512:q * TOK + (tc + 1) * 512],
                                start=False, stop=(k == 1 and q == 1))
                _inc(ins, pe_b)

        def fd_vector(v, e):
            dp1, dp2, dva = FD[e]["pe_a"], FD[e]["pe_b"], FD[e]["ve_a"]
            for idx, (k, q, tc) in enumerate(FD_ORDER):
                _w(v, pe_a, dp1 + idx + 1)
                dst = a2acc[k][:, q * TOK + tc * 512:q * TOK + (tc + 1) * 512]
                _inc(v.tensor_tensor(dst, pfa[idx][:, :], dst, ALU.add), ve_a)
            if e == NFC - 1:
                _w(v, ve_a, dva + 8)
                _inc(v.tensor_tensor(m1b[0][:, :], a2acc[0][:, :], mex[0][:, :],
                                     ALU.mult), ve_c)
                _inc(v.tensor_tensor(m1b[1][:, :], a2acc[1][:, :], mex[1][:, :],
                                     ALU.mult), ve_c)
            for gidx in range(32):
                h, tc = divmod(gidx, 2)
                _w(v, pe_b, dp2 + gidx + 1)
                dst = sumT[:, h * TOK + tc * 512:h * TOK + (tc + 1) * 512]
                _inc(v.tensor_tensor(dst, pfa[gidx % 4][:, :], dst, ALU.add), ve_b)

        for e0 in range(0, NFC_RUN, 2):
            es = [e for e in (e0, e0 + 1) if e < NFC_RUN]
            with nc.Block(f"f{e0}") as blk:
                @blk.gpsimd
                def _(g, es=es):
                    for e in es:
                        fb_gpsimd(g, e)

                @blk.tensor
                def _(te, es=es):
                    for e in es:
                        fb_tensor(te, e)
                        if EMIT_FD:
                            fd_tensor(te, e)

                @blk.scalar
                def _(sc, es=es):
                    for e in es:
                        fb_scalar(sc, e)

                @blk.vector
                def _(v, es=es):
                    for e in es:
                        fb_vector(v, e)
                        if EMIT_FD:
                            fd_vector(v, e)

        f_ctx.close()
        ffn_ctx.close()

        emit_final()
        nc._dbg = dict(out=outd, xnTo=xnTo, big0=big0, sumT=sumT, xkv=xkv)
    return nc


def _host_arrays(inputs):
    f32 = np.float32
    g = {}
    g["wq"] = np.ascontiguousarray(inputs["L0_attn_Wq"], f32)
    g["wk"] = np.ascontiguousarray(inputs["L0_attn_Wk"], f32)
    g["wv"] = np.ascontiguousarray(inputs["L0_attn_Wv"], f32)
    g["wo"] = np.ascontiguousarray(inputs["L0_attn_Wo"], f32)
    g["w1t"] = np.ascontiguousarray(inputs["L0_ffn_W1"].T, f32)
    g["w3t"] = np.ascontiguousarray(inputs["L0_ffn_W3"].T, f32)
    g["w2t"] = np.ascontiguousarray(inputs["L0_ffn_W2"].T, f32)
    g["a1t"] = np.ascontiguousarray(inputs["L0_ffn_A1"].reshape(ER, D).T, f32)
    g["a3t"] = np.ascontiguousarray(inputs["L0_ffn_A3"].reshape(ER, D).T, f32)
    g["a2t"] = np.ascontiguousarray(inputs["L0_ffn_A2"].reshape(ER, FF).T, f32)
    g["b1r"] = np.ascontiguousarray(inputs["L0_ffn_B1"].transpose(0, 2, 1).reshape(ER, FF), f32)
    g["b3r"] = np.ascontiguousarray(inputs["L0_ffn_B3"].transpose(0, 2, 1).reshape(ER, FF), f32)
    g["b2r"] = np.ascontiguousarray(inputs["L0_ffn_B2"].transpose(0, 2, 1).reshape(ER, D), f32)
    wgT = np.ascontiguousarray(inputs["L0_ffn_Wg"].T, f32)
    g["wgt"] = np.ascontiguousarray(
        wgT.reshape(HT, 128, 8).transpose(1, 0, 2).reshape(128, 128), f32)
    bq = np.asarray(inputs["L0_attn_bq"], f32)
    bk = np.asarray(inputs["L0_attn_bk"], f32)
    bv = np.asarray(inputs["L0_attn_bv"], f32)
    bo = np.asarray(inputs["L0_attn_bo"], f32)
    g["bqb"] = np.ascontiguousarray(np.tile(bq[None, :], (128, 1))).astype(BF)
    g["bkb"] = np.ascontiguousarray(np.tile(bk[None, :], (128, 1))).astype(BF)
    g["bvb"] = np.ascontiguousarray(np.tile(bv[None, :], (128, 1))).astype(BF)
    g["bot"] = np.ascontiguousarray(bo.reshape(16, 128).T, f32)
    e8 = np.zeros((8, ER), BF)
    for e in range(8):
        e8[e, e * 32:(e + 1) * 32] = 1
    g["e8"] = e8
    g["identb"] = np.eye(128, dtype=BF)
    g["identf"] = np.eye(128, dtype=f32)
    g["ones1"] = np.ones((1, 128), BF)
    return g


def _pack_cs(c, s, n):
    out = np.empty((128, n * 320), np.float32)
    for t in range(n):
        cc = c[:, t * 64:(t + 1) * 64]
        ss = s[:, t * 64:(t + 1) * 64]
        out[:, t * 320:(t + 1) * 320] = np.concatenate([cc, ss, cc, ss, cc], 1)
    return out


def _rope_tables(h):
    f32 = np.float32
    inv = 1.0 / (10000.0 ** (np.arange(64, dtype=np.float64) * 2.0 / 128.0))
    pos_own = h * TOK + np.arange(TOK, dtype=np.float64)
    ang_q = pos_own[:, None] * inv[None, :]
    cq = np.cos(ang_q).astype(f32) * np.float32(SCALE)
    sq = np.sin(ang_q).astype(f32) * np.float32(SCALE)
    cq = cq.reshape(NTT, 128, 64).transpose(1, 0, 2).reshape(128, NTT * 64)
    sq = sq.reshape(NTT, 128, 64).transpose(1, 0, 2).reshape(128, NTT * 64)
    pos_kv = np.zeros(NKT * 128, np.float64)
    pos_kv[:1536] = h * TOK - 512 + np.arange(1536)
    pos_kv[1536:1540] = np.arange(4)
    ang_k = pos_kv[:, None] * inv[None, :]
    ck = np.cos(ang_k).astype(f32)
    sk = np.sin(ang_k).astype(f32)
    ck = ck.reshape(NKT, 128, 64).transpose(1, 0, 2).reshape(128, NKT * 64)
    sk = sk.reshape(NKT, 128, 64).transpose(1, 0, 2).reshape(128, NKT * 64)
    return (np.ascontiguousarray(_pack_cs(cq, sq, NTT)),
            np.ascontiguousarray(_pack_cs(ck, sk, NKT)))


def _mask_for(h):
    m = np.full((NTT, SLOTS, 128, 128), MASKNEG, np.float32)
    for qb in range(NTT):
        i = (h * TOK + qb * 128 + np.arange(128))[:, None]
        for j in range(SLOTS):
            if j < 5:
                s = (qb + j) * 128 + np.arange(128)[None, :]
                gk = h * TOK - 512 + s
                valid = np.ones_like(gk, bool) if h == 1 else (s >= 512)
                allow = valid & (gk <= i) & ((gk >= i - 512) | (gk < 4))
            else:
                c = np.arange(128)[None, :]
                if h == 0 and qb <= 4:
                    allow = np.zeros((128, 128), bool)
                else:
                    allow = (c < 4) & (c <= i)
            m[qb, j][allow] = 0.0
    return np.ascontiguousarray(
        m.transpose(2, 0, 1, 3).reshape(128, NTT * SLOTS * 128)).astype(BF)


def kernel(**inputs):
    if "nc" not in _CACHE:
        _CACHE["nc"] = build_program()
    nc = _CACHE["nc"]

    x = np.asarray(inputs["L0_attn_x"], np.float32)
    g = _host_arrays(inputs)
    per_h = {}
    for h in range(2):
        cq3, ck3 = _rope_tables(h)
        per_h[h] = dict(cqt=cq3, ckt=ck3, maskb=_mask_for(h))

    in_maps = []
    for core in range(8):
        b, h = divmod(core, 2)
        halo = x[b, 512:1024] if h == 1 else np.zeros((512, D), np.float32)
        xkv = np.concatenate(
            [halo, x[b, h * TOK:(h + 1) * TOK], x[b, 0:4],
             np.zeros((124, D), np.float32)], axis=0)
        m = dict(g)
        m.update(per_h[h])
        m["xkv"] = np.ascontiguousarray(xkv)
        in_maps.append(m)

    res = run_bass_kernel_spmd(nc, in_maps, core_ids=list(range(8)))
    out = np.empty((4, D, D), np.float32)
    for core in range(8):
        b, h = divmod(core, 2)
        out[b, h * TOK:(h + 1) * TOK] = res.results[core]["out"]
    return out


# version 17 (final)
# speedup vs baseline: 1.4579x; 1.0116x over previous
import math
import numpy as np
import ml_dtypes
from contextlib import ExitStack

import concourse.bass as bass
import concourse.mybir as mybir
from concourse.bass_utils import run_bass_kernel_spmd

F32 = mybir.dt.float32
BF16 = mybir.dt.bfloat16
AF = mybir.ActivationFunctionType
ALU = mybir.AluOpType
AX = mybir.AxisListType
BF = ml_dtypes.bfloat16

D = 2048
HT = 16
TOK = 1024
NTT = 8
KV = 1664
NKT = 13
NH = 16
NG = 4
DH = 128
FF = 7168
NFC = 14
SUBS = 4
ER = 256
SLOTS = 6
EPS = 1e-5
SCALE = 1.0 / math.sqrt(DH)
MASKNEG = -1.0e9

_CACHE = {}


class _Sem:
    __slots__ = ("h", "n")

    def __init__(self, ctx, nc, name):
        self.h = ctx.enter_context(nc.semaphore(name))
        self.n = 0


def _w(eng, sem, v):
    if v > 0:
        eng.wait_ge(sem.h, v)


def _inc(ins, sem, d=1):
    ins.then_inc(sem.h, d)
    sem.n += d
    return sem.n


def build_program(upto=99):
    nc = bass.Bass("TRN2", target_bir_lowering=False)

    xkv = nc.dram_tensor("xkv", [KV, D], F32, kind="ExternalInput")
    maskd = nc.dram_tensor("maskb", [128, NTT * SLOTS * 128], BF16, kind="ExternalInput")
    cqtd = nc.dram_tensor("cqt", [128, NTT * 320], F32, kind="ExternalInput")
    cktd = nc.dram_tensor("ckt", [128, NKT * 320], F32, kind="ExternalInput")
    wqd = nc.dram_tensor("wq", [D, D], F32, kind="ExternalInput")
    wkd = nc.dram_tensor("wk", [D, 512], F32, kind="ExternalInput")
    wvd = nc.dram_tensor("wv", [D, 512], F32, kind="ExternalInput")
    wod = nc.dram_tensor("wo", [D, D], F32, kind="ExternalInput")
    bqbd = nc.dram_tensor("bqb", [128, D], BF16, kind="ExternalInput")
    bkbd = nc.dram_tensor("bkb", [128, 512], BF16, kind="ExternalInput")
    bvbd = nc.dram_tensor("bvb", [128, 512], BF16, kind="ExternalInput")
    botd = nc.dram_tensor("bot", [128, 16], F32, kind="ExternalInput")
    wgtd = nc.dram_tensor("wgt", [128, 128], F32, kind="ExternalInput")
    a1td = nc.dram_tensor("a1t", [D, ER], F32, kind="ExternalInput")
    a3td = nc.dram_tensor("a3t", [D, ER], F32, kind="ExternalInput")
    w1td = nc.dram_tensor("w1t", [D, FF], F32, kind="ExternalInput")
    w3td = nc.dram_tensor("w3t", [D, FF], F32, kind="ExternalInput")
    w2td = nc.dram_tensor("w2t", [FF, D], F32, kind="ExternalInput")
    a2td = nc.dram_tensor("a2t", [FF, ER], F32, kind="ExternalInput")
    b1rd = nc.dram_tensor("b1r", [ER, FF], F32, kind="ExternalInput")
    b3rd = nc.dram_tensor("b3r", [ER, FF], F32, kind="ExternalInput")
    b2rd = nc.dram_tensor("b2r", [ER, D], F32, kind="ExternalInput")
    e8d = nc.dram_tensor("e8", [8, ER], BF16, kind="ExternalInput")
    identbd = nc.dram_tensor("identb", [128, 128], BF16, kind="ExternalInput")
    identfd = nc.dram_tensor("identf", [128, 128], F32, kind="ExternalInput")
    ones1d = nc.dram_tensor("ones1", [1, 128], BF16, kind="ExternalInput")
    outd = nc.dram_tensor("out", [TOK, D], F32, kind="ExternalOutput")

    ctx = ExitStack()
    with ctx:
        d_const = _Sem(ctx, nc, "d_const")
        d_x = [_Sem(ctx, nc, f"d_x{i}") for i in range(2)]
        d_wr = [_Sem(ctx, nc, f"d_wr{i}") for i in range(3)]
        d_w1 = [_Sem(ctx, nc, f"d_w1{i}") for i in range(2)]
        d_w3 = [_Sem(ctx, nc, f"d_w3{i}") for i in range(2)]
        d_w2 = [_Sem(ctx, nc, f"d_w2{i}") for i in range(3)]
        d_be = [_Sem(ctx, nc, f"d_be{i}") for i in range(2)]
        d_sm = _Sem(ctx, nc, "d_sm")
        d_b2 = _Sem(ctx, nc, "d_b2")
        d_out = _Sem(ctx, nc, "d_out")
        d_out2 = _Sem(ctx, nc, "d_out2")
        pe_a = _Sem(ctx, nc, "pe_a")
        pe_b = _Sem(ctx, nc, "pe_b")
        pe_c = _Sem(ctx, nc, "pe_c")
        ve_a = _Sem(ctx, nc, "ve_a")
        ve_b = _Sem(ctx, nc, "ve_b")
        ve_c = _Sem(ctx, nc, "ve_c")
        ac_a = _Sem(ctx, nc, "ac_a")
        ac_b = _Sem(ctx, nc, "ac_b")
        ac_c = _Sem(ctx, nc, "ac_c")
        po_a = _Sem(ctx, nc, "po_a")
        ve_s = _Sem(ctx, nc, "ve_s")
        po_s = _Sem(ctx, nc, "po_s")
        SEMS = [d_const, d_sm, d_b2, d_out, d_out2, pe_a, pe_b, pe_c, ve_a, ve_b,
                ve_c, ac_a, ac_b, ac_c, po_a, ve_s, po_s] + d_x + d_wr + d_w1 + d_w3 + d_w2 + d_be

        xnTo = ctx.enter_context(nc.sbuf_tensor("xnTo", [128, HT * TOK], BF16))
        big0 = ctx.enter_context(nc.sbuf_tensor("big0", [128, HT * TOK], BF16))
        sumT = ctx.enter_context(nc.sbuf_tensor("sumT", [128, HT * TOK], BF16))
        identb = ctx.enter_context(nc.sbuf_tensor("identb_s", [128, 128], BF16))
        identf = ctx.enter_context(nc.sbuf_tensor("identf_s", [128, 128], F32))
        ones1 = ctx.enter_context(nc.sbuf_tensor("ones1_s", [1, 128], BF16))
        e8s = ctx.enter_context(nc.sbuf_tensor("e8_s", [8, ER], BF16))
        bots = ctx.enter_context(nc.sbuf_tensor("bot_s", [128, 16], F32))

        def emit_final():
            fin_ctx = ExitStack()
            xr = [fin_ctx.enter_context(nc.sbuf_tensor(f"xr{i}", [128, D], F32)) for i in range(2)]
            osb = [fin_ctx.enter_context(nc.sbuf_tensor(f"osb{i}", [128, D], F32)) for i in range(2)]
            pfin = [fin_ctx.enter_context(nc.psum_tensor(f"pfin{i}", [128, D], BF16))
                    for i in range(2)]
            fp, fv = pe_a.n, ve_a.n
            d_st = [d_out, d_out2]
            xr_wait = []
            st_wait = []
            fin_gate = [(sm, sm.n) for sm in SEMS]
            with nc.Block("final") as blk:
                @blk.sync
                def _(s):
                    for sm, n in fin_gate:
                        _w(s, sm, n)
                    for tt in range(NTT + 1):
                        if tt < NTT:
                            sl = tt % 2
                            _w(s, ve_a, fv + (tt - 2) + 1)
                            s.dma_start(xr[sl][:, :],
                                        xkv.ap()[512 + tt * 128:512 + (tt + 1) * 128, :])\
                                .then_inc(d_x[sl].h, 16)
                            d_x[sl].n += 16
                            xr_wait.append(d_x[sl].n)
                        if tt >= 1:
                            osl = (tt - 1) % 2
                            _w(s, ve_a, fv + tt)
                            s.dma_start(outd.ap()[(tt - 1) * 128:tt * 128, :],
                                        osb[osl][:, :]).then_inc(d_st[osl].h, 16)
                            d_st[osl].n += 16
                            st_wait.append(d_st[osl].n)
                    s.wait_ge(d_out.h, d_out.n)
                    s.wait_ge(d_out2.h, d_out2.n)

                @blk.tensor
                def _(te):
                    for sm, n in fin_gate:
                        if sm not in (d_out,):
                            _w(te, sm, n)
                    for tt in range(NTT):
                        _w(te, ve_a, fv + (tt - 2) + 1)
                        for hh in range(HT):
                            ins = te.transpose(pfin[tt % 2][:, hh * 128:(hh + 1) * 128],
                                               sumT[:, hh * TOK + tt * 128:hh * TOK + (tt + 1) * 128],
                                               identb[:, :])
                        _inc(ins, pe_a)

                @blk.vector
                def _(v):
                    for tt in range(NTT):
                        sl = tt % 2
                        _w(v, pe_a, fp + tt + 1)
                        _w(v, d_x[sl], xr_wait[tt])
                        if tt >= 2:
                            _w(v, d_st[sl], st_wait[tt - 2])
                        _inc(v.tensor_add(osb[sl][:, :], pfin[sl][:, :], xr[sl][:, :]), ve_a)

            fin_ctx.close()


        def xnTo_t(h):
            return xnTo[:, h * TOK:(h + 1) * TOK]

        def xnTo_s(h, tc):
            return xnTo[:, h * TOK + tc * 512:h * TOK + (tc + 1) * 512]

        def x2v(k):
            return big0[:, k * SUBS * TOK:(k + 1) * SUBS * TOK]

        w2slot_ap = [big0[:, 8192:12288], big0[:, 12288:16384], None]

        attn_ctx = ExitStack()
        KTs = attn_ctx.enter_context(nc.sbuf_tensor("KTs", [128, NG * KV], BF16))
        Vs = attn_ctx.enter_context(nc.sbuf_tensor("Vs", [128, NKT * 512], BF16))

        lnqkv_ctx = ExitStack()
        xnTh = lnqkv_ctx.enter_context(nc.sbuf_tensor("xnTh", [128, HT * 640], BF16))

        def xnTh_t(h):
            return xnTh[:, h * 640:(h + 1) * 640]

        def xn_kv(h, kt):
            if kt < 4:
                return xnTh_t(h)[:, kt * 128:(kt + 1) * 128]
            if kt < 12:
                return xnTo_t(h)[:, (kt - 4) * 128:(kt - 3) * 128]
            return xnTh_t(h)[:, 512:640]

        with nc.Block("consts") as blk:
            @blk.sync
            def _(s):
                for dst, src in ((identb, identbd), (identf, identfd), (ones1, ones1d),
                                 (e8s, e8d), (bots, botd)):
                    s.dma_start(dst[:, :], src.ap()).then_inc(d_const.h, 16)
                    d_const.n += 16
        CONST_N = d_const.n

        ln_ctx = ExitStack()
        xin = [ln_ctx.enter_context(nc.sbuf_tensor(f"xin{i}", [128, D], F32)) for i in range(2)]
        sqs = ln_ctx.enter_context(nc.sbuf_tensor("sqs", [128, D], F32))
        sms = ln_ctx.enter_context(nc.sbuf_tensor("sms", [128, D], F32))
        xnt = [ln_ctx.enter_context(nc.sbuf_tensor(f"xnt{i}", [128, D], BF16)) for i in range(2)]
        lst = [ln_ctx.enter_context(nc.sbuf_tensor(f"lst{i}", [128, 8], F32)) for i in range(2)]
        pt_ln = [ln_ctx.enter_context(nc.psum_tensor(f"pt_ln{i}", [128, 2048], BF16))
                 for i in range(2)]

        lnv, lna, lnp, lne = ve_a.n, ac_a.n, pe_a.n, ve_b.n

        def _ln_evac(v, t):
            _w(v, pe_a, lnp + t + 1)
            if t < 4:
                dst = xnTh[:, :].rearrange("p (h c) -> p h c", c=640)[:, :, t * 128:(t + 1) * 128]
            elif t < 12:
                dst = xnTo[:, :].rearrange("p (h c) -> p h c", c=TOK)[:, :, (t - 4) * 128:(t - 3) * 128]
            else:
                dst = xnTh[:, :].rearrange("p (h c) -> p h c", c=640)[:, :, 512:640]
            src = pt_ln[t % 2][:, :].rearrange("p (h c) -> p h c", c=128)
            _inc(v.tensor_copy(dst, src), ve_b)

        with nc.Block("ln") as blk:
            @blk.sync
            def _(s):
                for t in range(NKT):
                    sl = t % 2
                    _w(s, ac_a, lna + (t - 2) * 2 + 2)
                    s.dma_start(xin[sl][:, :], xkv.ap()[t * 128:(t + 1) * 128, :])\
                        .then_inc(d_x[sl].h, 16)
                    d_x[sl].n += 16

            lnc = ac_c.n

            @blk.vector
            def _(v):
                for t in range(NKT):
                    sl = t % 2
                    st = lst[sl]
                    _w(v, d_x[sl], 16 * (t // 2 + 1))
                    _inc(v.tensor_scalar(sms[:, :], xin[sl][:, :], 1.0, 0.0, ALU.mult,
                                         ALU.add, accum_out=st[:, 0:1]), ve_s)
                    _w(v, ve_s, ve_s.n)
                    v.tensor_scalar(st[:, 2:3], st[:, 0:1], 1.0 / D, None, ALU.mult)
                    _w(v, ac_c, lnc + t + 1)
                    _w(v, ve_s, _inc(v.tensor_scalar(st[:, 3:4], st[:, 1:2], 1.0 / D,
                                                     EPS, ALU.mult, ALU.add), ve_s))
                    _w(v, ve_s, _inc(v.scalar_tensor_tensor(st[:, 4:5], st[:, 2:3], 1.0,
                                                            st[:, 2:3], ALU.mult,
                                                            ALU.mult), ve_s))
                    _inc(v.scalar_tensor_tensor(st[:, 5:6], st[:, 4:5], -1.0, st[:, 3:4],
                                                ALU.mult, ALU.add), ve_a)
                    _w(v, ac_a, lna + t * 2 + 1)
                    _w(v, ve_s, _inc(v.reciprocal(st[:, 6:7], st[:, 6:7]), ve_s))
                    _inc(v.scalar_tensor_tensor(st[:, 7:8], st[:, 2:3], -1.0, st[:, 6:7],
                                                ALU.mult, ALU.mult), ve_a)
                    if t >= 1:
                        _ln_evac(v, t - 1)
                _ln_evac(v, NKT - 1)

            @blk.scalar
            def _(sc):
                for t in range(NKT):
                    sl = t % 2
                    st = lst[sl]
                    _w(sc, d_x[sl], 16 * (t // 2 + 1))
                    _inc(sc.activation(sqs[:, :], xin[sl][:, :], AF.Square,
                                       bias=0.0, scale=1.0, accum_out=st[:, 1:2]), ac_c)
                    _w(sc, ve_a, lnv + t * 2 + 1)
                    _inc(sc.activation(st[:, 6:7], st[:, 5:6], AF.Sqrt, bias=0.0, scale=1.0),
                         ac_a)
                    _w(sc, ve_a, lnv + t * 2 + 2)
                    _inc(sc.activation(xnt[sl][:, :], xin[sl][:, :], AF.Identity,
                                       bias=st[:, 7:8], scale=st[:, 6:7]), ac_a)

            @blk.tensor
            def _(te):
                for t in range(NKT):
                    sl = t % 2
                    if t == 0:
                        _w(te, d_const, CONST_N)
                    _w(te, ac_a, lna + t * 2 + 2)
                    _w(te, ve_b, lne + (t - 2) + 1)
                    for hh in range(HT):
                        ins = te.transpose(pt_ln[sl][:, hh * 128:(hh + 1) * 128],
                                           xnt[sl][:, hh * 128:(hh + 1) * 128], identb[:, :])
                    _inc(ins, pe_a)

        LN_END = [(pe_a, pe_a.n), (ve_a, ve_a.n), (ve_b, ve_b.n), (ac_a, ac_a.n)]
        ln_ctx.close()
        if upto <= 1:
            qkv_ctx = None
            lnqkv_ctx.close(); attn_ctx.close()
            emit_final()
            nc._dbg = dict(out=outd, xnTo=xnTo, big0=big0, sumT=sumT, xkv=xkv)
            return nc

        qkv_ctx = ExitStack()
        wr = [qkv_ctx.enter_context(nc.sbuf_tensor(f"wr{i}", [128, HT * 256], BF16))
              for i in range(3)]
        cqs = qkv_ctx.enter_context(nc.sbuf_tensor("cqs", [128, NTT * 320], F32))
        cks = qkv_ctx.enter_context(nc.sbuf_tensor("cks", [128, NKT * 320], F32))
        bqbs = qkv_ctx.enter_context(nc.sbuf_tensor("bqbs", [128, D], BF16))
        bkbs = qkv_ctx.enter_context(nc.sbuf_tensor("bkbs", [128, 512], BF16))
        bvbs = qkv_ctx.enter_context(nc.sbuf_tensor("bvbs", [128, 512], BF16))
        qscr = qkv_ctx.enter_context(nc.sbuf_tensor("qscr", [128, 256], F32))
        rop = qkv_ctx.enter_context(nc.sbuf_tensor("rop", [128, 512], F32))
        qrot = [qkv_ctx.enter_context(nc.sbuf_tensor(f"qrot{i}", [128, 256], BF16))
                for i in range(2)]
        pq = [qkv_ctx.enter_context(nc.psum_tensor(f"pq{i}", [128, 256], F32))
              for i in range(4)]
        ptq = [qkv_ctx.enter_context(nc.psum_tensor(f"ptq{i}", [128, 256], BF16))
               for i in range(2)]

        qkv_items = []
        for c in range(10):
            if c < 8:
                for tt in range(NTT):
                    qkv_items.append(("q", c, tt))
            else:
                for kt in range(NKT):
                    qkv_items.append(("k", c - 8, kt))
        for c in range(2):
            for kt in range(NKT):
                qkv_items.append(("v", c, kt))
        NQKV = len(qkv_items)
        thru = []
        acc = 0
        for c in range(12):
            acc += NTT if c < 8 else NKT
            thru.append(acc)

        qp, qv, qv2, qv3, qp2 = pe_a.n, ve_a.n, ve_b.n, ve_c.n, pe_b.n
        qd = d_const.n
        with nc.Block("qkv") as blk:
            wr_wait = []
            rope_ves = {}

            @blk.gpsimd
            def _(g):
                for c in range(12):
                    sl = c % 3
                    if c == 0:
                        for sm, n in LN_END:
                            _w(g, sm, n)
                    if c >= 3:
                        _w(g, pe_a, qp + thru[c - 3])
                    if c < 8:
                        src = wqd.ap().rearrange("(h p) q -> p h q", p=128)[:, :, c * 256:(c + 1) * 256]
                    elif c < 10:
                        src = wkd.ap().rearrange("(h p) q -> p h q", p=128)[:, :, (c - 8) * 256:(c - 7) * 256]
                    else:
                        src = wvd.ap().rearrange("(h p) q -> p h q", p=128)[:, :, (c - 10) * 256:(c - 9) * 256]
                    g.dma_start(wr[sl][:, :].rearrange("p (h q) -> p h q", q=256), src)\
                        .then_inc(d_wr[sl].h, 16)
                    d_wr[sl].n += 16
                    wr_wait.append(d_wr[sl].n)

            @blk.sync
            def _(s):
                for sm, n in LN_END:
                    _w(s, sm, n)
                for dst, src in ((cqs, cqtd), (cks, cktd),
                                 (bqbs, bqbd), (bkbs, bkbd), (bvbs, bvbd)):
                    s.dma_start(dst[:, :], src.ap()).then_inc(d_const.h, 16)
                    d_const.n += 16

            def _qkv_tr(te, j):
                if qkv_items[j][0] == "v":
                    return
                _w(te, ve_c, qv3 + j + 1)
                _w(te, ve_b, qv2 + (j - 2) + 1)
                for u in range(2):
                    ins = te.transpose(ptq[j % 2][:, u * 128:(u + 1) * 128],
                                       qrot[j % 2][:, u * 128:(u + 1) * 128],
                                       identb[:, :])
                _inc(ins, pe_b)

            @blk.tensor
            def _(te):
                for i in range(NQKV + 2):
                    if i < NQKV:
                        kind, c, sub = qkv_items[i]
                        cg = {"q": c, "k": 8 + c, "v": 10 + c}[kind]
                        sl = cg % 3
                        if i == 0:
                            _w(te, ve_b, lne + NKT)
                        if sub == 0:
                            _w(te, d_wr[sl], wr_wait[cg])
                        _w(te, ve_a, qv + (i - 4) + 1)
                        ps = pq[i % 4]
                        for hh in range(HT):
                            lhsT = (xnTo_t(hh)[:, sub * 128:(sub + 1) * 128] if kind == "q"
                                    else xn_kv(hh, sub))
                            ins = te.matmul(ps[:, :], lhsT, wr[sl][:, hh * 256:(hh + 1) * 256],
                                            start=(hh == 0), stop=(hh == HT - 1))
                        _inc(ins, pe_a)
                    if i >= 2:
                        _qkv_tr(te, i - 2)

            @blk.vector
            def _(v):
                for i, (kind, c, sub) in enumerate(qkv_items):
                    if i == 0:
                        _w(v, d_const, qd + 5 * 16)
                    ps = pq[i % 4]
                    _w(v, pe_a, qp + i + 1)
                    if kind == "v":
                        _inc(v.tensor_tensor(
                            Vs[:, sub * 512 + c * 256:sub * 512 + (c + 1) * 256],
                            ps[:, :], bvbs[:, c * 256:(c + 1) * 256], ALU.add), ve_a)
                        continue
                    bias = (bqbs if kind == "q" else bkbs)[:, c * 256:(c + 1) * 256]
                    if i >= 1:
                        _w(v, ve_s, rope_ves[i - 1])
                        _w(v, ve_c, qv3 + i)
                    _inc(v.tensor_tensor(qscr[:, :], ps[:, :], bias, ALU.add), ve_a)
                    _w(v, ve_a, qv + i + 1)
                    tab = cqs if kind == "q" else cks
                    base = sub * 320
                    _w(v, pe_b, qp2 + (i - 2) + 1)
                    qr = qrot[i % 2]
                    v.tensor_tensor(rop[:, 0:256], qscr[:, :],
                                    tab[:, base:base + 256], ALU.mult)
                    ins = v.tensor_tensor(rop[:, 256:512], qscr[:, :],
                                          tab[:, base + 64:base + 320], ALU.mult)
                    rope_ves[i] = _inc(ins, ve_s)
                    _w(v, ve_s, rope_ves[i])
                    for hd in range(2):
                        o = hd * 128
                        v.tensor_sub(qr[:, o:o + 64], rop[:, o:o + 64],
                                     rop[:, o + 64:o + 128])
                        ins = v.tensor_add(qr[:, o + 64:o + 128], rop[:, 256 + o:256 + o + 64],
                                           rop[:, 256 + o + 64:256 + o + 128])
                    _inc(ins, ve_c)
                    _w(v, pe_b, qp2 + i + 1)
                    src3 = ptq[i % 2][:, :].rearrange("p (u c2) -> p u c2", c2=128)
                    if kind == "q":
                        dst3 = big0[:, :].rearrange("p (hh t) -> p hh t", t=TOK)[
                            :, 2 * c:2 * c + 2, sub * 128:(sub + 1) * 128]
                    else:
                        dst3 = KTs[:, :].rearrange("p (gg t) -> p gg t", t=KV)[
                            :, 2 * c:2 * c + 2, sub * 128:(sub + 1) * 128]
                    _inc(v.tensor_copy(dst3, src3), ve_b)

        qkv_ctx.close()
        lnqkv_ctx.close()
        if upto <= 2:
            attn_ctx.close()
            emit_final()
            nc._dbg = dict(out=outd, xnTo=xnTo, big0=big0, sumT=sumT, xkv=xkv)
            return nc

        sc_ctx = ExitStack()
        attnT = sc_ctx.enter_context(nc.sbuf_tensor("attnT", [128, HT * TOK], BF16))
        sm_ctx = ExitStack()
        masks = sm_ctx.enter_context(nc.sbuf_tensor("masks", [128, NTT * SLOTS * 128], BF16))
        ssb = [sm_ctx.enter_context(nc.sbuf_tensor(f"ssb{i}", [128, SLOTS * 128], F32))
               for i in range(2)]
        pnb = [sm_ctx.enter_context(nc.sbuf_tensor(f"pnb{i}", [128, SLOTS * 128], BF16))
               for i in range(4)]
        pnT = [sm_ctx.enter_context(nc.sbuf_tensor(f"pnT{i}", [128, SLOTS * 128], BF16))
               for i in range(4)]
        nmz = sm_ctx.enter_context(nc.sbuf_tensor("nmz", [128, 12], F32))
        psc = [sm_ctx.enter_context(nc.psum_tensor(f"psc{i}", [128, SLOTS * 128], F32))
               for i in range(2)]
        ppn = [sm_ctx.enter_context(nc.psum_tensor(f"ppn{i}", [128, SLOTS * 128], BF16))
               for i in range(2)]
        pwv = [sm_ctx.enter_context(nc.psum_tensor(f"pwv{i}", [128, 512], F32))
               for i in range(2)]

        sp1, sp2, sp3 = pe_a.n, pe_b.n, pe_c.n
        sv1, sv2, sv3 = ve_a.n, ve_b.n, ve_c.n
        sa1, sa2, sa3 = ac_a.n, ac_b.n, ac_c.n
        sxd = d_x[0].n

        def slot_kt(qb, j):
            return qb + j if j < 5 else 12

        with nc.Block("scores") as blk:
            @blk.sync
            def _(s):
                _w(s, pe_a, sp1)
                _w(s, pe_b, sp2)
                _w(s, ve_b, sv2)
                s.dma_start(masks[:, :], maskd.ap()).then_inc(d_x[0].h, 16)
                d_x[0].n += 16

            def _sc_mm(te, i):
                hh, qb = divmod(i, NTT)
                g = hh // 4
                _w(te, ve_a, sv1 + (i - 2) + 1)
                qtile = big0[:, hh * TOK + qb * 128:hh * TOK + (qb + 1) * 128]
                base = g * KV
                te.matmul(psc[i % 2][:, 0:512], qtile,
                          KTs[:, base + qb * 128:base + qb * 128 + 512],
                          start=True, stop=True)
                te.matmul(psc[i % 2][:, 512:640], qtile,
                          KTs[:, base + (qb + 4) * 128:base + (qb + 5) * 128],
                          start=True, stop=True)
                ins = te.matmul(psc[i % 2][:, 640:768], qtile,
                                KTs[:, base + 12 * 128:base + 13 * 128],
                                start=True, stop=True)
                _inc(ins, pe_a)

            def _sc_tr(te, j):
                _w(te, ac_b, sa2 + j + 1)
                _w(te, ve_c, sv3 + (j - 2) + 1)
                for u in range(SLOTS):
                    ins = te.transpose(ppn[j % 2][:, u * 128:(u + 1) * 128],
                                       pnb[j % 4][:, u * 128:(u + 1) * 128],
                                       identb[:, :])
                _inc(ins, pe_b)

            def _sc_wv(te, w):
                hh, qb = divmod(w, NTT)
                g = hh // 4
                q4 = w // 4
                _w(te, ve_c, sv3 + w + 1)
                if w % 4 == 0:
                    _w(te, ac_c, sa3 + (q4 - 2) + 1)
                for j in range(SLOTS):
                    kt = slot_kt(qb, j)
                    ins = te.matmul(
                        pwv[q4 % 2][:, (w % 4) * 128:(w % 4 + 1) * 128],
                        Vs[:, kt * 512 + g * 128:kt * 512 + (g + 1) * 128],
                        pnT[w % 4][:, j * 128:(j + 1) * 128],
                        start=(j == 0), stop=(j == SLOTS - 1))
                _inc(ins, pe_c)

            @blk.tensor
            def _(te):
                _w(te, ve_a, sv1)
                _w(te, ve_b, sv2)
                for i in range(NH * NTT + 6):
                    if i < NH * NTT:
                        _sc_mm(te, i)
                    if i >= 4 and i - 4 < NH * NTT:
                        _sc_tr(te, i - 4)
                    if i >= 6:
                        _sc_wv(te, i - 6)

            @blk.vector
            def _(v):
                def _norm(j):
                    jl = j % 2
                    _w(v, ac_b, sa2 + (j - 2) + 1)
                    _w(v, ve_b, _inc(v.reciprocal(nmz[:, 4 + jl:5 + jl],
                                                  nmz[:, 2 + jl:3 + jl]), ve_b))
                    _w(v, pe_b, sp2 + (j - 4) + 1)
                    _inc(v.tensor_scalar(pnb[j % 4][:, :], pnb[j % 4][:, :],
                                         nmz[:, 4 + jl:5 + jl], None, ALU.mult), ac_b)

                for i in range(NH * NTT):
                    hh, qb = divmod(i, NTT)
                    sl = i % 2
                    if i == 0:
                        _w(v, d_x[0], sxd + 16)
                    _w(v, pe_a, sp1 + i + 1)
                    _w(v, ac_a, sa1 + (i - 2) + 1)
                    _w(v, ve_s, _inc(v.tensor_tensor(ssb[sl][:, :], psc[sl][:, :],
                                                     masks[:, qb * 768:(qb + 1) * 768],
                                                     ALU.add), ve_s))
                    _inc(v.reduce_max(nmz[:, sl:sl + 1], ssb[sl][:, :], axis=AX.X,
                                      negate=True), ve_a)
                    if i >= 2:
                        _norm(i - 2)
                _w(v, ac_a, sa1 + NH * NTT)
                _norm(NH * NTT - 2)
                _norm(NH * NTT - 1)

            @blk.scalar
            def _(sc):
                def _evac(g):
                    il = 4 * g + 3
                    ghh, gqb = divmod(il, NTT)
                    _w(sc, pe_c, sp3 + il + 1)
                    _inc(sc.activation(
                        attnT[:, ghh * TOK + (gqb - 3) * 128:ghh * TOK + (gqb + 1) * 128],
                        pwv[g % 2][:, :], AF.Identity, bias=0.0, scale=1.0), ac_c)

                for i in range(NH * NTT):
                    hh, qb = divmod(i, NTT)
                    sl = i % 2
                    _w(sc, ve_a, sv1 + i + 1)
                    _w(sc, ve_b, sv2 + (i - 2) + 1)
                    _w(sc, pe_b, sp2 + (i - 4) + 1)
                    _inc(sc.activation(pnb[i % 4][:, :], ssb[sl][:, :], AF.Exp,
                                       bias=nmz[:, sl:sl + 1], scale=1.0,
                                       accum_out=nmz[:, 2 + sl:3 + sl]), ac_a)
                    if i >= 2:
                        j = i - 2
                        _w(sc, pe_b, sp2 + j + 1)
                        _w(sc, pe_c, sp3 + (j - 4) + 1)
                        _inc(sc.activation(pnT[j % 4][:, :], ppn[j % 2][:, :],
                                           AF.Identity, bias=0.0, scale=1.0), ve_c)
                    if i % 4 == 3 and i >= 7:
                        _evac(i // 4 - 1)
                for j in (NH * NTT - 2, NH * NTT - 1):
                    _w(sc, pe_b, sp2 + j + 1)
                    _w(sc, pe_c, sp3 + (j - 4) + 1)
                    _inc(sc.activation(pnT[j % 4][:, :], ppn[j % 2][:, :],
                                       AF.Identity, bias=0.0, scale=1.0), ve_c)
                _evac(NH * NTT // 4 - 1)

        sm_ctx.close()
        if upto <= 3:
            sc_ctx.close(); attn_ctx.close()
            emit_final()
            nc._dbg = dict(out=outd, xnTo=xnTo, big0=big0, sumT=sumT, xkv=xkv, attnT=attnT)
            return nc

        wo_ctx = ExitStack()
        wow = [wo_ctx.enter_context(nc.sbuf_tensor(f"wow{i}", [128, HT * 256], BF16))
               for i in range(3)]
        pwo = [wo_ctx.enter_context(nc.psum_tensor(f"pwo{i}", [128, 512], F32))
               for i in range(4)]
        op, oa = pe_a.n, ac_a.n
        WO_GATE = [(pe_b, pe_b.n), (pe_c, pe_c.n), (ve_a, ve_a.n),
                   (ve_c, ve_c.n), (ac_b, ac_b.n), (ac_c, ac_c.n)]
        with nc.Block("wo") as blk:
            wo_wait = []

            @blk.gpsimd
            def _(g):
                for c in range(8):
                    sl = c % 3
                    if c == 0:
                        for sm, n in WO_GATE:
                            _w(g, sm, n)
                    if c >= 3:
                        _w(g, pe_a, op + (c - 3 + 1) * 4)
                    g.dma_start(
                        wow[sl][:, :].rearrange("p (h q) -> p h q", q=256),
                        wod.ap().rearrange("(h p) q -> p h q", p=128)[:, :, c * 256:(c + 1) * 256]
                    ).then_inc(d_wr[sl].h, 16)
                    d_wr[sl].n += 16
                    wo_wait.append(d_wr[sl].n)

            @blk.tensor
            def _(te):
                for m in range(32):
                    co, tc = divmod(m, 2)
                    c = co // 2
                    sl = c % 3
                    if m == 0:
                        for sm, n in WO_GATE:
                            _w(te, sm, n)
                    if m % 4 == 0:
                        _w(te, d_wr[sl], wo_wait[c])
                    _w(te, ac_a, oa + (m - 4) + 1)
                    for ci in range(HT):
                        ins = te.matmul(
                            pwo[m % 4][:, :],
                            wow[sl][:, ci * 256 + (co % 2) * 128:ci * 256 + (co % 2) * 128 + 128],
                            attnT[:, ci * TOK + tc * 512:ci * TOK + (tc + 1) * 512],
                            start=(ci == 0), stop=(ci == HT - 1))
                    _inc(ins, pe_a)

            @blk.scalar
            def _(sc):
                for m in range(32):
                    co, tc = divmod(m, 2)
                    _w(sc, pe_a, op + m + 1)
                    _inc(sc.activation(sumT[:, co * TOK + tc * 512:co * TOK + (tc + 1) * 512],
                                       pwo[m % 4][:, :], AF.Identity,
                                       bias=bots[:, co:co + 1], scale=1.0), ac_a)

        wo_ctx.close()
        sc_ctx.close()
        attn_ctx.close()
        if upto <= 4:
            emit_final()
            nc._dbg = dict(out=outd, xnTo=xnTo, big0=big0, sumT=sumT, xkv=xkv)
            return nc

        ffn_ctx = ExitStack()
        m1b = [ffn_ctx.enter_context(nc.sbuf_tensor(f"m1b{i}", [128, 2 * TOK], BF16))
               for i in range(2)]
        m3b = [ffn_ctx.enter_context(nc.sbuf_tensor(f"m3b{i}", [128, 2 * TOK], BF16))
               for i in range(2)]
        mex = [ffn_ctx.enter_context(nc.sbuf_tensor(f"mex{i}", [128, 2 * TOK], BF16))
               for i in range(2)]
        rwb = [ffn_ctx.enter_context(nc.sbuf_tensor(f"rwb{i}", [128, TOK], F32))
               for i in range(2)]
        a2acc = [ffn_ctx.enter_context(nc.sbuf_tensor(f"a2acc{i}", [128, 2 * TOK], BF16))
                 for i in range(2)]
        b2rs = ffn_ctx.enter_context(nc.sbuf_tensor("b2rs", [128, 2 * D], BF16))

        rt_ctx = ExitStack()
        wgts = rt_ctx.enter_context(nc.sbuf_tensor("wgts", [128, 128], BF16))
        a1ts = rt_ctx.enter_context(nc.sbuf_tensor("a1ts", [128, HT * ER], BF16))
        a3ts = rt_ctx.enter_context(nc.sbuf_tensor("a3ts", [128, HT * ER], BF16))
        lg = rt_ctx.enter_context(nc.sbuf_tensor("lg", [8, TOK], F32))
        lgt = rt_ctx.enter_context(nc.sbuf_tensor("lgt", [128, 64], F32))
        oh = [rt_ctx.enter_context(nc.sbuf_tensor(f"oh{i}", [128, 64], F32)) for i in range(2)]
        lg2 = rt_ctx.enter_context(nc.sbuf_tensor("lg2", [128, 8], F32))
        mcol = rt_ctx.enter_context(nc.sbuf_tensor("mcol", [128, 4], F32))
        dall = rt_ctx.enter_context(nc.sbuf_tensor("dall", [128, 8], F32))
        eall = rt_ctx.enter_context(nc.sbuf_tensor("eall", [128, 8], F32))
        rwa = [rt_ctx.enter_context(nc.sbuf_tensor(f"rwa{i}", [128, 8], F32)) for i in range(2)]
        ohT = [rt_ctx.enter_context(nc.sbuf_tensor(f"ohT{i}", [8, TOK], BF16)) for i in range(2)]
        rwT = [rt_ctx.enter_context(nc.sbuf_tensor(f"rwT{i}", [1, TOK], BF16)) for i in range(2)]
        pbig = [rt_ctx.enter_context(nc.psum_tensor(f"pbig{i}", [128, 512], F32))
                for i in range(4)]
        plog = rt_ctx.enter_context(nc.psum_tensor("plog", [8, TOK], F32))
        poh = rt_ctx.enter_context(nc.psum_tensor("poh", [8, TOK], F32))

        rp, rv, ra = pe_a.n, ve_a.n, ac_a.n
        rsm = d_sm.n
        with nc.Block("routing") as blk:
            @blk.gpsimd
            def _(g):
                _w(g, pe_a, rp)
                g.dma_start(wgts[:, :], wgtd.ap()).then_inc(d_sm.h, 16); d_sm.n += 16
                g.dma_start(a1ts[:, :].rearrange("p (h e) -> p h e", e=ER),
                            a1td.ap().rearrange("(h p) e -> p h e", p=128))\
                    .then_inc(d_sm.h, 16); d_sm.n += 16
                g.dma_start(a3ts[:, :].rearrange("p (h e) -> p h e", e=ER),
                            a3td.ap().rearrange("(h p) e -> p h e", p=128))\
                    .then_inc(d_sm.h, 16); d_sm.n += 16
                g.dma_start(b2rs[:, :].rearrange("p (q h) -> p q h", h=D),
                            b2rd.ap().rearrange("(q p) h -> p q h", p=128))\
                    .then_inc(d_b2.h, 16); d_b2.n += 16

            @blk.tensor
            def _(te):
                _w(te, d_sm, rsm + 48)
                _w(te, ac_a, ra)
                for tc in range(2):
                    for hh in range(HT):
                        ins = te.matmul(plog[:, tc * 512:(tc + 1) * 512],
                                        wgts[:, hh * 8:(hh + 1) * 8],
                                        xnTo_s(hh, tc), start=(hh == 0), stop=(hh == HT - 1))
                _inc(ins, pe_a)
                _w(te, ve_a, rv + 1)
                for tt in range(NTT):
                    ins = te.transpose(pbig[3][:, tt * 8:(tt + 1) * 8],
                                       lg[:, tt * 128:(tt + 1) * 128], identf[0:8, 0:8])
                _inc(ins, pe_a)
                _w(te, ve_a, rv + 10)
                for tt in range(NTT):
                    ins = te.transpose(poh[:, tt * 128:(tt + 1) * 128],
                                       oh[0][:, tt * 8:(tt + 1) * 8], identf[:, :])
                _inc(ins, pe_a)
                _w(te, ve_a, rv + 11)
                for tt in range(NTT):
                    pb = pbig[2 + tt // 4]
                    ins = te.transpose(pb[0:1, (tt % 4) * 128:(tt % 4 + 1) * 128],
                                       rwa[0][:, tt:tt + 1], identf[:, :])
                _inc(ins, pe_a)
                _w(te, ve_a, rv + 12)
                for tt in range(NTT):
                    ins = te.transpose(poh[:, tt * 128:(tt + 1) * 128],
                                       oh[1][:, tt * 8:(tt + 1) * 8], identf[:, :])
                _inc(ins, pe_a)
                _w(te, ve_a, rv + 13)
                for tt in range(NTT):
                    pb = pbig[2 + tt // 4]
                    ins = te.transpose(pb[0:1, (tt % 4) * 128:(tt % 4 + 1) * 128],
                                       rwa[1][:, tt:tt + 1], identf[:, :])
                _inc(ins, pe_a)
                for k in range(2):
                    _w(te, ve_a, rv + 15 + k)
                    for q in range(2):
                        for tc in range(2):
                            ins = te.matmul(pbig[q * 2 + tc][:, :],
                                            e8s[0:8, q * 128:(q + 1) * 128],
                                            ohT[k][0:8, tc * 512:(tc + 1) * 512],
                                            start=True, stop=True)
                    _inc(ins, pe_a)
                for k in range(2):
                    _w(te, ve_a, rv + 17)
                    for tc in range(2):
                        ins = te.matmul(pbig[k * 2 + tc][:, :], ones1[:, :],
                                        rwT[k][0:1, tc * 512:(tc + 1) * 512],
                                        start=True, stop=True)
                    _inc(ins, pe_a)
                _w(te, d_sm, rsm + 48)
                for si, src in enumerate((a1ts, a3ts)):
                    _w(te, ve_a, rv + 19 + si)
                    for q in range(2):
                        for tc in range(2):
                            for hh in range(HT):
                                ins = te.matmul(pbig[q * 2 + tc][:, :],
                                                src[:, hh * ER + q * 128:hh * ER + q * 128 + 128],
                                                xnTo_s(hh, tc),
                                                start=(hh == 0), stop=(hh == HT - 1))
                    _inc(ins, pe_a)

            @blk.vector
            def _(v):
                _w(v, pe_a, rp + 1)
                _inc(v.tensor_copy(lg[:, :], plog[:, :]), ve_a)
                _w(v, pe_a, rp + 2)
                _inc(v.tensor_copy(lgt[:, :], pbig[3][:, 0:64]), ve_a)
                _w(v, ve_a, rv + 2)
                for tt in range(NTT):
                    sl_ = lgt[:, tt * 8:(tt + 1) * 8]
                    if tt >= 1:
                        _w(v, ve_a, rv + 2 + tt)
                    _w(v, ve_s, _inc(v.reduce_max(mcol[:, 0:1], sl_, axis=AX.X), ve_s))
                    _w(v, ve_s, _inc(v.tensor_scalar(oh[0][:, tt * 8:(tt + 1) * 8], sl_,
                                                     mcol[:, 0:1], None, ALU.is_ge), ve_s))
                    _w(v, ve_s, _inc(v.scalar_tensor_tensor(lg2[:, :], oh[0][:, tt * 8:(tt + 1) * 8],
                                                            -1.0e30, sl_, ALU.mult, ALU.add), ve_s))
                    _w(v, ve_s, _inc(v.reduce_max(mcol[:, 1:2], lg2[:, :], axis=AX.X), ve_s))
                    v.tensor_scalar(oh[1][:, tt * 8:(tt + 1) * 8], lg2[:, :], mcol[:, 1:2],
                                    None, ALU.is_ge)
                    _inc(v.tensor_sub(dall[:, tt:tt + 1], mcol[:, 1:2], mcol[:, 0:1]),
                         ve_a)
                _w(v, ac_a, ra + 1)
                _w(v, ve_s, _inc(v.tensor_scalar(eall[:, :], eall[:, :], 1.0, None,
                                                 ALU.add), ve_s))
                _w(v, ve_s, _inc(v.reciprocal(rwa[0][:, :], eall[:, :]), ve_s))
                _inc(v.tensor_scalar(rwa[1][:, :], rwa[0][:, :], -1.0, 1.0,
                                     ALU.mult, ALU.add), ve_a)
                _w(v, pe_a, rp + 3)
                _inc(v.tensor_copy(ohT[0][0:8, :], poh[0:8, :]), ve_a)
                _w(v, pe_a, rp + 4)
                v.tensor_copy(rwT[0][0:1, 0:512], pbig[2][0:1, :])
                _inc(v.tensor_copy(rwT[0][0:1, 512:1024], pbig[3][0:1, :]), ve_a)
                _w(v, pe_a, rp + 5)
                _inc(v.tensor_copy(ohT[1][0:8, :], poh[0:8, :]), ve_a)
                _w(v, pe_a, rp + 6)
                v.tensor_copy(rwT[1][0:1, 0:512], pbig[2][0:1, :])
                _inc(v.tensor_copy(rwT[1][0:1, 512:1024], pbig[3][0:1, :]), ve_a)
                for k in range(2):
                    _w(v, pe_a, rp + 7 + k)
                    for q in range(2):
                        for tc in range(2):
                            ins = v.tensor_copy(
                                mex[k][:, q * TOK + tc * 512:q * TOK + (tc + 1) * 512],
                                pbig[q * 2 + tc][:, :])
                    _inc(ins, ve_a)
                for k in range(2):
                    _w(v, pe_a, rp + 9 + k)
                    for tc in range(2):
                        ins = v.tensor_copy(rwb[k][:, tc * 512:(tc + 1) * 512],
                                            pbig[k * 2 + tc][:, :])
                    _inc(ins, ve_a)
                _w(v, ve_a, rv + 17)
                for si, mb in enumerate((m1b, m3b)):
                    _w(v, pe_a, rp + 11 + si)
                    for k in range(2):
                        for q in range(2):
                            for tc in range(2):
                                sl_ = slice(q * TOK + tc * 512, q * TOK + (tc + 1) * 512)
                                ins = v.tensor_tensor(mb[k][:, sl_], pbig[q * 2 + tc][:, :],
                                                      mex[k][:, sl_], ALU.mult)
                    _inc(ins, ve_a)
                v.memset(a2acc[0][:, :], 0.0)
                v.memset(a2acc[1][:, :], 0.0)

            @blk.scalar
            def _(sc):
                _w(sc, ve_a, rv + 10)
                _inc(sc.activation(eall[:, :], dall[:, :], AF.Exp, bias=0.0, scale=1.0),
                     ac_a)

        B2RS_N = d_b2.n
        rt_ctx.close()
        if upto <= 5:
            ffn_ctx.close()
            emit_final()
            nc._dbg = dict(out=outd, xnTo=xnTo, big0=big0, sumT=sumT, xkv=xkv,
                           m1b=m1b, m3b=m3b, mex=mex, rwb=rwb)
            return nc

        f_ctx = ExitStack()
        w1b = [f_ctx.enter_context(nc.sbuf_tensor(f"w1b{i}", [128, HT * 128], BF16))
               for i in range(2)]
        w3b = [f_ctx.enter_context(nc.sbuf_tensor(f"w3b{i}", [128, HT * 128], BF16))
               for i in range(2)]
        b1re = [f_ctx.enter_context(nc.sbuf_tensor(f"b1re{i}", [128, 2 * 512], BF16))
                for i in range(2)]
        b3re = [f_ctx.enter_context(nc.sbuf_tensor(f"b3re{i}", [128, 2 * 512], BF16))
                for i in range(2)]
        a2te = [f_ctx.enter_context(nc.sbuf_tensor(f"a2te{i}", [128, SUBS * ER], BF16))
                for i in range(2)]
        w2x = f_ctx.enter_context(nc.sbuf_tensor("w2x", [128, 2 * D], BF16))
        w2slot_ap[2] = w2x[:, :]
        b1sb = [f_ctx.enter_context(nc.sbuf_tensor(f"b1sb{i}", [128, 512], F32))
                for i in range(2)]
        b3sb = [f_ctx.enter_context(nc.sbuf_tensor(f"b3sb{i}", [128, 512], F32))
                for i in range(2)]
        x1sb = [f_ctx.enter_context(nc.sbuf_tensor(f"x1sb{i}", [128, 512], F32))
                for i in range(2)]
        s1sb = [f_ctx.enter_context(nc.sbuf_tensor(f"s1sb{i}", [128, 512], F32))
                for i in range(2)]
        x3sb = [f_ctx.enter_context(nc.sbuf_tensor(f"x3sb{i}", [128, 512], F32))
                for i in range(2)]
        pfa = [f_ctx.enter_context(nc.psum_tensor(f"pfa{i}", [128, 512], F32))
               for i in range(8)]

        be_wait = {}
        w2_wait = {}

        w1_wait = {}
        w3_wait = {}

        def _ffn_w13(g, c):
            sl = c % 2
            g.dma_start(w1b[sl][:, :].rearrange("p (h f) -> p h f", f=128),
                        w1td.ap().rearrange("(h p) f -> p h f", p=128)[:, :, c * 128:(c + 1) * 128])\
                .then_inc(d_w1[sl].h, 16)
            d_w1[sl].n += 16
            w1_wait[c] = d_w1[sl].n
            g.dma_start(w3b[sl][:, :].rearrange("p (h f) -> p h f", f=128),
                        w3td.ap().rearrange("(h p) f -> p h f", p=128)[:, :, c * 128:(c + 1) * 128])\
                .then_inc(d_w3[sl].h, 16)
            d_w3[sl].n += 16
            w3_wait[c] = d_w3[sl].n

        def _ffn_bundle(g, e):
            sl = e % 2
            g.dma_start(b1re[sl][:, :].rearrange("p (q f) -> p q f", f=512),
                        b1rd.ap().rearrange("(q p) f -> p q f", p=128)[:, :, e * 512:(e + 1) * 512])\
                .then_inc(d_be[sl].h, 16)
            d_be[sl].n += 16
            g.dma_start(b3re[sl][:, :].rearrange("p (q f) -> p q f", f=512),
                        b3rd.ap().rearrange("(q p) f -> p q f", p=128)[:, :, e * 512:(e + 1) * 512])\
                .then_inc(d_be[sl].h, 16)
            d_be[sl].n += 16
            g.dma_start(a2te[sl][:, :].rearrange("p (j e2) -> p j e2", e2=ER),
                        a2td.ap().rearrange("(f p) e2 -> p f e2", p=128)[:, e * SUBS:(e + 1) * SUBS, :])\
                .then_inc(d_be[sl].h, 16)
            d_be[sl].n += 16
            be_wait[e] = d_be[sl].n

        def _ffn_w2(g, he):
            sl = he % 3
            g.dma_start(w2slot_ap[sl].rearrange("p (j h) -> p j h", h=D),
                        w2td.ap().rearrange("(f p) h -> p f h", p=128)[:, he * 2:(he + 1) * 2, :])\
                .then_inc(d_w2[sl].h, 16)
            d_w2[sl].n += 16
            w2_wait[he] = d_w2[sl].n


        NFC_RUN = NFC if upto > 7 else 1
        EMIT_FD = upto != 6

        FB, FD = [], []
        _cur = dict(pe_a=pe_a.n, pe_b=pe_b.n, ve_a=ve_a.n, ve_b=ve_b.n,
                    ve_c=ve_c.n, ac_a=ac_a.n, ac_b=ac_b.n, po_a=po_a.n)
        for e in range(NFC_RUN):
            FB.append(dict(_cur))
            _cur["pe_a"] += 8; _cur["pe_b"] += 16; _cur["ve_a"] += 16
            _cur["ac_a"] += 8; _cur["ac_b"] += 16; _cur["po_a"] += 16
            FD.append(dict(_cur))
            if EMIT_FD:
                _cur["pe_a"] += 8; _cur["pe_b"] += 32; _cur["ve_a"] += 8
                _cur["ve_b"] += 32
                if e == NFC - 1:
                    _cur["ve_c"] += 2

        def fb_gpsimd(g, e):
            bp1, bp2, bva = FB[e]["pe_a"], FB[e]["pe_b"], FB[e]["ve_a"]
            if e == 0:
                _w(g, pe_a, bp1)
                _w(g, ve_a, bva)
                _ffn_bundle(g, 0)
                _ffn_w13(g, 0)
                _ffn_w13(g, 1)
                _ffn_w2(g, 0)
                _ffn_w2(g, 1)
            if e > 0:
                _w(g, pe_b, bp2)
                _ffn_w2(g, 2 * e)
                _ffn_w2(g, 2 * e + 1)
            if e + 1 < NFC:
                _ffn_bundle(g, e + 1)
            _w(g, pe_a, bp1 + 2)
            _ffn_w13(g, e * SUBS + 2)
            _w(g, pe_a, bp1 + 4)
            _ffn_w13(g, e * SUBS + 3)
            if e + 1 < NFC:
                _w(g, pe_a, bp1 + 6)
                _ffn_w13(g, (e + 1) * SUBS)
                _w(g, pe_a, bp1 + 8)
                _ffn_w13(g, (e + 1) * SUBS + 1)

        def fb_tensor(te, e):
            bva, bac = FB[e]["ve_a"], FB[e]["ac_a"]
            for m in range(SUBS * 2):
                j, tc = divmod(m, 2)
                c = e * SUBS + j
                sl = c % 2
                if m == 0:
                    _w(te, ve_a, bva)
                    _w(te, ve_b, FB[e]["ve_b"])
                if tc == 0:
                    _w(te, d_w1[sl], w1_wait[c])
                    _w(te, d_w3[sl], w3_wait[c])
                    if j == 0:
                        _w(te, d_be[e % 2], be_wait[e])
                _w(te, ac_a, bac + (m - 2) + 1)
                for hh in range(HT):
                    te.matmul(pfa[m % 2][:, :], w1b[sl][:, hh * 128:(hh + 1) * 128],
                              xnTo_s(hh, tc), start=(hh == 0), stop=(hh == HT - 1))
                for hh in range(HT):
                    ins = te.matmul(pfa[2 + m % 2][:, :],
                                    w3b[sl][:, hh * 128:(hh + 1) * 128],
                                    xnTo_s(hh, tc), start=(hh == 0), stop=(hh == HT - 1))
                _inc(ins, pe_a)
                for k in range(2):
                    mk = m * 2 + k
                    _w(te, ve_a, bva + (mk - 2) + 1)
                    for q in range(2):
                        te.matmul(pfa[4 + k][:, :],
                                  b1re[e % 2][:, q * 512 + j * 128:q * 512 + (j + 1) * 128],
                                  m1b[k][:, q * TOK + tc * 512:q * TOK + (tc + 1) * 512],
                                  start=(q == 0), stop=(q == 1))
                    for q in range(2):
                        ins = te.matmul(pfa[6 + k][:, :],
                                        b3re[e % 2][:, q * 512 + j * 128:q * 512 + (j + 1) * 128],
                                        m3b[k][:, q * TOK + tc * 512:q * TOK + (tc + 1) * 512],
                                        start=(q == 0), stop=(q == 1))
                    _inc(ins, pe_b)

        def fb_scalar(sc, e):
            bp1, bva, bpo = FB[e]["pe_a"], FB[e]["ve_a"], FB[e]["po_a"]
            for m in range(SUBS * 2):
                j, tc = divmod(m, 2)
                _w(sc, pe_a, bp1 + m + 1)
                _w(sc, ve_a, bva + (m - 2) * 2 + 2)
                sc.activation(b1sb[m % 2][:, :], pfa[m % 2][:, :], AF.Identity,
                              bias=0.0, scale=1.0)
                _inc(sc.activation(b3sb[m % 2][:, :], pfa[2 + m % 2][:, :],
                                   AF.Identity, bias=0.0, scale=1.0), ac_a)
                for k in range(2):
                    mk = m * 2 + k
                    _w(sc, ve_a, bva + mk + 1)
                    _w(sc, po_a, bpo + (mk - 2) + 1)
                    _inc(sc.activation(s1sb[k][:, :], x1sb[k][:, :], AF.Silu,
                                       bias=0.0, scale=1.0), ac_b)

        def fb_vector(v, e):
            bp2, bac, bab, bpo = (FB[e]["pe_b"], FB[e]["ac_a"], FB[e]["ac_b"],
                                  FB[e]["po_a"])

            def pool_part(mk2):
                m2, k2 = divmod(mk2, 2)
                j2, tc2 = divmod(m2, 2)
                _w(v, ac_b, bab + mk2 + 1)
                _w(v, po_s, _inc(v.tensor_tensor(
                    x3sb[k2][:, :], x3sb[k2][:, :],
                    rwb[k2][:, tc2 * 512:(tc2 + 1) * 512], ALU.mult), po_s))
                _inc(v.tensor_tensor(
                    x2v(k2)[:, j2 * TOK + tc2 * 512:j2 * TOK + (tc2 + 1) * 512],
                    s1sb[k2][:, :], x3sb[k2][:, :], ALU.mult), po_a)

            for m in range(SUBS * 2):
                j, tc = divmod(m, 2)
                for k in range(2):
                    mk = m * 2 + k
                    _w(v, pe_b, bp2 + mk + 1)
                    _w(v, ac_a, bac + m + 1)
                    _w(v, ac_b, bab + (mk - 2) + 1)
                    v.tensor_tensor(x1sb[k][:, :], pfa[4 + k][:, :],
                                    b1sb[m % 2][:, :], ALU.add)
                    if mk >= 2:
                        pool_part(mk - 2)
                    _w(v, po_a, bpo + (mk - 2) + 1)
                    _inc(v.tensor_tensor(x3sb[k][:, :], pfa[6 + k][:, :],
                                         b3sb[m % 2][:, :], ALU.add), ve_a)
            pool_part(14)
            pool_part(15)

        FD_ORDER = [(0, 0, 0), (0, 1, 0), (1, 0, 0), (1, 1, 0),
                    (0, 0, 1), (0, 1, 1), (1, 0, 1), (1, 1, 1)]

        def fd_tensor(te, e):
            dva, dvb, dvc, dpo = (FD[e]["ve_a"], FD[e]["ve_b"], FD[e]["ve_c"],
                                  FD[e]["po_a"])
            bpo = dpo - 16
            last_po = -1
            for idx, (k, q, tc) in enumerate(FD_ORDER):
                need = bpo + (6 + tc) * 2 + k + 1
                if need > last_po:
                    _w(te, po_a, need)
                    last_po = need
                for j in range(SUBS):
                    ins = te.matmul(
                        pfa[idx][:, :],
                        a2te[e % 2][:, j * ER + q * 128:j * ER + q * 128 + 128],
                        x2v(k)[:, j * TOK + tc * 512:j * TOK + (tc + 1) * 512],
                        start=(j == 0), stop=(j == SUBS - 1))
                _inc(ins, pe_a)
            for gidx in range(32):
                h, tc = divmod(gidx, 2)
                if gidx == 0:
                    _w(te, ve_a, dva + 8)
                    _w(te, d_w2[(2 * e) % 3], w2_wait[2 * e])
                    _w(te, d_w2[(2 * e + 1) % 3], w2_wait[2 * e + 1])
                    if e == NFC - 1:
                        _w(te, ve_c, dvc + 2)
                        _w(te, d_b2, B2RS_N)
                _w(te, ve_b, dvb + (gidx - 4) + 1)
                notlast = (e < NFC - 1)
                for k in range(2):
                    for j in range(SUBS):
                        half = j // 2
                        ap_ = w2slot_ap[(2 * e + half) % 3]
                        ins = te.matmul(
                            pfa[gidx % 4][:, :],
                            ap_[:, (j % 2) * D + h * 128:(j % 2) * D + (h + 1) * 128],
                            x2v(k)[:, j * TOK + tc * 512:j * TOK + (tc + 1) * 512],
                            start=(k == 0 and j == 0),
                            stop=(notlast and k == 1 and j == SUBS - 1))
                if e == NFC - 1:
                    for k in range(2):
                        for q in range(2):
                            ins = te.matmul(
                                pfa[gidx % 4][:, :],
                                b2rs[:, q * D + h * 128:q * D + (h + 1) * 128],
                                m1b[k][:, q * TOK + tc * 512:q * TOK + (tc + 1) * 512],
                                start=False, stop=(k == 1 and q == 1))
                _inc(ins, pe_b)

        def fd_vector(v, e):
            dp1, dp2, dva = FD[e]["pe_a"], FD[e]["pe_b"], FD[e]["ve_a"]
            for idx, (k, q, tc) in enumerate(FD_ORDER):
                _w(v, pe_a, dp1 + idx + 1)
                dst = a2acc[k][:, q * TOK + tc * 512:q * TOK + (tc + 1) * 512]
                _inc(v.tensor_tensor(dst, pfa[idx][:, :], dst, ALU.add), ve_a)
            if e == NFC - 1:
                _w(v, ve_a, dva + 8)
                _inc(v.tensor_tensor(m1b[0][:, :], a2acc[0][:, :], mex[0][:, :],
                                     ALU.mult), ve_c)
                _inc(v.tensor_tensor(m1b[1][:, :], a2acc[1][:, :], mex[1][:, :],
                                     ALU.mult), ve_c)
            for gidx in range(32):
                h, tc = divmod(gidx, 2)
                _w(v, pe_b, dp2 + gidx + 1)
                dst = sumT[:, h * TOK + tc * 512:h * TOK + (tc + 1) * 512]
                _inc(v.tensor_tensor(dst, pfa[gidx % 4][:, :], dst, ALU.add), ve_b)

        for e0 in range(0, NFC_RUN, 2):
            es = [e for e in (e0, e0 + 1) if e < NFC_RUN]
            with nc.Block(f"f{e0}") as blk:
                @blk.gpsimd
                def _(g, es=es):
                    for e in es:
                        fb_gpsimd(g, e)

                @blk.tensor
                def _(te, es=es):
                    for e in es:
                        fb_tensor(te, e)
                        if EMIT_FD:
                            fd_tensor(te, e)

                @blk.scalar
                def _(sc, es=es):
                    for e in es:
                        fb_scalar(sc, e)

                @blk.vector
                def _(v, es=es):
                    for e in es:
                        fb_vector(v, e)
                        if EMIT_FD:
                            fd_vector(v, e)

        f_ctx.close()
        ffn_ctx.close()

        emit_final()
        nc._dbg = dict(out=outd, xnTo=xnTo, big0=big0, sumT=sumT, xkv=xkv)
    return nc


def _host_arrays(inputs):
    f32 = np.float32
    g = {}
    g["wq"] = np.ascontiguousarray(inputs["L0_attn_Wq"], f32)
    g["wk"] = np.ascontiguousarray(inputs["L0_attn_Wk"], f32)
    g["wv"] = np.ascontiguousarray(inputs["L0_attn_Wv"], f32)
    g["wo"] = np.ascontiguousarray(inputs["L0_attn_Wo"], f32)
    g["w1t"] = np.ascontiguousarray(inputs["L0_ffn_W1"].T, f32)
    g["w3t"] = np.ascontiguousarray(inputs["L0_ffn_W3"].T, f32)
    g["w2t"] = np.ascontiguousarray(inputs["L0_ffn_W2"].T, f32)
    g["a1t"] = np.ascontiguousarray(inputs["L0_ffn_A1"].reshape(ER, D).T, f32)
    g["a3t"] = np.ascontiguousarray(inputs["L0_ffn_A3"].reshape(ER, D).T, f32)
    g["a2t"] = np.ascontiguousarray(inputs["L0_ffn_A2"].reshape(ER, FF).T, f32)
    g["b1r"] = np.ascontiguousarray(inputs["L0_ffn_B1"].transpose(0, 2, 1).reshape(ER, FF), f32)
    g["b3r"] = np.ascontiguousarray(inputs["L0_ffn_B3"].transpose(0, 2, 1).reshape(ER, FF), f32)
    g["b2r"] = np.ascontiguousarray(inputs["L0_ffn_B2"].transpose(0, 2, 1).reshape(ER, D), f32)
    wgT = np.ascontiguousarray(inputs["L0_ffn_Wg"].T, f32)
    g["wgt"] = np.ascontiguousarray(
        wgT.reshape(HT, 128, 8).transpose(1, 0, 2).reshape(128, 128), f32)
    bq = np.asarray(inputs["L0_attn_bq"], f32)
    bk = np.asarray(inputs["L0_attn_bk"], f32)
    bv = np.asarray(inputs["L0_attn_bv"], f32)
    bo = np.asarray(inputs["L0_attn_bo"], f32)
    g["bqb"] = np.ascontiguousarray(np.tile(bq[None, :], (128, 1))).astype(BF)
    g["bkb"] = np.ascontiguousarray(np.tile(bk[None, :], (128, 1))).astype(BF)
    g["bvb"] = np.ascontiguousarray(np.tile(bv[None, :], (128, 1))).astype(BF)
    g["bot"] = np.ascontiguousarray(bo.reshape(16, 128).T, f32)
    e8 = np.zeros((8, ER), BF)
    for e in range(8):
        e8[e, e * 32:(e + 1) * 32] = 1
    g["e8"] = e8
    g["identb"] = np.eye(128, dtype=BF)
    g["identf"] = np.eye(128, dtype=f32)
    g["ones1"] = np.ones((1, 128), BF)
    return g


def _pack_cs(c, s, n):
    out = np.empty((128, n * 320), np.float32)
    for t in range(n):
        cc = c[:, t * 64:(t + 1) * 64]
        ss = s[:, t * 64:(t + 1) * 64]
        out[:, t * 320:(t + 1) * 320] = np.concatenate([cc, ss, cc, ss, cc], 1)
    return out


def _rope_tables(h):
    f32 = np.float32
    inv = 1.0 / (10000.0 ** (np.arange(64, dtype=np.float64) * 2.0 / 128.0))
    pos_own = h * TOK + np.arange(TOK, dtype=np.float64)
    ang_q = pos_own[:, None] * inv[None, :]
    cq = np.cos(ang_q).astype(f32) * np.float32(SCALE)
    sq = np.sin(ang_q).astype(f32) * np.float32(SCALE)
    cq = cq.reshape(NTT, 128, 64).transpose(1, 0, 2).reshape(128, NTT * 64)
    sq = sq.reshape(NTT, 128, 64).transpose(1, 0, 2).reshape(128, NTT * 64)
    pos_kv = np.zeros(NKT * 128, np.float64)
    pos_kv[:1536] = h * TOK - 512 + np.arange(1536)
    pos_kv[1536:1540] = np.arange(4)
    ang_k = pos_kv[:, None] * inv[None, :]
    ck = np.cos(ang_k).astype(f32)
    sk = np.sin(ang_k).astype(f32)
    ck = ck.reshape(NKT, 128, 64).transpose(1, 0, 2).reshape(128, NKT * 64)
    sk = sk.reshape(NKT, 128, 64).transpose(1, 0, 2).reshape(128, NKT * 64)
    return (np.ascontiguousarray(_pack_cs(cq, sq, NTT)),
            np.ascontiguousarray(_pack_cs(ck, sk, NKT)))


def _mask_for(h):
    m = np.full((NTT, SLOTS, 128, 128), MASKNEG, np.float32)
    for qb in range(NTT):
        i = (h * TOK + qb * 128 + np.arange(128))[:, None]
        for j in range(SLOTS):
            if j < 5:
                s = (qb + j) * 128 + np.arange(128)[None, :]
                gk = h * TOK - 512 + s
                valid = np.ones_like(gk, bool) if h == 1 else (s >= 512)
                allow = valid & (gk <= i) & ((gk >= i - 512) | (gk < 4))
            else:
                c = np.arange(128)[None, :]
                if h == 0 and qb <= 4:
                    allow = np.zeros((128, 128), bool)
                else:
                    allow = (c < 4) & (c <= i)
            m[qb, j][allow] = 0.0
    return np.ascontiguousarray(
        m.transpose(2, 0, 1, 3).reshape(128, NTT * SLOTS * 128)).astype(BF)


def kernel(**inputs):
    if "nc" not in _CACHE:
        _CACHE["nc"] = build_program()
    nc = _CACHE["nc"]

    x = np.asarray(inputs["L0_attn_x"], np.float32)
    g = _host_arrays(inputs)
    per_h = {}
    for h in range(2):
        cq3, ck3 = _rope_tables(h)
        per_h[h] = dict(cqt=cq3, ckt=ck3, maskb=_mask_for(h))

    in_maps = []
    for core in range(8):
        b, h = divmod(core, 2)
        halo = x[b, 512:1024] if h == 1 else np.zeros((512, D), np.float32)
        xkv = np.concatenate(
            [halo, x[b, h * TOK:(h + 1) * TOK], x[b, 0:4],
             np.zeros((124, D), np.float32)], axis=0)
        m = dict(g)
        m.update(per_h[h])
        m["xkv"] = np.ascontiguousarray(xkv)
        in_maps.append(m)

    res = run_bass_kernel_spmd(nc, in_maps, core_ids=list(range(8)))
    out = np.empty((4, D, D), np.float32)
    for core in range(8):
        b, h = divmod(core, 2)
        out[b, h * TOK:(h + 1) * TOK] = res.results[core]["out"]
    return out
